# version 06
# baseline (speedup 1.0000x reference)
import os
import numpy as np
import concourse.bass as bass
import concourse.mybir as mybir
from concourse.bass_utils import run_bass_kernel_spmd
from concourse.replica_groups import maybe_share_collective_output_space
from concourse.bass_types import DynSlice

F32 = mybir.dt.float32
BF16 = mybir.dt.bfloat16
I32 = mybir.dt.int32
AF = mybir.ActivationFunctionType
ALU = mybir.AluOpType
AX = mybir.AxisListType.X

NCORES = 8
B, L, D = 2, 8192, 1024
NH, DK, DV = 4, 256, 256
CH = 128
NC = L // CH
TPC = L // 4
QC = TPC // CH
FFN_H = 2048
ABH, ABHD, ABKV, ABQL = 16, 64, 64, 512

ENGS = ("tensor", "vector", "scalar", "gpsimd", "sync")

CONSTS = [1.0, 0.5, 1e-5, 1e-6, 1e-4, -0.5, 1.0 / 1024, 1.0 / 256,
          -1e30, 1.5, 1.1920928955078125e-07, 2.0, 0.125, 1.0 / FFN_H, 0.0,
          -1.0]
(C_ONE, C_HALF, C_EPS5, C_EPS6, C_EPS4, C_NHALF, C_INV1024, C_INV256,
 C_M1E30, C_3HALF, C_ULP, C_TWO, C_EIGHTH, C_INVH, C_ZERO,
 C_MONE) = range(len(CONSTS))


DYN_MAXS = [32, 6144, 124, 124, 124, 124, 1024, 256]


def dyn_dma(E, st, out_ap, in_ap_fn, ks_sizes, in_off=None, in_size=None):
    sls = []
    regs = []
    for i, (k, size) in enumerate(ks_sizes):
        v = E.value_load(st.dyn_sb[0:1, k:k + 1])
        if in_off is not None and in_off[i]:
            v = v + in_off[i]
        sz = in_size[i] if in_size is not None else size
        sls.append(DynSlice(v, sz))
        regs.append(v)
    inst = E.dma_start(out_ap, in_ap_fn(*sls))
    for r in regs:
        try:
            E.free_register(r.val)
        except Exception:
            pass
    return inst


class Sem:

    def __init__(self, handle, name):
        self.h = handle
        self.name = name
        self.n = 0
        self.base = 0

    def bump(self, k=1):
        self.n += k
        return self.n


class Kb:

    def __init__(self, taps=()):
        import os
        self.nc = bass.Bass(
            "TRN2", target_bir_lowering=False,
            detect_race_conditions=not os.environ.get("KSIMNORACE"))
        self.taps = set(taps)
        self.tap_tensors = {}
        self.q = {e: [] for e in ENGS}
        self._last_real = {e: None for e in ENGS}
        self._ctx = []
        self._sems = []
        self.junk = None
        self._names = set()

    def _uniq(self, name):
        assert name not in self._names, f"dup tensor name {name}"
        self._names.add(name)
        return name

    def sb(self, name, shape, dt):
        g = self.nc.sbuf_tensor(self._uniq(name), shape, dt)
        t = g.__enter__()
        self._ctx.append(g)
        return t

    def ps(self, name, shape, dt=F32):
        g = self.nc.psum_tensor(self._uniq(name), shape, dt)
        t = g.__enter__()
        self._ctx.append(g)
        return t

    def sem_dedicated(self, name):
        g = self.nc.semaphore(name)
        h = g.__enter__()
        self._ctx.append(g)
        s = Sem(h, name)
        self._sems.append(s)
        return s

    def new_phase(self):
        self._pool_cursor = 0

    def sem(self, name):
        if not hasattr(self, "_pool"):
            self._pool = []
            self._pool_cursor = 0
        i = self._pool_cursor
        self._pool_cursor += 1
        if i >= len(self._pool):
            self._pool.append(self.sem_dedicated(f"gp{i}"))
        s = self._pool[i]
        s.base += s.n
        s.n = 0
        return s

    def semg(self, name):
        return self.sem_dedicated(name)

    def din(self, name, shape, dt=F32):
        return self.nc.dram_tensor(self._uniq(name), shape, dt,
                                   kind="ExternalInput")

    def dout(self, name, shape, dt=F32):
        return self.nc.dram_tensor(self._uniq(name), shape, dt,
                                   kind="ExternalOutput")

    def dint(self, name, shape, dt=F32, addr_space="Local"):
        return self.nc.dram_tensor(self._uniq(name), shape, dt,
                                   kind="Internal", addr_space=addr_space)

    def on(self, eng, fn, real=True):
        import os
        tr = os.environ.get("KTRUNC")
        self._ecount = getattr(self, "_ecount", 0) + 1
        if tr is not None and self._ecount > int(tr):
            return None
        ent = [fn, [], real]
        self.q[eng].append(ent)
        if real:
            self._last_real[eng] = ent
        return ent

    def inc_last(self, eng, sem, k=1):
        import os
        if os.environ.get("KTRUNC") is not None and (
                self._ecount > int(os.environ["KTRUNC"])):
            return sem.bump(k)
        ent = self.q[eng][-1]
        assert ent[2], "then_inc target must be a real instruction"
        ent[1].append((sem, k))
        return sem.bump(k)

    def wait(self, eng, sem, tgt):
        self.on(eng, lambda E, s=sem.h, t=sem.base + tgt: E.wait_ge(s, t),
                real=False)

    def pad(self, eng, k=2):
        assert self.junk is not None
        for i in range(k):
            if eng == "scalar":
                self.on(eng, lambda E, i=i: E.copy(
                    self.junk[:, 4 + i % 2:5 + i % 2],
                    self.junk[:, i % 2:i % 2 + 1]))
            else:
                self.on(eng, lambda E, i=i: E.tensor_copy(
                    self.junk[:, 6 + i % 2:7 + i % 2],
                    self.junk[:, 2 + i % 2:3 + i % 2]))

    def barrier(self, bar, engs=ENGS):
        if not hasattr(self, "_bar_aux"):
            self._bar_aux = {e: self.sem_dedicated(f"aux_{e}") for e in ENGS}
        for e in engs:
            ent = self._last_real[e]
            if ent is not None:
                if not ent[1]:
                    assert e not in ("sync", "gpsimd"), \
                        f"DMA on {e} without a semaphore before barrier"
                    ent[1].append((self._bar_aux[e], 1))
                    self._bar_aux[e].bump(1)
                s0, _ = ent[1][-1]
                t0 = s0.base + s0.n
                self.on(e, lambda E, s=s0.h, t=t0: E.wait_ge(s, t), real=False)
            self.on(e, lambda E, s=bar.h: E.sem_inc(s, 1), real=False)
            bar.bump(1)
        tgt = bar.n
        for e in engs:
            self.wait(e, bar, tgt)
            self._last_real[e] = None

    def finalize(self):
        nc = self.nc
        with nc.Block() as blk:
            @blk.tensor
            def _(E):
                for fn, incs, _ in self.q["tensor"]:
                    inst = fn(E)
                    for s, k in incs:
                        inst.then_inc(s.h, k)

            @blk.vector
            def _(E):
                for fn, incs, _ in self.q["vector"]:
                    inst = fn(E)
                    for s, k in incs:
                        inst.then_inc(s.h, k)

            @blk.scalar
            def _(E):
                for fn, incs, _ in self.q["scalar"]:
                    inst = fn(E)
                    for s, k in incs:
                        inst.then_inc(s.h, k)

            @blk.gpsimd
            def _(E):
                for fn, incs, _ in self.q["gpsimd"]:
                    inst = fn(E)
                    for s, k in incs:
                        inst.then_inc(s.h, k)

            @blk.sync
            def _(E):
                for fn, incs, _ in self.q["sync"]:
                    inst = fn(E)
                    for s, k in incs:
                        inst.then_inc(s.h, k)
        for g in reversed(self._ctx):
            g.__exit__(None, None, None)
        return nc

    def begin_scope(self):
        if not hasattr(self, "_scopes"):
            self._scopes = []
        self._scopes.append([])

    def sbs(self, name, shape, dt):
        g = self.nc.sbuf_tensor(self._uniq(name), shape, dt)
        t = g.__enter__()
        self._scopes[-1].append(g)
        return t

    def end_scope(self):
        for g in reversed(self._scopes.pop()):
            g.__exit__(None, None, None)

    def rsqrt(self, eng, out, x, s1, s2, s3, eps_col=None, nsteps=2):
        def e(fn):
            self.on(eng, fn)
            self.pad(eng, 1)
        if eps_col is not None:
            e(lambda E: E.tensor_scalar(s1, x, eps_col, None, ALU.add))
            xs = s1
        else:
            xs = x
        e(lambda E: E.tensor_scalar(s2.bitcast(I32), xs.bitcast(I32),
                                    1, None, ALU.arith_shift_right))
        e(lambda E: E.tensor_scalar(s2.bitcast(I32), s2.bitcast(I32),
                                    -1, 0x5f3759df, ALU.mult, ALU.add))
        for it in range(nsteps):
            e(lambda E: E.tensor_tensor(s3, xs, s2, ALU.mult))
            e(lambda E: E.tensor_tensor(s3, s3, s2, ALU.mult))
            e(lambda E: E.tensor_scalar(s3, s3, -0.5, 1.5, ALU.mult, ALU.add))
            dst = out if it == nsteps - 1 else s2
            e(lambda E, d=dst: E.tensor_tensor(d, s3, s2, ALU.mult))
        return out



def _A(x):
    return np.ascontiguousarray(np.asarray(x, dtype=np.float32))


def core_inputs(p, c):
    b, h = divmod(c, 4)
    hs = slice(256 * h, 256 * h + 256)
    d = {}
    d["xb"] = _A(p["L0_attn_a_hidden_states"][b])
    d["wq"] = _A(p["L0_attn_a_Wq"][:, hs])
    d["wk"] = _A(p["L0_attn_a_Wk"][:, hs])
    d["wv"] = _A(p["L0_attn_a_Wv"][:, hs])
    d["wb"] = _A(np.asarray(p["L0_attn_a_Wb"])[:, h].reshape(8, 128).T)
    d["cw_q"] = _A(p["L0_attn_a_conv_q_w"][hs])
    d["cw_k"] = _A(p["L0_attn_a_conv_k_w"][hs])
    d["cw_v"] = _A(p["L0_attn_a_conv_v_w"][hs])
    d["firs"] = _A(p["L0_attn_a_fir_short_w"][h])
    d["firl"] = _A(p["L0_attn_a_fir_long_w"][h])
    d["w1a"] = _A(p["L0_attn_a_mlp_w1"][:1024])
    perm = [8 * (j // 2) + 4 * (j % 2) + hh for hh in range(4) for j in range(8)]
    d["w1b"] = _A(p["L0_attn_a_mlp_w1"][1024:][perm])
    d["w2"] = _A(p["L0_attn_a_mlp_w2"])
    d["b1t"] = _A(np.asarray(p["L0_attn_a_mlp_b1"]).reshape(16, 128).T)
    d["b2c"] = _A(np.asarray(p["L0_attn_a_mlp_b2"]).reshape(16, 1))
    d["glt"] = _A(np.asarray(p["L0_attn_a_gate_log_temp"])[h].reshape(1, 1))
    d["onw"] = _A(np.tile(p["L0_attn_a_o_norm_w"], (128, 1)))
    d["wo"] = _A(p["L0_attn_a_Wo"])
    d["wqdT"] = _A(np.asarray(p["L0_attn_b_wq_d"]).T)
    d["bqdt"] = _A(np.asarray(p["L0_attn_b_bq_d"]).reshape(4, 128).T)
    d["wqkT"] = _A(np.asarray(p["L0_attn_b_w_qk"]).T)
    d["bqkt"] = _A(np.asarray(p["L0_attn_b_b_qk"]).reshape(8, 128).T)
    d["wkvdT"] = _A(np.asarray(p["L0_attn_b_wkv_d"]).T)
    d["bkvc"] = _A(np.asarray(p["L0_attn_b_bkv_d"]).reshape(64, 1))
    d["wvuT"] = _A(np.asarray(p["L0_attn_b_wv_u"]).T)
    d["bvur"] = _A(np.asarray(p["L0_attn_b_bv_u"]).reshape(1, 1024))
    d["wprT"] = _A(np.asarray(p["L0_attn_b_w_proj"]).T)
    d["bprt"] = _A(np.asarray(p["L0_attn_b_b_proj"]).reshape(8, 128).T)
    d["gw"] = _A(p["L0_ffn_gate_w"])
    d["gb128"] = _A(np.tile(p["L0_ffn_gate_b"], (128, 1)))
    d["rw1"] = _A(p["L0_ffn_rw1"])
    d["rw2"] = _A(p["L0_ffn_rw2"])
    d["rw3"] = _A(p["L0_ffn_rw3"])
    d["sw1"] = _A(p["L0_ffn_sw1"])
    d["sw2"] = _A(p["L0_ffn_sw2"])
    d["sw3"] = _A(p["L0_ffn_sw3"])
    d["rb1t"] = _A(np.asarray(p["L0_ffn_rb1"]).reshape(8, 16, 128).transpose(0, 2, 1))
    d["rb2t"] = _A(np.asarray(p["L0_ffn_rb2"]).reshape(8, 16, 128).transpose(0, 2, 1))
    d["sb1t"] = _A(np.asarray(p["L0_ffn_sb1"]).reshape(2, 16, 128).transpose(0, 2, 1))
    d["sb2t"] = _A(np.asarray(p["L0_ffn_sb2"]).reshape(2, 16, 128).transpose(0, 2, 1))
    d["b3"] = _A(np.concatenate([np.asarray(p["L0_ffn_rb3"]),
                                 np.asarray(p["L0_ffn_sb3"])], 0))
    d["xq"] = _A(p["L0_attn_a_hidden_states"][b][2048 * h:2048 * (h + 1)])
    d["dyno"] = np.array([[32 * b, 2048 * h,
                           64 * b + 0 + 4 * h, 64 * b + 16 + 4 * h,
                           64 * b + 32 + 4 * h, 64 * b + 48 + 4 * h,
                           1024 * b, 256 * b]], dtype=np.int32)
    d["consts"] = np.tile(np.array(CONSTS, np.float32), (128, 1))
    d["ident"] = np.eye(128, dtype=np.float32)
    d["mtril"] = np.tril(np.ones((128, 128), np.float32), -1)
    d["mtriu"] = np.triu(np.ones((128, 128), np.float32), 0)
    return d


_NC_CACHE = {}


def kernel(**inputs):
    taps = inputs.pop("_taps", ())
    key = tuple(sorted(taps))
    if key not in _NC_CACHE:
        _NC_CACHE[key] = build_program(taps)
    nc, out_names = _NC_CACHE[key]
    in_maps = [core_inputs(inputs, c) for c in range(NCORES)]
    res = run_bass_kernel_spmd(nc, in_maps, core_ids=list(range(NCORES)))
    out = np.zeros((B, L, D), np.float32)
    for c in range(NCORES):
        b, h = divmod(c, 4)
        out[b, 2048 * h:2048 * (h + 1)] = res.results[c]["out"]
    if taps:
        kernel.tap_results = [{n: res.results[c][n] for n in out_names if n != "out"}
                              for c in range(NCORES)]
    return out



class St:
    pass


def build_program(taps=(), phases=(1, 2, 3, 4, 5, 6)):
    kb = Kb(taps)
    st = St()
    st.phases = phases
    T, V, A, G, Sy = ENGS

    st.bk = [kb.ps(f"bk{i}", [128, 512], F32) for i in range(8)]

    st.d = {}
    for n, sh in [("xb", [L, D]), ("wq", [D, DK]), ("wk", [D, DK]),
                  ("wv", [D, DV]), ("wb", [128, 8]), ("cw_q", [DK, 4]),
                  ("cw_k", [DK, 4]), ("cw_v", [DV, 4]), ("firs", [DV, 3]),
                  ("firl", [DV, 63]), ("w1a", [D, 2048]), ("w1b", [32, 2048]),
                  ("w2", [2048, 16]), ("b1t", [128, 16]), ("b2c", [16, 1]),
                  ("glt", [1, 1]), ("onw", [128, DV]), ("wo", [D, D]),
                  ("wqdT", [D, ABQL]), ("bqdt", [128, 4]),
                  ("wqkT", [ABQL, D]), ("bqkt", [128, 8]),
                  ("wkvdT", [D, ABKV]), ("bkvc", [ABKV, 1]),
                  ("wvuT", [ABKV, D]), ("bvur", [1, D]),
                  ("wprT", [D, D]), ("bprt", [128, 8]),
                  ("gw", [D, 8]), ("gb128", [128, 8]),
                  ("rw1", [8, D, 2048]), ("rw2", [8, D, 2048]),
                  ("rw3", [8, 2048, D]), ("sw1", [2, D, 2048]),
                  ("sw2", [2, D, 2048]), ("sw3", [2, 2048, D]),
                  ("rb1t", [8, 128, 16]), ("rb2t", [8, 128, 16]),
                  ("sb1t", [2, 128, 16]), ("sb2t", [2, 128, 16]),
                  ("b3", [10, D]), ("consts", [128, len(CONSTS)]),
                  ("ident", [128, 128]), ("mtril", [128, 128]),
                  ("mtriu", [128, 128]), ("xq", [TPC, D])]:
        st.d[n] = kb.din(n, sh, F32)
    st.d["dyno"] = kb.din("dyno", [1, 8], I32)
    st.out = kb.dout("out", [TPC, D], F32)

    st.qbn = (kb.dout if "qbn" in taps else kb.dint)("qbn", [NC, 2, 128, 128], BF16)
    st.kbn = (kb.dout if "kbn" in taps else kb.dint)("kbn", [NC, 2, 128, 128], BF16)
    st.fsd = (kb.dout if "fsd" in taps else kb.dint)("fsd", [NC, 128, DV], BF16)
    st.fld = (kb.dout if "fld" in taps else kb.dint)("fld", [NC, 128, DV], BF16)
    st.xres = (kb.dout if "xres" in taps else kb.dint)("xres", [QC, 128, D], F32)
    st.x2s = kb.dout("x2s", [QC, 128, D], F32) if "x2s" in taps else None
    st.x3s = kb.dout("x3s", [QC, 128, D], F32) if "x3s" in taps else None
    st.cq5s = kb.dout("cq5s", [128, 4, TPC], BF16) if "cq5s" in taps else None
    st.lgs = kb.dout("lgs", [16, TPC], F32) if "lgs" in taps else None
    st.st32s = kb.dout("st32s", [32, TPC], BF16) if "st32s" in taps else None
    st.g1s = kb.dout("g1s", [128, 16, 512], BF16) if "g1s" in taps else None
    st.rg3s = kb.dout("rg3s", [128, 4, 8, 128], BF16) if "rg3s" in taps else None
    st.q5s = kb.dout("q5s", [128, 8, TPC], BF16) if "q5s" in taps else None
    st.cq5l = kb.dout("cq5l", [128, 4, TPC], BF16) if "cq5l" in taps else None
    st.xh5s2 = kb.dout("xh5s2", [128, 8, TPC], BF16) if "xh5s2" in taps else None
    sh_ag = maybe_share_collective_output_space("AllGather",
                                                [list(range(NCORES))])
    st.stats_in = kb.dint("stats_in", [8, L], F32)
    st.stats_out = kb.dint("stats_out", [64, L], F32, addr_space=sh_ag)
    st.logit_in = kb.dint("logit_in", [16, TPC], F32)
    st.logit_out = kb.dint("logit_out", [128, TPC], F32, addr_space=sh_ag)
    st.o_in = kb.dint("o_in", [DV, L], BF16)
    st.o_out = kb.dint("o_out", [8 * DV, L], BF16, addr_space=sh_ag)
    st.ckv_in = kb.dint("ckv_in", [ABKV, TPC], BF16)
    st.ckv_out = kb.dint("ckv_out", [8 * ABKV, TPC], BF16, addr_space=sh_ag)

    st.consts = kb.sb("c_consts", [128, len(CONSTS)], F32)
    st.ones_bf = kb.sb("c_onesbf", [128, 1], BF16)
    st.idf = kb.sb("c_idf", [128, 128], F32)
    st.idb = kb.sb("c_idb", [128, 128], BF16)
    st.mtrilS = kb.sb("c_mtrilS", [128, 128], BF16)
    st.mtriuI = kb.sb("c_mtriuI", [128, 128], BF16)
    st.x_tm = None

    st.bar = kb.sem_dedicated("bar")

    def cst(col, p=128):
        return st.consts[0:p, col:col + 1]

    kb.new_phase()
    s0 = kb.sem_dedicated("s0")
    s0g = kb.sem_dedicated("s0g")
    kb.on(Sy, lambda E: E.dma_start(st.consts[:], st.d["consts"].ap()))
    kb.inc_last(Sy, s0, 16)
    kb.on(Sy, lambda E: E.dma_start(st.idf[:], st.d["ident"].ap()))
    kb.inc_last(Sy, s0, 16)
    kb.on(G, lambda E: E.dma_start(st.idb[:], st.d["ident"].ap()))
    kb.inc_last(G, s0g, 16)
    kb.on(G, lambda E: E.dma_start(st.mtrilS[:], st.d["mtril"].ap()))
    kb.inc_last(G, s0g, 16)
    kb.on(G, lambda E: E.dma_start(st.mtriuI[:], st.d["mtriu"].ap()))
    kb.inc_last(G, s0g, 16)

    st.wqkv = None
    for i, n in (enumerate(("wq", "wk", "wv")) if 1 in phases else ()):
        kb.on(G, lambda E, i=i, n=n: E.dma_start(
            st.wqkv[:, i, :, :],
            st.d[n].ap().rearrange("(kc p) m -> p kc m", p=128)))
        kb.inc_last(G, s0g, 16)
    if 1 in phases:
        kb.on(G, lambda E: E.dma_start(st.wbt[:], st.d["wb"].ap()))
        kb.inc_last(G, s0g, 16)
    for i, n in (enumerate(("cw_q", "cw_k", "cw_v")) if 1 in phases else ()):
        kb.on(G, lambda E, i=i, n=n: E.dma_start(
            st.cw[:, i, :, :],
            st.d[n].ap().rearrange("(g p) t -> p g t", p=128)))
        kb.inc_last(G, s0g, 16)
    if 1 in phases:
        kb.on(G, lambda E: E.dma_start(
            st.fw[:, :, 0:3],
            st.d["firs"].ap().rearrange("(g p) t -> p g t", p=128)))
        kb.inc_last(G, s0g, 16)
        kb.on(G, lambda E: E.dma_start(
            st.fw[:, :, 3:66],
            st.d["firl"].ap().rearrange("(g p) t -> p g t", p=128)))
        kb.inc_last(G, s0g, 16)

    st.dyn_sb = kb.sb("dyn_sb", [1, 8], I32)
    st.ones_row = kb.sb("ones_row", [1, 128], F32)
    st.ones_rb = kb.sb("ones_rb", [1, 128], BF16)
    kb.on(Sy, lambda E: E.dma_start(st.dyn_sb[:], st.d["dyno"].ap()))
    kb.inc_last(Sy, s0, 16)
    tdyn = s0.n
    kb.wait(Sy, s0, tdyn)
    kb.wait(G, s0, tdyn)
    kb.on(V, lambda E: E.memset(st.ones_row[:], 1.0))
    kb.on(V, lambda E: E.memset(st.ones_rb[:], 1.0))
    kb.junk = kb.sb("c_junk", [128, 8], F32)
    sjk = kb.sem_dedicated("sjk")
    kb.on(V, lambda E: E.memset(kb.junk[:], 0.0))
    kb.inc_last(V, sjk, 1)
    kb.wait(A, sjk, 1)

    kb.wait(V, s0, s0.n)
    kb.wait(V, s0g, s0g.n)
    kb.on(V, lambda E: E.tensor_copy(st.ones_bf[:], cst(C_ONE)))
    for m in range(3 if 1 in phases else 0):
        for g in range(2):
            for t in range(4):
                kb.on(V, lambda E, m=m, g=g, t=t: E.tensor_scalar(
                    st.cdiag[:, m, g, t, :], st.idb[:],
                    st.cw[:, m, g, t:t + 1], None, ALU.mult))
    for g in range(2 if 1 in phases else 0):
        for t in range(66):
            kb.on(V, lambda E, g=g, t=t: E.tensor_scalar(
                st.fdiag[:, g, t, :], st.idb[:],
                st.fw[:, g, t:t + 1], None, ALU.mult))
    for nm in (("preq", "prek", "prev", "vbuf") if 1 in phases else ()):
        kb.on(V, lambda E, nm=nm: E.memset(getattr(st, nm)[:, :, 0:64], 0.0))
    kb.barrier(st.bar)

    kb.begin_scope()
    st.v_tm = kb.sbs("v_tm", [128, NC, DV], BF16)
    st.dl_tm = kb.sbs("dl_tm", [128, NC, DV], BF16)
    st.braw = kb.sbs("braw", [128, NC], F32)
    st.beta = kb.sbs("beta", [128, NC], F32)
    st.ssq = kb.sbs("ssq", [128, 2, NC], F32)
    st.mr = kb.sbs("mr", [128, NC, 2], F32)
    st.stats_tm = kb.sbs("stats_tm", [128, NC, 8], F32)
    kb.new_phase()
    if 1 in phases:
        p1_trunk(kb, st, cst)
    kb.barrier(st.bar)
    kb.new_phase()
    if 2 in phases:
        p2_scan(kb, st, cst)
    kb.barrier(st.bar)
    kb.new_phase()
    if 3 in phases:
        p3_gate(kb, st, cst)
    kb.barrier(st.bar)
    kb.end_scope()
    kb.new_phase()
    if 4 in phases:
        p4_ffn(kb, st, cst, 0)
    kb.barrier(st.bar)
    kb.new_phase()
    if 5 in phases:
        p5_attnb(kb, st, cst)
    kb.barrier(st.bar)
    kb.new_phase()
    if 6 in phases:
        p4_ffn(kb, st, cst, 1)
    kb.barrier(st.bar)

    emit_taps(kb, st)

    out_names = ["out"]
    for t in sorted(kb.taps):
        if t in ("qbn", "kbn", "fsd", "fld", "xres", "x2s", "x3s",
                 "cq5s", "xh5s2", "q5s", "cq5l", "lgs", "st32s", "g1s",
                 "rg3s"):
            out_names.append(t)
        elif t == "nbq":
            out_names.append("p5_nbqd")
        elif getattr(st, t, None) is not None:
            out_names.append("tap_" + t)
    return kb.finalize(), out_names


def emit_taps(kb, st):
    T, V, A, G, Sy = ENGS
    stap = kb.sem_dedicated("stap")
    for name in sorted(kb.taps):
        if name in ("qbn", "kbn", "fsd", "fld", "xres", "x2s", "x3s"):
            continue
        t = getattr(st, name, None)
        if t is None:
            continue
        shp = list(t.ap().shape)
        o = kb.dout("tap_" + name, shp, t.ap().dtype)
        kb.on(Sy, lambda E, o=o, t=t: E.dma_start(o.ap(), t[:]))
        kb.inc_last(Sy, stap, 16)
    if stap.n:
        kb.wait(Sy, stap, stap.n)


def p1_trunk(kb, st, cst):
    import os
    NSG = int(os.environ.get("P1N", "16"))
    T, V, A, G, Sy = ENGS
    kb.begin_scope()
    st.wqkv = kb.sbs("wqkv", [128, 3, 8, DK], BF16)
    st.wbt = kb.sbs("wbt", [128, 8], BF16)
    st.cw = kb.sbs("cw", [128, 3, 2, 4], F32)
    st.fw = kb.sbs("fw", [128, 2, 66], F32)
    st.cdiag = kb.sbs("cdiag", [128, 3, 2, 4, 128], BF16)
    st.fdiag = kb.sbs("fdiag", [128, 2, 66, 128], BF16)
    st.preq = kb.sbs("preq", [128, 2, 576], BF16)
    st.prek = kb.sbs("prek", [128, 2, 576], BF16)
    st.prev = kb.sbs("prev", [128, 2, 576], BF16)
    st.vbuf = kb.sbs("vbuf", [128, 2, 576], BF16)
    xt = kb.sbs("p1_xt", [128, 2, D], F32)
    xh = kb.sbs("p1_xh", [128, 2, D], BF16)
    st.p1xt = xt
    st.p1xh = xh
    fstg = kb.sbs("p1_fstg", [128, 2, 2, DV], BF16)
    ring = kb.sbs("p1_ring", [128, 4, 8, 128], BF16)
    st.p1ring = ring
    qkb = kb.sbs("p1_qkb", [128, 2, 2, 512], BF16)
    sqs = kb.sbs("p1_sqs", [128, 2, 512], BF16)
    ffm = kb.sbs("p1_ffm", [128, 2, 2, 512], BF16)
    scr = kb.sbs("p1_scr", [128, D], F32)
    col = kb.sbs("p1_col", [128, 8], F32)
    stc = kb.sbs("p1_stc", [128, 4], F32)

    s_x = [kb.sem("p1_x0"), kb.sem("p1_x1")]
    s_ln = kb.sem("p1_ln")
    s_tr = kb.sem("p1_tr")
    s_xf = kb.sem("p1_xf")
    s_bb = kb.sem("p1_bb")
    s_bev = kb.sem("p1_bev")
    s_pp = kb.sem("p1_pp")
    s_pre = kb.sem("p1_pre")
    s_cv = kb.sem("p1_cv")
    s_si = kb.sem("p1_si")
    s_fir = kb.sem("p1_fir")
    s_fe = kb.sem("p1_fe")
    s_tt = kb.sem("p1_tt")
    s_te = kb.sem("p1_te")
    s_sq = kb.sem("p1_sq")
    s_ss = kb.sem("p1_ss")
    s_sse = kb.sem("p1_sse")
    s_qko = kb.sem("p1_qko")
    s_fso = [kb.sem("p1_fso0"), kb.sem("p1_fso1")]
    s_tl = kb.sem("p1_tl")

    bk = st.bk
    trv = bk[0].ap().bitcast(BF16)
    t0v = bk[6].ap().bitcast(BF16)
    t1v = bk[7].ap().bitcast(BF16)

    if NSG < 16:
        for t in (st.braw[:], st.beta[:], st.ssq[:],
                  st.stats_tm[:].rearrange("p a b -> p (a b)"),
                  st.v_tm[:].rearrange("p a b -> p (a b)")):
            kb.on(V, lambda E, t=t: E.memset(t, 0.0))

    for g in range(NSG):
        c0 = 4 * g
        for cc in range(4):
            c = c0 + cc
            if c >= 2:
                kb.wait(Sy, s_ln, c - 1)
            kb.on(Sy, lambda E, c=c, cc=cc: E.dma_start(
                xt[:, cc % 2, :], st.d["xb"].ap()[128 * c:128 * (c + 1), :]))
            kb.inc_last(Sy, s_x[c % 2], 16)
            kb.wait(V, s_x[c % 2], 16 * (c // 2 + 1))
            kb.on(V, lambda E, cc=cc: E.reduce_sum(col[:, 0:1], xt[:, cc % 2, :], axis=AX))
            kb.pad(V, 1)
            kb.on(V, lambda E: E.tensor_scalar(col[:, 0:1], col[:, 0:1],
                                               cst(C_INV1024), None, ALU.mult))
            kb.on(V, lambda E, cc=cc: E.tensor_tensor(scr[:], xt[:, cc % 2, :],
                                                      xt[:, cc % 2, :], ALU.mult))
            kb.on(V, lambda E: E.reduce_sum(col[:, 1:2], scr[:], axis=AX))
            kb.pad(V, 1)
            kb.on(V, lambda E: E.tensor_scalar(col[:, 1:2], col[:, 1:2],
                                               cst(C_INV1024), None, ALU.mult))
            kb.on(V, lambda E: E.tensor_tensor(col[:, 2:3], col[:, 0:1],
                                               col[:, 0:1], ALU.mult))
            kb.pad(V, 1)
            kb.on(V, lambda E: E.tensor_tensor(col[:, 1:2], col[:, 1:2],
                                               col[:, 2:3], ALU.subtract))
            kb.pad(V, 1)
            kb.rsqrt(V, col[:, 3:4], col[:, 1:2], col[:, 4:5], col[:, 5:6],
                     col[:, 6:7], eps_col=cst(C_EPS5))
            kb.on(V, lambda E, c=c: E.tensor_copy(st.mr[:, c, 0:1], col[:, 0:1]))
            kb.on(V, lambda E, c=c: E.tensor_copy(st.mr[:, c, 1:2], col[:, 3:4]))
            if c >= 2:
                kb.wait(V, s_tr, c - 1)
            kb.on(V, lambda E, cc=cc: E.tensor_scalar(
                xh[:, cc % 2, :], xt[:, cc % 2, :], col[:, 0:1], col[:, 3:4],
                ALU.subtract, ALU.mult))
            kb.inc_last(V, s_ln, 1)
            kb.wait(T, s_ln, c + 1)
            if c >= 1:
                kb.wait(T, s_xf, c)
            for j in range(8):
                kb.on(T, lambda E, j=j, cc=cc: E.transpose(
                    trv[:, 128 * j:128 * (j + 1)],
                    xh[:, cc % 2, 128 * j:128 * (j + 1)], st.idb[:]))
            kb.inc_last(T, s_tr, 1)
            kb.wait(A, s_tr, c + 1)
            if g >= 1:
                kb.wait(A, s_pp, 6 * g)
            kb.on(A, lambda E, cc=cc: E.copy(ring[:, cc, :, :].rearrange(
                "p a b -> p (a b)"), trv[:]))
            kb.inc_last(A, s_xf, 1)

        kb.wait(T, s_xf, c0 + 4)
        if g >= 1:
            kb.wait(T, s_sse, g)
        for cc in range(4):
            for kc in range(8):
                kb.on(T, lambda E, cc=cc, kc=kc: E.matmul(
                    bk[5].ap()[:, cc:cc + 1], ring[:, cc, kc, :],
                    st.wbt[:, kc:kc + 1], start=(kc == 0), stop=(kc == 7)))
        kb.inc_last(T, s_bb, 1)
        kb.wait(V, s_bb, g + 1)
        kb.on(V, lambda E, c0=c0: E.tensor_copy(st.braw[:, c0:c0 + 4],
                                                bk[5].ap()[:, 0:4]))
        kb.inc_last(V, s_bev, 1)

        if g >= 1:
            kb.wait(T, s_fe, 4 * g)
        nsil = 0
        for m in range(3):
            for mt in range(2):
                u = 2 * m + mt
                pb = bk[1 + (u % 2)]
                if g * 6 + u >= 2:
                    kb.wait(T, s_pre, g * 6 + u - 1)
                for kc in range(8):
                    kb.on(T, lambda E, m=m, mt=mt, kc=kc, pb=pb: E.matmul(
                        pb.ap()[:], st.wqkv[:, m, kc, 128 * mt:128 * (mt + 1)],
                        ring[:, :, kc, :],
                        start=(kc == 0), stop=(kc == 7)))
                kb.inc_last(T, s_pp, 1)
                kb.wait(A, s_pp, g * 6 + u + 1)
                pre = (st.preq, st.prek, st.prev)[m]
                if g >= 1:
                    kb.wait(A, s_tl, g)
                kb.on(A, lambda E, pre=pre, mt=mt, pb=pb: E.copy(
                    pre[:, mt, 64:576], pb.ap()[:]))
                kb.inc_last(A, s_pre, 1)
                cb = bk[3 + (u % 2)]
                kb.wait(T, s_pre, g * 6 + u + 1)
                if g * 6 + u >= 2:
                    kb.wait(T, s_si, g * 6 + u - 1)
                for t in range(4):
                    kb.on(T, lambda E, m=m, mt=mt, t=t, cb=cb, pre=pre: E.matmul(
                        cb.ap()[:], st.cdiag[:, m, mt, t, :],
                        pre[:, mt, 61 + t:61 + t + 512],
                        start=(t == 0), stop=(t == 3)))
                kb.inc_last(T, s_cv, 1)
                kb.wait(A, s_cv, g * 6 + u + 1)
                if m < 2:
                    if g >= 1:
                        kb.wait(A, s_qko, 64 * g)
                    kb.on(A, lambda E, m=m, mt=mt, cb=cb: E.activation(
                        qkb[:, m, mt, :], cb.ap()[:], AF.Silu))
                else:
                    if g >= 1:
                        kb.wait(A, s_tl, g)
                    kb.on(A, lambda E, mt=mt, cb=cb: E.activation(
                        st.vbuf[:, mt, 64:576], cb.ap()[:], AF.Silu))
                kb.inc_last(A, s_si, 1)
                nsil += 1

        kb.wait(T, s_si, 6 * g + 6)
        for br in range(2):
            taps = 3 if br == 0 else 63
            base = 0 if br == 0 else 3
            for pt in range(2):
                fu = 4 * g + 2 * br + pt
                fb = bk[3 + ((2 * br + pt) % 2)]
                if fu >= 2:
                    kb.wait(T, s_fe, fu - 1)
                for t in range(taps):
                    off = 64 - (taps - 1) + t
                    kb.on(T, lambda E, pt=pt, t=t, base=base, off=off,
                          fb=fb, stp=(t == taps - 1): E.matmul(
                              fb.ap()[:], st.fdiag[:, pt, base + t, :],
                              st.vbuf[:, pt, off:off + 512],
                              start=(t == 0), stop=stp))
                kb.inc_last(T, s_fir, 1)
                kb.wait(A, s_fir, fu + 1)
                if g >= 1:
                    kb.wait(A, s_tt, 12 * g)
                kb.on(A, lambda E, br=br, pt=pt, fb=fb: E.copy(
                    ffm[:, br, pt, :], fb.ap()[:]))
                kb.inc_last(A, s_fe, 1)

        kb.wait(V, s_fir, 4 * g + 4)
        kb.wait(V, s_cv, 6 * g + 6)
        for t in (st.preq, st.prek, st.prev, st.vbuf):
            kb.on(V, lambda E, t=t: E.tensor_copy(t[:, :, 0:64], t[:, :, 512:576]))
        kb.inc_last(V, s_tl, 1)

        for ti in range(3):
            for cc in range(4):
                c = c0 + cc
                tu = g * 12 + ti * 4 + cc
                tb = bk[6 + (tu % 2)]
                tv = (t0v, t1v)[tu % 2]
                if ti == 0:
                    kb.wait(T, s_si, 6 * g + 6)
                else:
                    kb.wait(T, s_fe, 4 * g + 4)
                if tu >= 2:
                    kb.wait(T, s_te, tu - 1)
                for pt in range(2):
                    if ti == 0:
                        sap = st.vbuf[:, pt, 64 + 128 * cc:64 + 128 * (cc + 1)]
                    else:
                        sap = ffm[:, ti - 1, pt, 128 * cc:128 * (cc + 1)]
                    kb.on(T, lambda E, sap=sap, pt=pt, tv=tv: E.transpose(
                        tv[:, 128 * pt:128 * (pt + 1)], sap, st.idb[:]))
                kb.inc_last(T, s_tt, 1)
                kb.wait(V, s_tt, tu + 1)
                if ti == 0:
                    dst = st.v_tm[:, c, :]
                else:
                    nfir = g * 8 + (ti - 1) * 4 + cc
                    if nfir >= 2:
                        kb.wait(V, s_fso[cc % 2], 16 * (nfir // 2))
                    dst = fstg[:, ti - 1, cc % 2, :]
                kb.on(V, lambda E, dst=dst, tv=tv: E.tensor_copy(
                    dst, tv[:, 0:256]))
                kb.inc_last(V, s_te, 1)
                if ti > 0:
                    kb.wait(Sy, s_te, s_te.n)
                    dn = st.fsd if ti == 1 else st.fld
                    kb.on(Sy, lambda E, dn=dn, c=c, ti=ti, cc=cc: E.dma_start(
                        dn.ap()[c], fstg[:, ti - 1, cc % 2, :]))
                    kb.inc_last(Sy, s_fso[cc % 2], 16)
                jm = (6, 0, 2)[ti]
                kb.on(V, lambda E, dst=dst: E.reduce_sum(
                    stc[:, 0:1], dst, axis=AX))
                kb.pad(V, 1)
                kb.on(V, lambda E: E.tensor_scalar(
                    stc[:, 0:1], stc[:, 0:1], cst(C_INV256), None, ALU.mult))
                kb.on(V, lambda E, dst=dst: E.tensor_tensor(
                    scr[:, 0:256], dst, dst, ALU.mult))
                kb.on(V, lambda E: E.reduce_sum(stc[:, 1:2], scr[:, 0:256], axis=AX))
                kb.pad(V, 1)
                kb.on(V, lambda E: E.tensor_scalar(
                    stc[:, 1:2], stc[:, 1:2], cst(C_INV256), None, ALU.mult))
                kb.on(V, lambda E: E.tensor_tensor(
                    stc[:, 2:3], stc[:, 0:1], stc[:, 0:1], ALU.mult))
                kb.pad(V, 1)
                kb.on(V, lambda E: E.tensor_tensor(
                    stc[:, 1:2], stc[:, 1:2], stc[:, 2:3], ALU.subtract))
                kb.pad(V, 1)
                kb.on(V, lambda E: E.tensor_scalar(
                    stc[:, 1:2], stc[:, 1:2], cst(C_EPS6), None, ALU.max))
                kb.pad(V, 1)
                kb.rsqrt(V, stc[:, 2:3], stc[:, 1:2], col[:, 4:5], col[:, 5:6],
                         col[:, 6:7])
                kb.on(V, lambda E, c=c, jm=jm: E.tensor_tensor(
                    st.stats_tm[:, c, jm + 1:jm + 2], stc[:, 1:2], stc[:, 2:3],
                    ALU.mult))
                kb.on(V, lambda E, c=c, jm=jm: E.tensor_copy(
                    st.stats_tm[:, c, jm:jm + 1], stc[:, 0:1]))

        kb.wait(T, s_bev, g + 1)
        for m in range(2):
            if m == 1:
                kb.wait(A, s_ss, 2 * g + 1)
            kb.on(A, lambda E, m=m: E.activation(
                sqs[:].rearrange("p a b -> p (a b)"),
                qkb[:, m, :, :].rearrange("p a b -> p (a b)"), AF.Square))
            kb.inc_last(A, s_sq, 1)
            kb.wait(T, s_sq, 2 * g + m + 1)
            for cc in range(4):
                for pt in range(2):
                    kb.on(T, lambda E, m=m, cc=cc, pt=pt: E.matmul(
                        bk[5].ap()[:, 4 + 4 * m + cc:5 + 4 * m + cc],
                        sqs[:, pt, 128 * cc:128 * (cc + 1)], st.ones_bf[:],
                        start=(pt == 0), stop=(pt == 1)))
            kb.inc_last(T, s_ss, 1)
        kb.wait(V, s_ss, 2 * g + 2)
        kb.on(V, lambda E, c0=c0: E.tensor_copy(
            st.ssq[:, 0, c0:c0 + 4], bk[5].ap()[:, 4:8]))
        kb.on(V, lambda E, c0=c0: E.tensor_copy(
            st.ssq[:, 1, c0:c0 + 4], bk[5].ap()[:, 8:12]))
        kb.inc_last(V, s_sse, 1)

        kb.wait(Sy, s_si, 6 * g + 4)
        for m, dn in ((0, st.qbn), (1, st.kbn)):
            for x in range(2):
                kb.on(Sy, lambda E, m=m, dn=dn, c0=c0, x=x: E.dma_start(
                    dn.ap()[c0:c0 + 4, x].rearrange("c p t -> p c t"),
                    qkb[:, m, x, :].rearrange("p (c t) -> p c t", c=4)))
                kb.inc_last(Sy, s_qko, 16)

    nbc = 4 * NSG
    kb.wait(A, s_bev, NSG)
    kb.on(A, lambda E: E.activation(st.beta[:, 0:nbc], st.braw[:, 0:nbc],
                                    AF.Tanh, scale=cst(C_HALF)))
    sfin = kb.sem("p1_fin")
    kb.inc_last(A, sfin, 1)
    kb.wait(V, sfin, 1)
    kb.on(V, lambda E: E.tensor_scalar(st.beta[:, 0:nbc], st.beta[:, 0:nbc],
                                       cst(C_HALF), cst(C_HALF),
                                       ALU.mult, ALU.add))
    kb.end_scope()


class Banks:

    def __init__(self, kb, banks, tag):
        self.kb = kb
        self.banks = banks
        self.freed = [None] * len(banks)
        self.i = 0

    def acquire(self):
        i = self.i
        self.i = (self.i + 1) % len(self.banks)
        if self.freed[i] is not None:
            s, t = self.freed[i]
            self.kb.wait("tensor", s, t)
        return i

    def release(self, i, sem, tgt):
        self.freed[i] = (sem, tgt)


def p2_scan(kb, st, cst):
    T, V, A, G, Sy = ENGS
    kb.begin_scope()
    qf = kb.sbs("p2_qf", [128, 2, 2, 128], BF16)
    kf = kb.sbs("p2_kf", [128, 2, 2, 128], BF16)
    ktm = kb.sbs("p2_ktm", [128, 256], BF16)
    kbt = kb.sbs("p2_kbt", [128, 256], BF16)
    vb = kb.sbs("p2_vb", [128, 256], BF16)
    Pa = kb.sbs("p2_Pa", [128, 7, 128], BF16)
    Pt = kb.sbs("p2_Pt", [128, 6, 128], BF16)
    Xb = kb.sbs("p2_X", [128, 2, 128], BF16)
    atT = kb.sbs("p2_atT", [128, 128], BF16)
    wT = kb.sbs("p2_wT", [128, 256], BF16)
    u_sb = kb.sbs("p2_u", [128, 256], BF16)
    u2r = kb.sbs("p2_u2r", [128, 256], BF16)
    S32 = kb.sbs("p2_S32", [128, 2, 256], F32)
    S16 = kb.sbs("p2_S16", [128, 2, 256], BF16)
    rn = kb.sbs("p2_rn", [128, 2], F32)
    colp = kb.sbs("p2_col", [128, 8], F32)
    scr2 = kb.sbs("p2_scr", [128, 256], F32)
    stc2 = kb.sbs("p2_stc", [128, 4], F32)
    stg = kb.sbs("p2_stg", [8, 128], F32)

    s_ld = [kb.sem("p2_ld0"), kb.sem("p2_ld1")]
    s_pe = kb.sem("p2_pe")
    po_marks = []
    s_dv = kb.sem("p2_dv")
    s_stg = kb.sem("p2_stg")
    s_std = kb.sem("p2_std")
    bks = Banks(kb, st.bk, "p2")

    kb.on(V, lambda E: E.memset(S32[:], 0.0))
    kb.on(V, lambda E: E.memset(S16[:], 0.0))
    pe_end_prev = None
    NCH = int(os.environ.get("P2N", str(NC)))
    if NCH < NC:
        kb.on(V, lambda E: E.memset(
            st.dl_tm[:].rearrange("p a b -> p (a b)"), 0.0))

    def pe(fn, mark=False):
        kb.on(T, fn)
        if mark:
            return kb.inc_last(T, s_pe, 1)

    def dv(fn, mark=False):
        kb.on(V, fn)
        if mark:
            return kb.inc_last(V, s_dv, 1)

    def prod(nmm, extra, width=128):
        bi = bks.acquire()
        bap = st.bk[bi].ap()[:, 0:width]
        for i, (lh, rh) in enumerate(nmm):
            pe(lambda E, b=bap, l=lh, r=rh, s1=(i == 0), s2=(i == len(nmm) - 1):
               E.matmul(b, l, r, start=s1, stop=s2))
        tpe = kb.inc_last(T, s_pe, 1)
        kb.wait(V, s_pe, tpe)
        extra(bap)
        tdv = kb.inc_last(V, s_dv, 1)
        bks.release(bi, s_dv, tdv)
        return tdv, tpe

    for c in range(NCH):
        sl = c % 2
        if c >= 2:
            kb.wait(Sy, s_pe, po_marks[c - 2])
        kb.on(Sy, lambda E, c=c, sl=sl: E.dma_start(
            qf[:, sl, :, :], st.qbn.ap()[c].rearrange("x p t -> p x t")))
        kb.inc_last(Sy, s_ld[sl], 16)
        kb.on(Sy, lambda E, c=c, sl=sl: E.dma_start(
            kf[:, sl, :, :], st.kbn.ap()[c].rearrange("x p t -> p x t")))
        kb.inc_last(Sy, s_ld[sl], 16)

        kb.rsqrt(V, rn[:], st.ssq[:, :, c], colp[:, 0:2], colp[:, 2:4],
                 colp[:, 4:6], eps_col=cst(C_EPS6))
        dv(lambda E, c=c: E.tensor_tensor(colp[:, 6:7], st.beta[:, c:c + 1],
                                          rn[:, 1:2], ALU.mult))
        dv(lambda E, c=c: E.tensor_scalar(vb[:], st.v_tm[:, c, :],
                                          st.beta[:, c:c + 1], None, ALU.mult))
        t_vb = kb.inc_last(V, s_dv, 1)

        kb.wait(T, s_ld[sl], 32 * (c // 2 + 1))
        bi = bks.acquire()
        bv = st.bk[bi].ap().bitcast(BF16)
        for pt in range(2):
            pe(lambda E, pt=pt, sl=sl, bv=bv: E.transpose(
                bv[:, 128 * pt:128 * (pt + 1)], kf[:, sl, pt, :], st.idb[:]))
        tp = kb.inc_last(T, s_pe, 1)
        kb.wait(V, s_pe, tp)
        dv(lambda E, bv=bv: E.tensor_copy(ktm[:], bv[:, 0:256]))
        t_ktm = kb.inc_last(V, s_dv, 1)
        bks.release(bi, s_dv, t_ktm)
        dv(lambda E: E.tensor_scalar(kbt[:], ktm[:], colp[:, 6:7], None,
                                     ALU.mult))
        t_kbt = kb.inc_last(V, s_dv, 1)

        if pe_end_prev is not None:
            kb.wait(V, s_pe, pe_end_prev)
        t_aap, _ = prod(
            nmm=[(kf[:, sl, 0, :], kf[:, sl, 0, :]),
                 (kf[:, sl, 1, :], kf[:, sl, 1, :])],
            extra=lambda bap: dv(lambda E, b=bap: E.scalar_tensor_tensor(
                Pa[:, 0, :], b, colp[:, 6:7], st.mtrilS[:],
                ALU.mult, ALU.mult)))
        bi = bks.acquire()
        bv2 = st.bk[bi].ap().bitcast(BF16)
        kb.wait(T, s_dv, t_aap)
        pe(lambda E, bv2=bv2: E.transpose(bv2[:, 0:128], Pa[:, 0, :], st.idb[:]))
        tp = kb.inc_last(T, s_pe, 1)
        kb.wait(V, s_pe, tp)
        dv(lambda E, bv2=bv2: E.tensor_scalar(Pt[:, 0, :], bv2[:, 0:128],
                                              rn[:, 1:2], None, ALU.mult))
        t_atf = kb.inc_last(V, s_dv, 1)
        bks.release(bi, s_dv, t_atf)
        bi = bks.acquire()
        bv3 = st.bk[bi].ap().bitcast(BF16)
        kb.wait(T, s_dv, t_atf)
        pe(lambda E, bv3=bv3: E.transpose(bv3[:, 0:128], Pt[:, 0, :], st.idb[:]))
        tp = kb.inc_last(T, s_pe, 1)
        kb.wait(V, s_pe, tp)
        dv(lambda E, bv3=bv3: E.tensor_copy(Pa[:, 0, :], bv3[:, 0:128]))
        t_aaf = kb.inc_last(V, s_dv, 1)
        bks.release(bi, s_dv, t_aaf)

        kb.wait(T, s_dv, t_aaf)
        for lev in range(1, 7):
            bi = bks.acquire()
            bap = st.bk[bi].ap()
            pe(lambda E, lev=lev, b=bap: E.matmul(
                b[:, 0:128], Pt[:, lev - 1, :], Pa[:, lev - 1, :],
                start=True, stop=True))
            if lev < 6:
                pe(lambda E, lev=lev, b=bap: E.matmul(
                    b[:, 128:256], Pa[:, lev - 1, :], Pt[:, lev - 1, :],
                    start=True, stop=True))
            tp = kb.inc_last(T, s_pe, 1)
            kb.wait(V, s_pe, tp)
            dv(lambda E, lev=lev, b=bap: E.tensor_copy(Pa[:, lev, :],
                                                       b[:, 0:128]))
            if lev < 6:
                dv(lambda E, lev=lev, b=bap: E.tensor_copy(Pt[:, lev, :],
                                                           b[:, 128:256]))
            t_lev = kb.inc_last(V, s_dv, 1)
            kb.wait(T, s_dv, t_lev)
            bks.release(bi, s_dv, t_lev)

        xcur = 0
        t_x = None
        first = True
        for e_idx, lev in ((6, "init"), (5, "add"), (4, "add"), (3, "add"),
                           (2, "add"), (1, "add"), (0, "sub")):
            bi = bks.acquire()
            bap = st.bk[bi].ap()[:, 0:128]
            rhs = st.idb[:] if first else Xb[:, xcur, :]
            if t_x is not None:
                kb.wait(T, s_dv, t_x)
            pe(lambda E, b=bap, e=e_idx, r=rhs: E.matmul(
                b, Pa[:, e, :], r, start=True, stop=True))
            tp = kb.inc_last(T, s_pe, 1)
            kb.wait(V, s_pe, tp)
            xn = 1 - xcur
            base = st.idb[:] if first else Xb[:, xcur, :]
            op = ALU.subtract if lev == "sub" else ALU.add
            dv(lambda E, b=bap, ba=base, xn=xn, op=op: E.tensor_tensor(
                Xb[:, xn, :], ba, b, op))
            t_x = kb.inc_last(V, s_dv, 1)
            bks.release(bi, s_dv, t_x)
            xcur = xn
            first = False
        Tt = Xb[:, xcur, :]
        kb.wait(T, s_dv, t_x)

        kb.wait(T, s_dv, t_vb)
        t_u, _ = prod(nmm=[(Tt, vb[:])], width=256,
                      extra=lambda bap: dv(lambda E, b=bap: E.tensor_copy(
                          u_sb[:], b)))
        kb.wait(T, s_dv, t_kbt)
        bi = bks.acquire()
        bap = st.bk[bi].ap()
        for hf in range(2):
            pe(lambda E, hf=hf, b=bap: E.matmul(
                b[:, 128 * hf:128 * (hf + 1)], kbt[:, 128 * hf:128 * (hf + 1)],
                Tt, start=True, stop=True))
        tp = kb.inc_last(T, s_pe, 1)
        kb.wait(V, s_pe, tp)
        dv(lambda E, b=bap: E.tensor_copy(wT[:], b[:, 0:256]))
        t_wt = kb.inc_last(V, s_dv, 1)
        bks.release(bi, s_dv, t_wt)

        t_at, _ = prod(
            nmm=[(kf[:, sl, 0, :], qf[:, sl, 0, :]),
                 (kf[:, sl, 1, :], qf[:, sl, 1, :])],
            extra=lambda bap: dv(lambda E, b=bap: E.tensor_tensor(
                atT[:], b, st.mtriuI[:], ALU.mult)))

        kb.wait(T, s_dv, t_wt)
        kb.wait(T, s_dv, t_at)
        bi = bks.acquire()
        bap = st.bk[bi].ap()[:, 0:256]
        for hf in range(2):
            pe(lambda E, hf=hf, b=bap: E.matmul(
                b, wT[:, 128 * hf:128 * (hf + 1)], S16[:, hf, :],
                start=(hf == 0), stop=(hf == 1)))
        tp = kb.inc_last(T, s_pe, 1)
        kb.wait(V, s_pe, tp)
        dv(lambda E, b=bap: E.tensor_tensor(scr2[:], u_sb[:], b, ALU.subtract))
        dv(lambda E: E.tensor_scalar(u2r[:], scr2[:], rn[:, 1:2], None,
                                     ALU.mult))
        t_u2 = kb.inc_last(V, s_dv, 1)
        bks.release(bi, s_dv, t_u2)

        kb.wait(T, s_dv, t_u2)
        bi = bks.acquire()
        bap = st.bk[bi].ap()[:, 0:256]
        pe(lambda E, b=bap, sl=sl: E.matmul(b, qf[:, sl, 0, :], S16[:, 0, :],
                                            start=True, stop=False))
        pe(lambda E, b=bap, sl=sl: E.matmul(b, qf[:, sl, 1, :], S16[:, 1, :],
                                            start=False, stop=False))
        pe(lambda E, b=bap: E.matmul(b, atT[:], u2r[:], start=False, stop=True))
        tp = kb.inc_last(T, s_pe, 1)
        po_marks.append(tp)
        kb.wait(V, s_pe, tp)
        dv(lambda E, b=bap, c=c: E.tensor_scalar(
            st.dl_tm[:, c, :], b, rn[:, 0:1], None, ALU.mult))
        t_dl = kb.inc_last(V, s_dv, 1)
        bks.release(bi, s_dv, t_dl)

        bi = bks.acquire()
        bap = st.bk[bi].ap()
        for hf in range(2):
            pe(lambda E, hf=hf, b=bap: E.matmul(
                b[:, 256 * hf:256 * (hf + 1)], ktm[:, 128 * hf:128 * (hf + 1)],
                u2r[:], start=True, stop=True))
        tp = kb.inc_last(T, s_pe, 1)
        kb.wait(V, s_pe, tp)
        for hf in range(2):
            dv(lambda E, hf=hf, b=bap: E.tensor_tensor(
                S32[:, hf, :], S32[:, hf, :], b[:, 256 * hf:256 * (hf + 1)],
                ALU.add))
        dv(lambda E: E.tensor_copy(S16[:].rearrange("p a b -> p (a b)"),
                                   S32[:].rearrange("p a b -> p (a b)")))
        t_s16 = kb.inc_last(V, s_dv, 1)
        bks.release(bi, s_dv, t_s16)
        kb.wait(T, s_dv, t_s16)

        dv(lambda E, c=c: E.reduce_sum(stc2[:, 0:1], st.dl_tm[:, c, :], axis=AX))
        kb.pad(V, 1)
        dv(lambda E: E.tensor_scalar(stc2[:, 0:1], stc2[:, 0:1],
                                     cst(C_INV256), None, ALU.mult))
        dv(lambda E, c=c: E.tensor_tensor(scr2[:], st.dl_tm[:, c, :],
                                          st.dl_tm[:, c, :], ALU.mult))
        dv(lambda E: E.reduce_sum(stc2[:, 1:2], scr2[:], axis=AX))
        kb.pad(V, 1)
        dv(lambda E: E.tensor_scalar(stc2[:, 1:2], stc2[:, 1:2],
                                     cst(C_INV256), None, ALU.mult))
        dv(lambda E: E.tensor_tensor(stc2[:, 2:3], stc2[:, 0:1], stc2[:, 0:1],
                                     ALU.mult))
        kb.pad(V, 1)
        dv(lambda E: E.tensor_tensor(stc2[:, 1:2], stc2[:, 1:2], stc2[:, 2:3],
                                     ALU.subtract))
        kb.pad(V, 1)
        dv(lambda E: E.tensor_scalar(stc2[:, 1:2], stc2[:, 1:2],
                                     cst(C_EPS6), None, ALU.max))
        kb.pad(V, 1)
        kb.rsqrt(V, stc2[:, 2:3], stc2[:, 1:2], colp[:, 0:1], colp[:, 1:2],
                 colp[:, 2:3])
        dv(lambda E, c=c: E.tensor_copy(st.stats_tm[:, c, 4:5], stc2[:, 0:1]))
        dv(lambda E, c=c: E.tensor_tensor(st.stats_tm[:, c, 5:6], stc2[:, 1:2],
                                          stc2[:, 2:3], ALU.mult))
        t_st = kb.inc_last(V, s_dv, 1)
        kb.wait(T, s_dv, t_st)
        bi = bks.acquire()
        bap = st.bk[bi].ap()[:, 0:128]
        pe(lambda E, b=bap, c=c: E.transpose(b[0:8, :], st.stats_tm[:, c, :],
                                             st.idf[:]))
        tp = kb.inc_last(T, s_pe, 1)
        kb.wait(V, s_pe, tp)
        if c >= 1:
            kb.wait(V, s_std, 16 * c)
        dv(lambda E, b=bap: E.tensor_copy(stg[:], b[0:8, :]))
        tstg = kb.inc_last(V, s_stg, 1)
        bks.release(bi, s_stg, tstg)
        kb.wait(Sy, s_stg, tstg)
        kb.on(Sy, lambda E, c=c: E.dma_start(
            st.stats_in.ap()[:, 128 * c:128 * (c + 1)], stg[:]))
        kb.inc_last(Sy, s_std, 16)
        pe_end_prev = s_pe.n

    kb.end_scope()


def p3_gate(kb, st, cst):
    T, V, A, G, Sy = ENGS
    kb.begin_scope()
    fsr = kb.sbs("p3_fsr", [128, 2, 2, DV], BF16)
    lg4 = kb.sbs("p3_lg4", [4, L], BF16)
    wsm = kb.sbs("p3_wsm", [128, 8], F32)
    acc = kb.sbs("p3_acc", [128, 256], F32)
    onwb = kb.sbs("p3_onw", [128, 256], BF16)
    ostg = kb.sbs("p3_ostg", [128, 2, 128], BF16)
    invt = kb.sbs("p3_invt", [128, 1], F32)
    col3 = kb.sbs("p3_col", [128, 8], F32)
    scr3 = kb.sbs("p3_scr", [128, 256], F32)
    glt = kb.sbs("p3_glt", [1, 4], F32)
    kb.begin_scope()
    st32 = kb.sbs("p3_st32", [32, TPC], BF16)
    st32f = kb.sbs("p3_st32f", [32, TPC], F32)
    xqt = kb.sbs("p3_xqt", [128, 2, D], F32)
    xh3 = kb.sbs("p3_xh3", [128, 2, D], BF16)
    ring3 = kb.sbs("p3_ring", [128, 4, 8, 128], BF16)
    w1a = kb.sbs("p3_w1a", [128, 8, 2048], BF16)
    w1b = kb.sbs("p3_w1b", [32, 2048], BF16)
    w2s = kb.sbs("p3_w2", [128, 16, 16], BF16)
    b1s = kb.sbs("p3_b1", [128, 16], F32)
    b2s = kb.sbs("p3_b2", [16, 1], F32)
    g1 = kb.sbs("p3_g1", [128, 16, 512], BF16)
    lgf = kb.sbs("p3_lgf", [16, TPC], F32)

    s_w = kb.sem("p3_w")
    s_wg = kb.semg("p3_wg")
    s_cc = kb.semg("p3_cc")
    s_bk = kb.sem("p3_bk")
    s_x = [kb.sem("p3_x0"), kb.sem("p3_x1")]
    s_ln = kb.sem("p3_ln")
    s_tr = kb.sem("p3_tr")
    s_xf = kb.sem("p3_xf")
    s_h1 = kb.sem("p3_h1")
    s_ge = kb.sem("p3_ge")
    s_l2 = kb.sem("p3_l2")
    s_le = kb.sem("p3_le")
    s_li = kb.sem("p3_li")
    s_tp = kb.sem("p3_tp")
    s_sm = kb.sem("p3_sm")
    s_ex = kb.sem("p3_ex")
    s_od = kb.sem("p3_od")
    s_oe = kb.sem("p3_oe")
    s_oi = kb.sem("p3_oi")
    s_xq = kb.sem("p3_xq")
    s_wo = kb.sem("p3_wo")
    s_we = kb.sem("p3_we")
    s_wt = kb.sem("p3_wt")
    s_wa = kb.sem("p3_wa")
    s_fl = [kb.sem("p3_fl0"), kb.sem("p3_fl1")]
    s_e1 = kb.sem("p3_e1")
    s_e1c = kb.sem("p3_e1c")
    s_fc = kb.sem("p3_fc")
    bk = st.bk

    kb.on(G, lambda E: E.collective_compute(
        "AllGather", ALU.bypass, replica_groups=[list(range(8))],
        ins=[st.stats_in.ap().opt()], outs=[st.stats_out.ap().opt()]))
    kb.inc_last(G, s_cc, 1)
    for n, dst, rr in [("w1a", w1a, "(kc p) m -> p kc m"),
                       ("w1b", w1b, None), ("w2", w2s, "(kc p) m -> p kc m"),
                       ("onw", onwb, None)]:
        kb.on(G, lambda E, n=n, dst=dst, rr=rr: E.dma_start(
            dst[:], st.d[n].ap() if rr is None
            else st.d[n].ap().rearrange(rr, p=128)))
        kb.inc_last(G, s_wg, 16)
    kb.wait(Sy, s_cc, 1)
    kb.on(Sy, lambda E: dyn_dma(
        E, st, st32f[:], lambda a, b: st.stats_out.ap()[a, b],
        [(0, 32), (1, TPC)]))
    kb.inc_last(Sy, s_e1, 16)
    kb.wait(V, s_e1, 16)
    kb.on(V, lambda E: E.tensor_copy(st32[:], st32f[:]))
    kb.inc_last(V, s_e1c, 1)
    if st.st32s is not None:
        s32g = kb.sem("p3_32s")
        kb.wait(Sy, s_e1c, 1)
        kb.on(Sy, lambda E: E.dma_start(st.st32s.ap(), st32[:]))
        kb.inc_last(Sy, s32g, 16)
        kb.wait(Sy, s32g, 16)
    for n, dst in [("b1t", b1s), ("b2c", b2s)]:
        kb.on(Sy, lambda E, n=n, dst=dst: E.dma_start(dst[:], st.d[n].ap()))
        kb.inc_last(Sy, s_w, 16)
    kb.on(Sy, lambda E: E.dma_start(glt[:, 0:1], st.d["glt"].ap()))
    kb.inc_last(Sy, s_w, 16)

    trv = bk[0].ap().bitcast(BF16)
    nweights = s_w.n
    nweightsg = s_wg.n
    for sg in range(4):
        for cc in range(4):
            qc_ = 4 * sg + cc
            if qc_ >= 2:
                kb.wait(Sy, s_ln, qc_ - 1)
            kb.on(Sy, lambda E, qc_=qc_, cc=cc: E.dma_start(
                xqt[:, cc % 2, :],
                st.d["xq"].ap()[128 * qc_:128 * (qc_ + 1), :]))
            kb.inc_last(Sy, s_x[qc_ % 2], 16)
            kb.wait(V, s_x[qc_ % 2], 16 * (qc_ // 2 + 1))
            kb.on(V, lambda E, cc=cc: E.reduce_sum(
                col3[:, 0:1], xqt[:, cc % 2, :], axis=AX))
            kb.on(V, lambda E: E.tensor_scalar(col3[:, 0:1], col3[:, 0:1],
                                               cst(C_INV1024), None, ALU.mult))
            kb.on(V, lambda E, cc=cc: E.tensor_tensor(
                scr3[:], xqt[:, cc % 2, 0:256], xqt[:, cc % 2, 0:256], ALU.mult))
            kb.on(V, lambda E: E.reduce_sum(col3[:, 1:2], scr3[:], axis=AX))
            kb.pad(V, 1)
            for pp in range(1, 4):
                kb.on(V, lambda E, cc=cc, pp=pp: E.tensor_tensor(
                    scr3[:], xqt[:, cc % 2, 256 * pp:256 * (pp + 1)],
                    xqt[:, cc % 2, 256 * pp:256 * (pp + 1)], ALU.mult))
                kb.on(V, lambda E: E.reduce_sum(col3[:, 2:3], scr3[:], axis=AX))
                kb.pad(V, 1)
                kb.on(V, lambda E: E.tensor_tensor(col3[:, 1:2], col3[:, 1:2],
                                                   col3[:, 2:3], ALU.add))
                kb.pad(V, 1)
            kb.pad(V, 1)
            kb.on(V, lambda E: E.tensor_scalar(col3[:, 1:2], col3[:, 1:2],
                                               cst(C_INV1024), None, ALU.mult))
            kb.on(V, lambda E: E.tensor_tensor(col3[:, 2:3], col3[:, 0:1],
                                               col3[:, 0:1], ALU.mult))
            kb.pad(V, 1)
            kb.on(V, lambda E: E.tensor_tensor(col3[:, 1:2], col3[:, 1:2],
                                               col3[:, 2:3], ALU.subtract))
            kb.pad(V, 1)
            kb.rsqrt(V, col3[:, 3:4], col3[:, 1:2], col3[:, 4:5], col3[:, 5:6],
                     col3[:, 6:7], eps_col=cst(C_EPS5))
            if qc_ >= 2:
                kb.wait(V, s_tr, qc_ - 1)
            kb.on(V, lambda E, cc=cc: E.tensor_scalar(
                xh3[:, cc % 2, :], xqt[:, cc % 2, :], col3[:, 0:1], col3[:, 3:4],
                ALU.subtract, ALU.mult))
            kb.inc_last(V, s_ln, 1)
            kb.wait(T, s_ln, qc_ + 1)
            if qc_ >= 1:
                kb.wait(T, s_xf, qc_)
            for j in range(8):
                kb.on(T, lambda E, j=j, cc=cc: E.transpose(
                    trv[:, 128 * j:128 * (j + 1)],
                    xh3[:, cc % 2, 128 * j:128 * (j + 1)], st.idb[:]))
            kb.inc_last(T, s_tr, 1)
            kb.wait(A, s_tr, qc_ + 1)
            if sg >= 1:
                kb.wait(A, s_h1, 16 * sg)
            kb.on(A, lambda E, cc=cc: E.copy(
                ring3[:, cc, :, :].rearrange("p a b -> p (a b)"), trv[:]))
            kb.inc_last(A, s_xf, 1)

        kb.wait(T, s_xf, 4 * sg + 4)
        if sg == 0:
            kb.wait(T, s_w, nweights)
            kb.wait(T, s_wg, nweightsg)
            kb.wait(T, s_e1c, 1)
            kb.wait(A, s_w, nweights)
        for mtl in range(16):
            un = 16 * sg + mtl
            pb = bk[1 + (un % 2)]
            if un >= 2:
                kb.wait(T, s_ge, un - 1)
            for kc in range(8):
                kb.on(T, lambda E, kc=kc, mtl=mtl, pb=pb: E.matmul(
                    pb.ap()[:], w1a[:, kc, 128 * mtl:128 * (mtl + 1)],
                    ring3[:, :, kc, :], start=(kc == 0), stop=False))
            kb.on(T, lambda E, mtl=mtl, pb=pb, sg=sg: E.matmul(
                pb.ap()[:], w1b[:, 128 * mtl:128 * (mtl + 1)],
                st32[:, 512 * sg:512 * (sg + 1)], start=False, stop=True))
            kb.inc_last(T, s_h1, 1)
            kb.wait(A, s_h1, un + 1)
            if mtl == 0 and sg >= 1:
                kb.wait(A, s_l2, sg)
            kb.on(A, lambda E, mtl=mtl, pb=pb: E.activation(
                g1[:, mtl, :], pb.ap()[:], AF.Gelu, bias=b1s[:, mtl:mtl + 1]))
            kb.inc_last(A, s_ge, 1)
        if sg == 0 and st.rg3s is not None:
            srg = kb.sem("p3_rg3s")
            kb.wait(Sy, s_xf, 4)
            kb.on(Sy, lambda E: E.dma_start(
                st.rg3s.ap().rearrange("p a b c -> p (a b c)"),
                ring3[:].rearrange("p a b c -> p (a b c)")))
            kb.inc_last(Sy, srg, 16)
            kb.wait(Sy, srg, 16)
        if sg == 0 and st.g1s is not None:
            sg1 = kb.sem("p3_g1s")
            kb.wait(Sy, s_ge, 16)
            kb.on(Sy, lambda E: E.dma_start(st.g1s.ap(), g1[:]))
            kb.inc_last(Sy, sg1, 16)
            kb.wait(Sy, sg1, 16)
        kb.wait(T, s_ge, 16 * sg + 16)
        if sg >= 1:
            kb.wait(T, s_le, sg)
        for kc in range(16):
            kb.on(T, lambda E, kc=kc: E.matmul(
                bk[3].ap()[0:16, :], w2s[:, kc, :], g1[:, kc, :],
                start=(kc == 0), stop=(kc == 15)))
        kb.inc_last(T, s_l2, 1)
        kb.wait(V, s_l2, sg + 1)
        kb.wait(V, s_w, nweights)
        kb.on(V, lambda E, sg=sg: E.tensor_scalar(
            lgf[:, 512 * sg:512 * (sg + 1)], bk[3].ap()[0:16, :],
            b2s[:], None, ALU.add))
        kb.inc_last(V, s_le, 1)
        kb.wait(Sy, s_le, sg + 1)
        kb.on(Sy, lambda E, sg=sg: E.dma_start(
            st.logit_in.ap()[:, 512 * sg:512 * (sg + 1)],
            lgf[:, 512 * sg:512 * (sg + 1)]))
        kb.inc_last(Sy, s_li, 16)

    if st.lgs is not None:
        slg = kb.sem("p3_lgs")
        kb.wait(Sy, s_le, 4)
        kb.on(Sy, lambda E: E.dma_start(st.lgs.ap(), lgf[:]))
        kb.inc_last(Sy, slg, 16)
        kb.wait(Sy, slg, 16)
    kb.wait(G, s_li, 64)
    kb.on(G, lambda E: E.collective_compute(
        "AllGather", ALU.bypass, replica_groups=[list(range(8))],
        ins=[st.logit_in.ap().opt()], outs=[st.logit_out.ap().opt()]))
    kb.inc_last(G, s_cc, 1)
    kb.wait(Sy, s_cc, 2)
    kb.begin_scope()
    lgstg = kb.sbs("p3_lgstg", [4, TPC], F32)
    for j in range(4):
        if j >= 1:
            kb.wait(Sy, s_e1c, 1 + j)
        kb.on(Sy, lambda E, j=j: dyn_dma(
            E, st, lgstg[:], lambda a: st.logit_out.ap()[a, :], [(2 + j, 4)]))
        kb.inc_last(Sy, s_bk, 16)
        kb.wait(V, s_bk, 16 * (j + 1))
        kb.on(V, lambda E, j=j: E.tensor_copy(
            lg4[:, 2048 * j:2048 * (j + 1)], lgstg[:]))
        kb.inc_last(V, s_e1c, 1)
    kb.end_scope()

    kb.end_scope()
    kb.barrier(st.bar)

    kb.wait(A, s_w, nweights)
    stmp = kb.sem("p3_tmp")
    kb.on(A, lambda E: E.activation(glt[:, 1:2], glt[:, 0:1], AF.Abs))
    kb.pad(A, 2)
    kb.on(A, lambda E: E.activation(glt[:, 2:3], glt[:, 1:2], AF.Exp,
                                    scale=cst(C_MONE, 1)))
    kb.pad(A, 2)
    kb.on(A, lambda E: E.activation(glt[:, 2:3], glt[:, 2:3], AF.Ln,
                                    bias=cst(C_ONE, 1)))
    kb.inc_last(A, stmp, 1)
    kb.wait(V, stmp, 1)
    kb.on(V, lambda E: E.tensor_scalar(glt[:, 3:4], glt[:, 0:1], 0.0, None,
                                       ALU.max))
    kb.pad(V, 1)
    kb.on(V, lambda E: E.tensor_tensor(glt[:, 3:4], glt[:, 3:4], glt[:, 2:3],
                                       ALU.add))
    kb.pad(V, 1)
    kb.on(V, lambda E: E.tensor_scalar(glt[:, 3:4], glt[:, 3:4],
                                       cst(C_EPS4, 1), None, ALU.add))
    kb.pad(V, 1)
    kb.on(V, lambda E: E.reciprocal(glt[:, 2:3], glt[:, 3:4]))
    kb.inc_last(V, stmp, 1)
    kb.wait(T, stmp, 2)
    kb.on(T, lambda E: E.matmul(bk[4].ap()[:, 0:1], st.ones_row[:],
                                glt[:, 2:3], start=True, stop=True))
    kb.inc_last(T, stmp, 1)
    kb.wait(V, stmp, 3)
    kb.on(V, lambda E: E.tensor_copy(invt[:], bk[4].ap()[:, 0:1]))
    kb.inc_last(V, stmp, 1)
    kb.pad(V, 1)

    kb.wait(T, s_e1c, 5)
    kb.wait(V, s_wg, nweightsg)
    kb.wait(T, stmp, 4)
    for c in range(NC):
        if c == 0:
            for cpre in (0, 1):
                for br, dn in ((0, st.fsd), (1, st.fld)):
                    kb.on(Sy, lambda E, br=br, dn=dn, cpre=cpre: E.dma_start(
                        fsr[:, br, cpre % 2, :], dn.ap()[cpre]))
                    kb.inc_last(Sy, s_fl[cpre % 2], 16)
        elif c + 1 < NC:
            kb.wait(Sy, s_fc, c - 1)
            for br, dn in ((0, st.fsd), (1, st.fld)):
                kb.on(Sy, lambda E, br=br, dn=dn, c=c: E.dma_start(
                    fsr[:, br, (c + 1) % 2, :], dn.ap()[c + 1]))
                kb.inc_last(Sy, s_fl[(c + 1) % 2], 16)
        tb = bk[5]
        tv5 = bk[5].ap().bitcast(BF16)
        if c >= 1:
            kb.wait(T, s_sm, 3 * c)
        kb.on(T, lambda E, c=c: E.transpose(
            tv5[:, 0:4], lg4[:, 128 * c:128 * (c + 1)], st.idb[0:4, 0:4]))
        kb.inc_last(T, s_tp, 1)
        kb.wait(V, s_tp, c + 1)
        kb.on(V, lambda E: E.tensor_scalar(wsm[:, 0:4], tv5[:, 0:4],
                                           invt[:, 0:1], None, ALU.mult))
        kb.inc_last(V, s_sm, 1)
        kb.pad(V, 1)
        kb.on(V, lambda E: E.reduce_max(wsm[:, 4:5], wsm[:, 0:4], axis=AX))
        kb.pad(V, 1)
        kb.on(V, lambda E: E.tensor_scalar(wsm[:, 0:4], wsm[:, 0:4],
                                           wsm[:, 4:5], None, ALU.subtract))
        sxc = kb.inc_last(V, s_sm, 1)
        kb.wait(A, s_sm, sxc)
        kb.on(A, lambda E: E.activation(wsm[:, 0:4], wsm[:, 0:4], AF.Exp))
        kb.inc_last(A, s_ex, 1)
        kb.wait(V, s_ex, c + 1)
        kb.on(V, lambda E: E.reduce_sum(wsm[:, 4:5], wsm[:, 0:4], axis=AX))
        kb.pad(V, 1)
        kb.on(V, lambda E: E.reciprocal(wsm[:, 5:6], wsm[:, 4:5]))
        kb.pad(V, 1)
        kb.on(V, lambda E: E.tensor_scalar(wsm[:, 0:4], wsm[:, 0:4],
                                           wsm[:, 5:6], None, ALU.mult))
        kb.pad(V, 1)
        kb.wait(V, s_fl[c % 2], 32 * (c // 2 + 1))
        kb.on(V, lambda E, c=c: E.tensor_scalar(
            acc[:], fsr[:, 0, c % 2, :], wsm[:, 0:1], None, ALU.mult))
        kb.on(V, lambda E, c=c: E.scalar_tensor_tensor(
            acc[:], fsr[:, 1, c % 2, :], wsm[:, 1:2], acc[:], ALU.mult, ALU.add))
        kb.on(V, lambda E, c=c: E.scalar_tensor_tensor(
            acc[:], st.dl_tm[:, c, :], wsm[:, 2:3], acc[:], ALU.mult, ALU.add))
        kb.on(V, lambda E, c=c: E.scalar_tensor_tensor(
            acc[:], st.v_tm[:, c, :], wsm[:, 3:4], acc[:], ALU.mult, ALU.add))
        kb.inc_last(V, s_fc, 1)
        kb.on(V, lambda E: E.tensor_tensor(scr3[:], acc[:], acc[:], ALU.mult))
        kb.on(V, lambda E: E.reduce_sum(col3[:, 0:1], scr3[:], axis=AX))
        kb.pad(V, 1)
        kb.on(V, lambda E: E.tensor_scalar(col3[:, 0:1], col3[:, 0:1],
                                           cst(C_INV256), None, ALU.mult))
        kb.pad(V, 1)
        kb.rsqrt(V, col3[:, 1:2], col3[:, 0:1], col3[:, 4:5], col3[:, 5:6],
                 col3[:, 6:7], eps_col=cst(C_EPS5))
        kb.on(V, lambda E: E.tensor_scalar(acc[:], acc[:], col3[:, 1:2],
                                           None, ALU.mult))
        kb.on(V, lambda E, c=c: E.tensor_tensor(st.dl_tm[:, c, :], acc[:],
                                                onwb[:], ALU.mult))
        kb.inc_last(V, s_sm, 1)
        kb.wait(T, s_sm, s_sm.n)
        if c >= 1:
            kb.wait(T, s_oe, c)
        for pt in range(2):
            kb.on(T, lambda E, c=c, pt=pt: E.transpose(
                tv5[:, 512 + 128 * pt:512 + 128 * (pt + 1)],
                st.dl_tm[:, c, 128 * pt:128 * (pt + 1)], st.idb[:]))
        kb.inc_last(T, s_od, 1)
        kb.wait(V, s_od, c + 1)
        if c >= 1:
            kb.wait(V, s_oi, 16 * c)
        kb.on(V, lambda E: E.tensor_copy(
            ostg[:].rearrange("p a b -> p (a b)"), tv5[:, 512:768]))
        kb.inc_last(V, s_oe, 1)
        kb.wait(Sy, s_oe, c + 1)
        kb.on(Sy, lambda E, c=c: E.dma_start(
            st.o_in.ap()[:, 128 * c:128 * (c + 1)].rearrange(
                "(x p) t -> p x t", p=128), ostg[:]))
        kb.inc_last(Sy, s_oi, 16)

    kb.barrier(st.bar)
    kb.begin_scope()
    oall = kb.sbs("p3_oall", [128, 8, 2048], BF16)
    wos = kb.sbs("p3_wos", [128, 8, 1024], BF16)
    wstg = kb.sbs("p3_wstg", [128, 512], BF16)
    st.x_tm = kb.sbs("p3_x", [128, QC, D], F32)
    kb.on(G, lambda E: E.dma_start(
        wos[:], st.d["wo"].ap().rearrange("(kc p) m -> p kc m", p=128)))
    kb.inc_last(G, s_wg, 16)
    kb.on(Sy, lambda E: E.dma_start(
        st.x_tm[:], st.d["xq"].ap().rearrange("(c p) d -> p c d", p=128)))
    kb.inc_last(Sy, s_xq, 16)

    kb.wait(G, s_oi, 16 * NC)
    kb.on(G, lambda E: E.collective_compute(
        "AllGather", ALU.bypass, replica_groups=[list(range(8))],
        ins=[st.o_in.ap().opt()], outs=[st.o_out.ap().opt()]))
    kb.inc_last(G, s_cc, 1)
    kb.wait(Sy, s_cc, 3)
    kb.on(Sy, lambda E: dyn_dma(
        E, st, oall[:],
        lambda a, b: st.o_out.ap()[a, b].rearrange("(kc p) t -> p kc t",
                                                   p=128),
        [(6, 1024), (1, TPC)]))
    kb.inc_last(Sy, s_bk, 16)

    kb.wait(T, s_bk, s_bk.n)
    kb.wait(T, s_wg, s_wg.n)
    kb.wait(V, s_xq, 16)
    for tt in range(4):
        for dm in range(8):
            un = 8 * tt + dm
            pb = bk[1 + (un % 2)]
            if un >= 2:
                kb.wait(T, s_we, un - 1)
            for kc in range(8):
                kb.on(T, lambda E, kc=kc, dm=dm, tt=tt, pb=pb: E.matmul(
                    pb.ap()[:], wos[:, kc, 128 * dm:128 * (dm + 1)],
                    oall[:, kc, 512 * tt:512 * (tt + 1)],
                    start=(kc == 0), stop=(kc == 7)))
            kb.inc_last(T, s_wo, 1)
            kb.wait(A, s_wo, un + 1)
            if un >= 1:
                kb.wait(A, s_wt, un)
            kb.on(A, lambda E, pb=pb: E.copy(wstg[:], pb.ap()[:]))
            kb.inc_last(A, s_we, 1)
            kb.wait(T, s_we, un + 1)
            if un >= 1:
                kb.wait(T, s_wa, un)
            tv6 = bk[6].ap().bitcast(BF16)
            for tc in range(4):
                kb.on(T, lambda E, tc=tc, tv6=tv6: E.transpose(
                    tv6[:, 128 * tc:128 * (tc + 1)],
                    wstg[:, 128 * tc:128 * (tc + 1)], st.idb[:]))
            kb.inc_last(T, s_wt, 1)
            kb.wait(V, s_wt, un + 1)
            for tc in range(4):
                kb.on(V, lambda E, tt=tt, tc=tc, dm=dm, tv6=tv6: E.tensor_tensor(
                    st.x_tm[:, 4 * tt + tc, 128 * dm:128 * (dm + 1)],
                    st.x_tm[:, 4 * tt + tc, 128 * dm:128 * (dm + 1)],
                    tv6[:, 128 * tc:128 * (tc + 1)], ALU.add))
            kb.inc_last(V, s_wa, 1)
    s_xs = kb.sem("p3_xs")
    kb.wait(Sy, s_wa, 32)
    kb.on(Sy, lambda E: E.dma_start(
        st.xres.ap().rearrange("c p d -> p c d"), st.x_tm[:]))
    kb.inc_last(Sy, s_xs, 16)
    kb.wait(Sy, s_xs, 16)
    kb.end_scope()
    kb.end_scope()


def p4_ffn(kb, st, cst, pi):
    T, V, A, G, Sy = ENGS
    fx = f"p4{pi}"
    kb.begin_scope()
    xh4 = kb.sbs(fx + "_xh", [128, 8, 2048], BF16)
    facc = kb.sbs(fx + "_facc", [128, 8, 2048], BF16)
    wts = kb.sbs(fx + "_wts", [16, TPC], F32)
    st.p4xh = xh4
    st.p4facc = facc
    st.p4wts = wts
    wtsb = kb.sbs(fx + "_wtsb", [16, TPC], BF16)
    gwb = kb.sbs(fx + "_gw", [128, 8, 8], BF16)
    gbs = kb.sbs(fx + "_gb", [128, 8], F32)
    b3s = kb.sbs(fx + "_b3", [10, 1024], BF16)
    s1b = kb.sbs(fx + "_s1", [128, 512], BF16)
    wbc = kb.sbs(fx + "_wbc", [128, 512], BF16)
    wrow = kb.sbs(fx + "_wrow", [1, 512], F32)
    xhs = kb.sbs(fx + "_xhs", [128, D], BF16)
    m8 = kb.sbs(fx + "_m8", [128, 8], F32)
    e1 = kb.sbs(fx + "_e1", [128, 8], F32)
    m2 = kb.sbs(fx + "_m2", [128, 8], F32)
    col4 = kb.sbs(fx + "_col", [128, 8], F32)
    scr4 = kb.sbs(fx + "_scr", [128, D], F32)

    s_w = kb.sem(fx + "_w")
    s_wg = kb.semg(fx + "_wg")
    s_x = kb.sem(fx + "_x")
    s_ln = kb.sem(fx + "_ln")
    s_tr = kb.sem(fx + "_tr")
    s_xf = kb.sem(fx + "_xf")
    s_gt = kb.sem(fx + "_gt")
    s_gd = kb.sem(fx + "_gd")
    s_ex = kb.sem(fx + "_ex")
    s_wtp = kb.sem(fx + "_wtp")
    s_wte = kb.sem(fx + "_wte")
    s_wr = kb.semg(fx + "_wr")
    s_bc = kb.sem(fx + "_bc")
    s_h = kb.sem(fx + "_h")
    s_sil = kb.sem(fx + "_sil")
    s_gl = kb.sem(fx + "_gl")
    s_w3 = kb.sem(fx + "_w3m")
    s_fa = kb.sem(fx + "_fa")
    s_b1 = [kb.sem(fx + "_b1d0"), kb.sem(fx + "_b1d1")]
    s_w12 = kb.semg(fx + "_w12d")
    s_w3d = kb.semg(fx + "_w3d")
    s_tp2 = kb.sem(fx + "_tp2")
    s_ra = kb.sem(fx + "_ra")
    bk = st.bk
    trv = bk[0].ap().bitcast(BF16)

    kb.on(G, lambda E: E.dma_start(
        gwb[:], st.d["gw"].ap().rearrange("(kc p) m -> p kc m", p=128)))
    kb.inc_last(G, s_wg, 16)
    kb.on(G, lambda E: E.dma_start(b3s[:], st.d["b3"].ap()))
    kb.inc_last(G, s_wg, 16)
    kb.on(Sy, lambda E: E.dma_start(gbs[:], st.d["gb128"].ap()))
    kb.inc_last(Sy, s_w, 16)
    kb.on(V, lambda E: E.memset(facc[:].rearrange("p a b -> p (a b)"), 0.0))
    kb.on(V, lambda E: E.memset(wts[:], 0.5))
    svm = kb.sem(fx + "_vm")
    kb.inc_last(V, svm, 1)
    nw0 = s_w.n
    nw0g = s_wg.n

    kb.begin_scope()
    xa = kb.sbs(fx + "_xa", [128, QC, D], F32)
    kb.on(Sy, lambda E: E.dma_start(
        xa[:], st.xres.ap().rearrange("c p d -> p c d")))
    kb.inc_last(Sy, s_x, 16)
    kb.wait(V, s_x, 16)
    for qc_ in range(QC):
        for pp in range(2):
            kb.on(V, lambda E, qc_=qc_, pp=pp: E.tensor_tensor(
                scr4[:, 512 * pp:512 * (pp + 1)],
                xa[:, qc_, 512 * pp:512 * (pp + 1)],
                xa[:, qc_, 512 * pp:512 * (pp + 1)], ALU.mult))
        kb.on(V, lambda E: E.reduce_sum(col4[:, 1:2], scr4[:], axis=AX))
        kb.on(V, lambda E, qc_=qc_: E.reduce_sum(col4[:, 0:1],
                                                 xa[:, qc_, :], axis=AX))
        kb.on(V, lambda E: E.tensor_scalar(col4[:, 0:1], col4[:, 0:1],
                                           cst(C_INV1024), None, ALU.mult))
        kb.on(V, lambda E: E.tensor_scalar(col4[:, 1:2], col4[:, 1:2],
                                           cst(C_INV1024), None, ALU.mult))
        kb.on(V, lambda E: E.tensor_tensor(col4[:, 2:3], col4[:, 0:1],
                                           col4[:, 0:1], ALU.mult))
        kb.pad(V, 1)
        kb.on(V, lambda E: E.tensor_tensor(col4[:, 1:2], col4[:, 1:2],
                                           col4[:, 2:3], ALU.subtract))
        kb.pad(V, 1)
        kb.rsqrt(V, col4[:, 3:4], col4[:, 1:2], col4[:, 4:5], col4[:, 5:6],
                 col4[:, 6:7], eps_col=cst(C_EPS5))
        if qc_ >= 1:
            kb.wait(V, s_tr, qc_)
        kb.on(V, lambda E, qc_=qc_: E.tensor_scalar(
            xhs[:], xa[:, qc_, :], col4[:, 0:1], col4[:, 3:4],
            ALU.subtract, ALU.mult))
        kb.inc_last(V, s_ln, 1)
        kb.wait(T, s_ln, qc_ + 1)
        if qc_ >= 1:
            kb.wait(T, s_xf, qc_)
        for j in range(8):
            kb.on(T, lambda E, j=j: E.transpose(
                trv[:, 128 * j:128 * (j + 1)], xhs[:, 128 * j:128 * (j + 1)],
                st.idb[:]))
        kb.inc_last(T, s_tr, 1)
        kb.wait(A, s_tr, qc_ + 1)
        kb.on(A, lambda E, qc_=qc_: E.copy(xh4[:, :, 128 * qc_:128 * (qc_ + 1)],
                                           trv[:].rearrange("p (a b) -> p a b",
                                                            a=8)))
        kb.inc_last(A, s_xf, 1)
        if qc_ == 0:
            kb.wait(T, s_w, nw0)
            kb.wait(T, s_wg, nw0g)
        kb.wait(T, s_xf, qc_ + 1)
        if qc_ >= 1:
            kb.wait(T, s_gd, 3 * qc_)
        for kc in range(8):
            kb.on(T, lambda E, kc=kc, qc_=qc_: E.matmul(
                bk[5].ap()[:, 0:8], xh4[:, kc, 128 * qc_:128 * (qc_ + 1)],
                gwb[:, kc, :], start=(kc == 0), stop=(kc == 7)))
        kb.inc_last(T, s_gt, 1)
        kb.wait(V, s_gt, qc_ + 1)
        if qc_ == 0:
            kb.wait(V, s_w, nw0)
        kb.on(V, lambda E: E.tensor_tensor(m8[:], bk[5].ap()[:, 0:8], gbs[:],
                                           ALU.add))
        kb.inc_last(V, s_gd, 1)
        kb.pad(V, 1)
        kb.on(V, lambda E: E.reduce_max(col4[:, 0:1], m8[:], axis=AX))
        kb.pad(V, 1)
        kb.on(V, lambda E: E.tensor_scalar(m8[:], m8[:], col4[:, 0:1], None,
                                           ALU.subtract))
        tex = kb.inc_last(V, s_gd, 1)
        kb.wait(A, s_gd, tex)
        kb.on(A, lambda E: E.activation(m8[:], m8[:], AF.Exp))
        kb.inc_last(A, s_ex, 1)
        kb.wait(V, s_ex, qc_ + 1)
        kb.on(V, lambda E: E.reduce_sum(col4[:, 0:1], m8[:], axis=AX))
        kb.pad(V, 1)
        kb.on(V, lambda E: E.reciprocal(col4[:, 1:2], col4[:, 0:1]))
        kb.pad(V, 1)
        kb.on(V, lambda E: E.tensor_scalar(m8[:], m8[:], col4[:, 1:2], None,
                                           ALU.mult))
        kb.pad(V, 1)
        kb.on(V, lambda E: E.reduce_max(col4[:, 0:1], m8[:], axis=AX))
        kb.pad(V, 1)
        kb.on(V, lambda E: E.tensor_scalar(e1[:], m8[:], col4[:, 0:1], None,
                                           ALU.is_equal))
        kb.pad(V, 1)
        kb.on(V, lambda E: E.reduce_sum(col4[:, 1:2], e1[:], axis=AX))
        kb.pad(V, 1)
        kb.on(V, lambda E: E.scalar_tensor_tensor(
            m2[:], e1[:], cst(C_M1E30), m8[:], ALU.mult, ALU.add))
        kb.pad(V, 1)
        kb.on(V, lambda E: E.reduce_max(col4[:, 2:3], m2[:], axis=AX))
        kb.pad(V, 1)
        kb.on(V, lambda E: E.tensor_scalar(m2[:], m8[:], col4[:, 2:3], None,
                                           ALU.is_equal))
        kb.on(V, lambda E: E.tensor_scalar(col4[:, 3:4], col4[:, 1:2], 1.5,
                                           None, ALU.is_lt))
        kb.pad(V, 1)
        kb.on(V, lambda E: E.scalar_tensor_tensor(
            e1[:], m2[:], col4[:, 3:4], e1[:], ALU.mult, ALU.add))
        kb.pad(V, 1)
        kb.on(V, lambda E: E.tensor_tensor(m8[:], m8[:], e1[:], ALU.mult))
        tw = kb.inc_last(V, s_gd, 1)
        kb.wait(T, s_gd, tw)
        if qc_ >= 1:
            kb.wait(T, s_wte, qc_)
        kb.on(T, lambda E: E.transpose(bk[6].ap()[0:8, 0:128], m8[:],
                                       st.idf[:]))
        kb.inc_last(T, s_wtp, 1)
        kb.wait(V, s_wtp, qc_ + 1)
        if qc_ == 0:
            kb.wait(V, svm, 1)
        kb.on(V, lambda E, qc_=qc_: E.tensor_copy(
            wts[0:8, 128 * qc_:128 * (qc_ + 1)], bk[6].ap()[0:8, 0:128]))
        kb.inc_last(V, s_wte, 1)
    kb.wait(V, s_wte, QC)
    kb.on(V, lambda E: E.tensor_copy(wtsb[:], wts[:]))
    swb = kb.sem(fx + "_swb")
    kb.inc_last(V, swb, 1)
    kb.end_scope()
    kb.barrier(st.bar)

    kb.begin_scope()
    w12r = kb.sbs(fx + "_w12", [128, 2, 8, 2048], BF16)
    w3s = kb.sbs(fx + "_w3", [128, 16, 1024], BF16)
    glu = kb.sbs(fx + "_glu", [128, 16, 512], BF16)
    b1c = kb.sbs(fx + "_b1", [128, 2, 16], F32)
    b2c = kb.sbs(fx + "_b2", [128, 2, 16], F32)
    for e in range(10):
        rn1, rn2, rn3, ei = (("rw1", "rw2", "rw3", e) if e < 8 else
                             ("sw1", "sw2", "sw3", e - 8))
        bn1, bn2 = (("rb1t", "rb2t") if e < 8 else ("sb1t", "sb2t"))
        if e >= 2:
            kb.wait(Sy, s_gl, 64 * (e - 1))
        for bn, dst in ((bn1, b1c), (bn2, b2c)):
            kb.on(Sy, lambda E, bn=bn, dst=dst, ei=ei, e=e: E.dma_start(
                dst[:, e % 2, :], st.d[bn].ap()[ei]))
            kb.inc_last(Sy, s_b1[e % 2], 16)
        if e >= 1:
            kb.wait(G, s_h, 64 * e)
        for mi, rn in ((0, rn1), (1, rn2)):
            kb.on(G, lambda E, rn=rn, ei=ei, mi=mi: E.dma_start(
                w12r[:, mi, :, :],
                st.d[rn].ap()[ei].rearrange("(kc p) m -> p kc m", p=128)))
            kb.inc_last(G, s_w12, 16)
        if e >= 1:
            kb.wait(G, s_w3, 64 * e)
        kb.on(G, lambda E, rn3=rn3, ei=ei: E.dma_start(
            w3s[:], st.d[rn3].ap()[ei].rearrange("(kc p) m -> p kc m", p=128)))
        kb.inc_last(G, s_w3d, 16)
        for t4 in range(4):
            nbu = 4 * e + t4 + 1
            kb.wait(G, s_bc, 2 * max(0, nbu - 1))
            kb.on(G, lambda E, e=e, t4=t4: E.dma_start(
                wrow[:], wts[e:e + 1, 512 * t4:512 * (t4 + 1)]))
            kb.inc_last(G, s_wr, 16)
            kb.wait(T, s_wr, 16 * nbu)
            if e == 0 and t4 == 0:
                kb.wait(T, s_gd, 3 * QC)
            kb.wait(T, s_gl, 16 * max(0, nbu - 1))
            kb.on(T, lambda E: E.matmul(bk[5].ap()[:], st.ones_row[:],
                                        wrow[:], start=True, stop=True))
            tbc = kb.inc_last(T, s_bc, 1)
            kb.wait(A, s_bc, tbc)
            kb.wait(A, s_gl, 16 * max(0, nbu - 1))
            kb.on(A, lambda E: E.copy(wbc[:], bk[5].ap()[:]))
            kb.inc_last(A, s_bc, 1)
            kb.wait(T, s_w12, 32 * (e + 1))
            for ht in range(16):
                un = 16 * (4 * e + t4) + ht
                hb1 = bk[1]
                hb2 = bk[2]
                if un >= 1:
                    kb.wait(T, s_gl, un)
                for kc in range(8):
                    kb.on(T, lambda E, kc=kc, ht=ht, t4=t4: E.matmul(
                        hb1.ap()[:], w12r[:, 0, kc, 128 * ht:128 * (ht + 1)],
                        xh4[:, kc, 512 * t4:512 * (t4 + 1)],
                        start=(kc == 0), stop=(kc == 7)))
                for kc in range(8):
                    kb.on(T, lambda E, kc=kc, ht=ht, t4=t4: E.matmul(
                        hb2.ap()[:], w12r[:, 1, kc, 128 * ht:128 * (ht + 1)],
                        xh4[:, kc, 512 * t4:512 * (t4 + 1)],
                        start=(kc == 0), stop=(kc == 7)))
                tmm = kb.inc_last(T, s_h, 1)
                kb.wait(A, s_h, tmm)
                if t4 == 0 and ht == 0:
                    kb.wait(A, s_b1[e % 2], 32 * (e // 2 + 1))
                kb.on(A, lambda E, ht=ht, e=e: E.activation(
                    s1b[:], hb1.ap()[:], AF.Silu, bias=b1c[:, e % 2, ht:ht + 1]))
                tsl = kb.inc_last(A, s_sil, 1)
                kb.wait(V, s_sil, tsl)
                kb.wait(V, s_bc, 2 * nbu)
                kb.on(V, lambda E, ht=ht, e=e: E.scalar_tensor_tensor(
                    glu[:, ht, :], hb2.ap()[:], b2c[:, e % 2, ht:ht + 1],
                    s1b[:], ALU.add, ALU.mult))
                kb.on(V, lambda E, ht=ht: E.tensor_tensor(
                    glu[:, ht, :], glu[:, ht, :], wbc[:], ALU.mult))
                kb.inc_last(V, s_gl, 1)
            if e == 9 and t4 == 0:
                kb.wait(T, swb, 1)
            kb.wait(T, s_gl, 16 * (4 * e + t4 + 1))
            if t4 == 0:
                kb.wait(T, s_w3d, 16 * (e + 1))
            for half in range(2):
                for dm in range(8):
                    w3u = 16 * (4 * e + t4) + 8 * half + dm
                    ob = bk[3 + (w3u % 2)]
                    if w3u >= 2:
                        kb.wait(T, s_fa, w3u - 1)
                    last = (e == 9)
                    for kc in range(16):
                        kb.on(T, lambda E, kc=kc, dm=dm, half=half, t4=t4,
                              ob=ob, stp=(kc == 15 and not last): E.matmul(
                                  ob.ap()[:, 0:256],
                                  w3s[:, kc, 128 * dm:128 * (dm + 1)],
                                  glu[:, kc, 256 * half:256 * (half + 1)],
                                  start=(kc == 0), stop=stp))
                    if last:
                        kb.on(T, lambda E, dm=dm, half=half, t4=t4, ob=ob:
                              E.matmul(
                                  ob.ap()[:, 0:256],
                                  b3s[0:10, 128 * dm:128 * (dm + 1)],
                                  wtsb[0:10, 512 * t4 + 256 * half:
                                       512 * t4 + 256 * (half + 1)],
                                  start=False, stop=True))
                    tpm = kb.inc_last(T, s_w3, 1)
                    kb.wait(V, s_w3, tpm)
                    kb.on(V, lambda E, dm=dm, half=half, t4=t4, ob=ob:
                          E.tensor_tensor(
                              facc[:, dm, 512 * t4 + 256 * half:
                                   512 * t4 + 256 * (half + 1)],
                              facc[:, dm, 512 * t4 + 256 * half:
                                   512 * t4 + 256 * (half + 1)],
                              ob.ap()[:, 0:256], ALU.add))
                    kb.inc_last(V, s_fa, 1)
    kb.end_scope()
    kb.barrier(st.bar)

    kb.begin_scope()
    xc = kb.sbs(fx + "_xc", [128, QC, D], F32)
    kb.on(Sy, lambda E: E.dma_start(
        xc[:], st.xres.ap().rearrange("c p d -> p c d")))
    kb.inc_last(Sy, s_x, 32)
    kb.wait(V, s_x, 48)
    tv6 = bk[6].ap().bitcast(BF16)
    for qc_ in range(QC):
        if qc_ >= 1:
            kb.wait(T, s_ra, qc_)
        for dm in range(8):
            kb.on(T, lambda E, qc_=qc_, dm=dm: E.transpose(
                tv6[:, 128 * dm:128 * (dm + 1)],
                facc[:, dm, 128 * qc_:128 * (qc_ + 1)], st.idb[:]))
        kb.inc_last(T, s_tp2, 1)
        kb.wait(V, s_tp2, qc_ + 1)
        kb.on(V, lambda E, qc_=qc_: E.tensor_tensor(
            xc[:, qc_, :], xc[:, qc_, :], tv6[:], ALU.add))
        kb.inc_last(V, s_ra, 1)
    kb.wait(Sy, s_ra, QC)
    if pi == 0:
        kb.on(Sy, lambda E: E.dma_start(
            st.xres.ap().rearrange("c p d -> p c d"), xc[:]))
    else:
        kb.on(Sy, lambda E: E.dma_start(
            st.out.ap().rearrange("(c p) d -> p c d", p=128), xc[:]))
    kb.inc_last(Sy, s_x, 16)
    if pi == 0 and st.x2s is not None:
        kb.on(Sy, lambda E: E.dma_start(
            st.x2s.ap().rearrange("c p d -> p c d"), xc[:]))
        kb.inc_last(Sy, s_x, 16)
        kb.wait(Sy, s_x, 80)
    else:
        kb.wait(Sy, s_x, 64)
    kb.end_scope()
    kb.end_scope()


def p5_attnb(kb, st, cst):
    T, V, A, G, Sy = ENGS
    kb.begin_scope()
    x5 = kb.sbs("p5_x", [128, QC, D], F32)
    ckvA = kb.sbs("p5_ckv", [65, L], BF16)
    q5 = kb.sbs("p5_q", [128, 8, TPC], BF16)
    st.p5ckvA = ckvA
    st.p5q5 = q5
    col5 = kb.sbs("p5_col", [128, 16], F32)
    scr5 = kb.sbs("p5_scr", [128, D], F32)
    qn2 = kb.sbs("p5_qn2", [128, 16], F32)
    bqs = kb.sbs("p5_bqs", [16, 128], BF16)
    kn2 = kb.sbs("p5_kn2", [128, 64], F32)
    knm = kb.sbs("p5_knm", [1, 8], F32)
    st.p5knm = knm
    wvuTs = kb.sbs("p5_wvu", [64, D], BF16)
    bvub = kb.sbs("p5_bvu", [1, D], BF16)

    s_w = kb.sem("p5_w")
    s_wg = kb.semg("p5_wg")
    s_cc = kb.semg("p5_cc")
    s_bk = kb.sem("p5_bk")
    s_pe = kb.sem("p5_pe")
    s_dv = kb.sem("p5_dv")
    s_ac = kb.sem("p5_ac")
    s_ex = kb.sem("p5_ex")
    s_av = kb.sem("p5_av")
    s_nb = kb.sem("p5_nb")
    s_ci = kb.sem("p5_ci")
    s_ra = kb.sem("p5_ra")
    bk = st.bk
    bks = Banks(kb, st.bk, "p5")

    def pe(fn):
        kb.on(T, fn)

    def dv(fn):
        kb.on(V, fn)

    nbq = (kb.dout if "nbq" in kb.taps else kb.dint)(
        "p5_nbqd", [16, TPC], BF16)

    kb.on(Sy, lambda E: E.dma_start(
        x5[:], st.xres.ap().rearrange("c p d -> p c d")))
    s_x5 = kb.sem("p5_x5")
    kb.inc_last(Sy, s_x5, 16)
    kb.wait(V, s_x5, 16)
    kb.on(G, lambda E: E.dma_start(
        wvuTs[:], st.d["wvuT"].ap()))
    kb.inc_last(G, s_wg, 16)
    kb.on(G, lambda E: E.dma_start(bvub[:], st.d["bvur"].ap()))
    kb.inc_last(G, s_wg, 16)

    kb.begin_scope()
    xh5 = kb.sbs("p5_xh5", [128, 8, TPC], BF16)
    cq5 = kb.sbs("p5_cq", [128, 4, TPC], BF16)
    ckvo = kb.sbs("p5_ckvo", [64, TPC], BF16)
    wqdTs = kb.sbs("p5_wqd", [128, 8, ABQL], BF16)
    wqkTs = kb.sbs("p5_wqk", [128, 4, D], BF16)
    wkvdTs = kb.sbs("p5_wkv", [128, 8, ABKV], BF16)
    bqds = kb.sbs("p5_bqd", [128, 4], F32)
    bqks = kb.sbs("p5_bqk", [128, 8], F32)
    bkvs = kb.sbs("p5_bkv", [64, 1], F32)
    xh5s = kb.sbs("p5_xhs", [128, D], BF16)
    qsq = kb.sbs("p5_qsq", [64, TPC], BF16)
    ksq = kb.sbs("p5_ksq", [64, 2048], BF16)
    kb.on(G, lambda E: E.dma_start(
        wqdTs[:], st.d["wqdT"].ap().rearrange("(kc p) m -> p kc m", p=128)))
    kb.inc_last(G, s_wg, 16)
    kb.on(G, lambda E: E.dma_start(
        wqkTs[:], st.d["wqkT"].ap().rearrange("(kc p) m -> p kc m", p=128)))
    kb.inc_last(G, s_wg, 16)
    kb.on(G, lambda E: E.dma_start(
        wkvdTs[:], st.d["wkvdT"].ap().rearrange("(kc p) m -> p kc m", p=128)))
    kb.inc_last(G, s_wg, 16)
    kb.on(Sy, lambda E: E.dma_start(bqds[:], st.d["bqdt"].ap()))
    kb.inc_last(Sy, s_w, 16)
    kb.on(Sy, lambda E: E.dma_start(bqks[:], st.d["bqkt"].ap()))
    kb.inc_last(Sy, s_w, 16)
    kb.on(Sy, lambda E: E.dma_start(bkvs[:], st.d["bkvc"].ap()))
    kb.inc_last(Sy, s_w, 16)
    nw = s_w.n
    nwg = s_wg.n
    kb.wait(V, s_w, nw)
    kb.wait(V, s_wg, nwg)
    kb.on(V, lambda E: E.tensor_scalar(bqks[:], bqks[:], cst(C_EIGHTH), None,
                                       ALU.mult))
    t_bqscale = kb.inc_last(V, s_dv, 1)
    kb.wait(A, s_w, nw)
    kb.wait(A, s_dv, t_bqscale)
    s_ln5 = kb.sem("p5_ln")
    s_tr5 = kb.sem("p5_tr")
    s_xf5 = kb.sem("p5_xf")
    trv = bk[0].ap().bitcast(BF16)
    for qc_ in range(QC):
        kb.on(V, lambda E, qc_=qc_: E.tensor_tensor(
            scr5[:], x5[:, qc_, :], x5[:, qc_, :], ALU.mult))
        kb.on(V, lambda E: E.reduce_sum(col5[:, 1:2], scr5[:], axis=AX))
        kb.on(V, lambda E, qc_=qc_: E.reduce_sum(col5[:, 0:1],
                                                 x5[:, qc_, :], axis=AX))
        kb.on(V, lambda E: E.tensor_scalar(col5[:, 0:1], col5[:, 0:1],
                                           cst(C_INV1024), None, ALU.mult))
        kb.on(V, lambda E: E.tensor_scalar(col5[:, 1:2], col5[:, 1:2],
                                           cst(C_INV1024), None, ALU.mult))
        kb.on(V, lambda E: E.tensor_tensor(col5[:, 2:3], col5[:, 0:1],
                                           col5[:, 0:1], ALU.mult))
        kb.pad(V, 1)
        kb.on(V, lambda E: E.tensor_tensor(col5[:, 1:2], col5[:, 1:2],
                                           col5[:, 2:3], ALU.subtract))
        kb.pad(V, 1)
        kb.rsqrt(V, col5[:, 3:4], col5[:, 1:2], col5[:, 4:5], col5[:, 5:6],
                 col5[:, 6:7], eps_col=cst(C_EPS5))
        if qc_ >= 1:
            kb.wait(V, s_tr5, qc_)
        kb.on(V, lambda E, qc_=qc_: E.tensor_scalar(
            xh5s[:], x5[:, qc_, :], col5[:, 0:1], col5[:, 3:4],
            ALU.subtract, ALU.mult))
        kb.inc_last(V, s_ln5, 1)
        kb.wait(T, s_ln5, qc_ + 1)
        if qc_ >= 1:
            kb.wait(T, s_xf5, qc_)
        for j in range(8):
            kb.on(T, lambda E, j=j: E.transpose(
                trv[:, 128 * j:128 * (j + 1)], xh5s[:, 128 * j:128 * (j + 1)],
                st.idb[:]))
        kb.inc_last(T, s_tr5, 1)
        kb.wait(A, s_tr5, qc_ + 1)
        kb.on(A, lambda E, qc_=qc_: E.copy(
            xh5[:, :, 128 * qc_:128 * (qc_ + 1)],
            trv[:].rearrange("p (a b) -> p a b", a=8)))
        kb.inc_last(A, s_xf5, 1)
    kb.wait(T, s_xf5, QC)
    kb.wait(T, s_w, nw)
    kb.wait(T, s_wg, nwg)
    s_cp = kb.sem("p5_cp")
    s_ce = kb.sem("p5_ce")
    ncp = 0
    for t4 in range(4):
        for mt in range(5):
            ncp += 1
            pb = bk[1 + (ncp % 2)]
            if ncp >= 3:
                kb.wait(T, s_ce, ncp - 2)
            for kc in range(8):
                if mt < 4:
                    pe(lambda E, kc=kc, mt=mt, t4=t4, pb=pb: E.matmul(
                        pb.ap()[:], wqdTs[:, kc, 128 * mt:128 * (mt + 1)],
                        xh5[:, kc, 512 * t4:512 * (t4 + 1)],
                        start=(kc == 0), stop=(kc == 7)))
                else:
                    pe(lambda E, kc=kc, t4=t4, pb=pb: E.matmul(
                        pb.ap()[0:64, :], wkvdTs[:, kc, :],
                        xh5[:, kc, 512 * t4:512 * (t4 + 1)],
                        start=(kc == 0), stop=(kc == 7)))
            kb.inc_last(T, s_cp, 1)
            kb.wait(A, s_cp, ncp)
            if mt < 4:
                kb.on(A, lambda E, mt=mt, t4=t4, pb=pb: E.activation(
                    cq5[:, mt, 512 * t4:512 * (t4 + 1)], pb.ap()[:], AF.Identity,
                    bias=bqds[:, mt:mt + 1]))
            else:
                kb.on(A, lambda E, t4=t4, pb=pb: E.activation(
                    ckvo[:, 512 * t4:512 * (t4 + 1)], pb.ap()[0:64, :],
                    AF.Identity, bias=bkvs[:, 0:1]))
            kb.inc_last(A, s_ce, 1)
    kb.wait(Sy, s_ce, 20)
    kb.on(Sy, lambda E: E.dma_start(st.ckv_in.ap(), ckvo[:]))
    kb.inc_last(Sy, s_ci, 16)
    kb.wait(G, s_ci, 16)
    kb.on(G, lambda E: E.collective_compute(
        "AllGather", ALU.bypass, replica_groups=[list(range(8))],
        ins=[st.ckv_in.ap().opt()], outs=[st.ckv_out.ap().opt()]))
    kb.inc_last(G, s_cc, 1)
    kb.wait(Sy, s_cc, 1)
    kb.on(Sy, lambda E: dyn_dma(
        E, st, ckvA[0:64, :].rearrange("p (j t) -> p j t", j=4),
        lambda a: st.ckv_out.ap()[a, :].rearrange("(j r) t -> r j t", r=64),
        [(7, 256)]))
    kb.inc_last(Sy, s_bk, 16)
    kb.wait(V, s_bk, 16)
    kb.on(V, lambda E: E.memset(ckvA[64:65, :], 1.0))
    kb.inc_last(V, s_dv, 1)
    t_ones = s_dv.n

    nq = 0
    s_qp = kb.sem("p5_qp")
    s_qe = kb.sem("p5_qe")
    kb.wait(T, s_ce, 20)
    for t4 in range(4):
        for mt in range(8):
            nq += 1
            pb = bk[1 + (nq % 2)]
            if nq >= 3:
                kb.wait(T, s_qe, nq - 2)
            for kc in range(4):
                pe(lambda E, kc=kc, mt=mt, t4=t4, pb=pb: E.matmul(
                    pb.ap()[:], wqkTs[:, kc, 128 * mt:128 * (mt + 1)],
                    cq5[:, kc, 512 * t4:512 * (t4 + 1)],
                    start=(kc == 0), stop=(kc == 3)))
            kb.inc_last(T, s_qp, 1)
            kb.wait(A, s_qp, nq)
            kb.on(A, lambda E, mt=mt, t4=t4, pb=pb: E.activation(
                q5[:, mt, 512 * t4:512 * (t4 + 1)], pb.ap()[:], AF.Identity,
                bias=bqks[:, mt:mt + 1], scale=cst(C_EIGHTH)))
            kb.inc_last(A, s_qe, 1)

    if st.cq5s is not None:
        sdbg = kb.sem("p5_dbg")
        kb.wait(Sy, s_ce, 20)
        kb.on(Sy, lambda E: E.dma_start(st.cq5s.ap(), cq5[:]))
        kb.inc_last(Sy, sdbg, 16)
        kb.on(Sy, lambda E: E.dma_start(st.xh5s2.ap(), xh5[:]))
        kb.inc_last(Sy, sdbg, 16)
        kb.wait(Sy, sdbg, 32)
    if st.q5s is not None:
        sdbg2 = kb.sem("p5_dbg2")
        kb.wait(Sy, s_qe, 32)
        kb.on(Sy, lambda E: E.dma_start(st.q5s.ap(), q5[:]))
        kb.inc_last(Sy, sdbg2, 16)
        if st.cq5l is not None:
            kb.on(Sy, lambda E: E.dma_start(st.cq5l.ap(), cq5[:]))
            kb.inc_last(Sy, sdbg2, 16)
        kb.wait(Sy, sdbg2, sdbg2.n)

    kb.wait(A, s_bk, 16)
    for pc in range(4):
        if pc >= 1:
            kb.wait(A, s_pe, s_pe.n)
        kb.on(A, lambda E, pc=pc: E.activation(
            ksq[:], ckvA[0:64, 2048 * pc:2048 * (pc + 1)], AF.Square))
        tks = kb.inc_last(A, s_ac, 1)
        kb.wait(T, s_ac, tks)
        for c in range(16):
            pe(lambda E, c=c, pc=pc: E.matmul(
                bk[5].ap()[:, 16 * pc + c:16 * pc + c + 1],
                ksq[0:64, 128 * c:128 * (c + 1)],
                st.ones_bf[0:64, :], start=True, stop=True))
        kb.inc_last(T, s_pe, 1)
    tkn = s_pe.n
    kb.wait(V, s_pe, tkn)
    dv(lambda E: E.tensor_copy(kn2[:], bk[5].ap()[:, 0:64]))
    kb.pad(V, 1)
    dv(lambda E: E.reduce_max(col5[:, 0:1], kn2[:], axis=AX))
    tkm = kb.inc_last(V, s_dv, 1)
    kb.wait(T, s_dv, tkm)
    pe(lambda E: E.transpose(bk[5].ap()[0:1, 64:192], col5[:, 0:1],
                             st.idf[:]))
    tkt = kb.inc_last(T, s_pe, 1)
    kb.wait(V, s_pe, tkt)
    dv(lambda E: E.reduce_max(knm[:, 0:1], bk[5].ap()[0:1, 64:192], axis=AX))
    kb.pad(V, 1)
    kb.rsqrt(V, knm[:, 1:2], knm[:, 0:1], knm[:, 2:3], knm[:, 3:4],
             knm[:, 4:5], eps_col=cst(C_EPS6, 1))
    dv(lambda E: E.tensor_tensor(knm[:, 2:3], knm[:, 0:1], knm[:, 1:2],
                                 ALU.mult))
    kb.pad(V, 1)
    dv(lambda E: E.tensor_scalar(knm[:, 2:3], knm[:, 2:3], -1.0, None,
                                 ALU.mult))
    tneg = kb.inc_last(V, s_dv, 1)
    kb.wait(T, s_dv, tneg)
    pe(lambda E: E.matmul(bk[5].ap()[:, 192:193], st.ones_row[:],
                          knm[:, 2:3], start=True, stop=True))
    tbc5 = kb.inc_last(T, s_pe, 1)
    kb.wait(V, s_pe, tbc5)
    dv(lambda E: E.tensor_copy(col5[:, 8:9], bk[5].ap()[:, 192:193]))
    tnk = kb.inc_last(V, s_dv, 1)

    s_qn = kb.sem("p5_qn")
    for h in range(16):
        kb.on(A, lambda E, h=h: E.activation(
            qsq[:], q5[64 * (h % 2):64 * (h % 2) + 64, h // 2, :], AF.Square))
        tqs = kb.inc_last(A, s_ac, 1)
        kb.wait(T, s_ac, tqs)
        if h >= 1:
            kb.wait(T, s_qn, 2 * h)
        for tc in range(16):
            pe(lambda E, tc=tc: E.matmul(
                bk[6].ap()[:, tc:tc + 1], qsq[0:64, 128 * tc:128 * (tc + 1)],
                st.ones_bf[0:64, :], start=True, stop=True))
        tqp = kb.inc_last(T, s_pe, 1)
        kb.wait(V, s_pe, tqp)
        kb.wait(V, s_dv, tnk)
        dv(lambda E: E.tensor_copy(qn2[:], bk[6].ap()[:, 0:16]))
        kb.inc_last(V, s_qn, 1)
        kb.pad(V, 1)
        kb.rsqrt(V, scr5[:, 0:16], qn2[:], scr5[:, 16:32], scr5[:, 32:48],
                 scr5[:, 48:64], eps_col=cst(C_EPS6))
        dv(lambda E: E.tensor_tensor(scr5[:, 0:16], scr5[:, 0:16], qn2[:],
                                     ALU.mult))
        kb.pad(V, 1)
        dv(lambda E: E.tensor_scalar(scr5[:, 0:16], scr5[:, 0:16],
                                     col5[:, 8:9], None, ALU.mult))
        tbq = kb.inc_last(V, s_dv, 1)
        kb.wait(T, s_dv, tbq)
        pe(lambda E: E.transpose(bk[6].ap()[0:16, 16:144], scr5[:, 0:16],
                                 st.idf[:]))
        tqt = kb.inc_last(T, s_pe, 1)
        kb.wait(V, s_pe, tqt)
        if h >= 1:
            kb.wait(V, s_nb, 16 * h)
        dv(lambda E: E.tensor_copy(bqs[:], bk[6].ap()[0:16, 16:144]))
        tbe = kb.inc_last(V, s_qn, 1)
        kb.wait(Sy, s_qn, tbe)
        kb.on(Sy, lambda E, h=h: E.dma_start(
            nbq.ap()[h].rearrange("(c t) -> c t", c=16), bqs[:]))
        kb.inc_last(Sy, s_nb, 16)
    kb.end_scope()

    yfull = kb.sbs("p5_yfull", [128, 8, TPC], BF16)
    st.p5yfull = yfull
    probs = kb.sbs("p5_pr", [128, 2, 2048], BF16)
    vh = kb.sbs("p5_vh", [128, 64, 65], BF16)
    wva = kb.sbs("p5_wva", [65, 64], BF16)
    wprTs = kb.sbs("p5_wpr", [128, 8, D], BF16)
    bprs = kb.sbs("p5_bpr", [128, 8], F32)
    nbqr = kb.sbs("p5_nbqr", [1, 512], BF16)
    qstg = kb.sbs("p5_qstg", [64, 512], BF16)
    rbs = kb.sbs("p5_rbs", [128, 512], BF16)
    pstg = kb.sbs("p5_pstg", [128, 512], BF16)
    yr = kb.sbs("p5_yr", [1, 512], F32)
    yrb = kb.sbs("p5_yrb", [1, 512], BF16)
    kb.wait(G, s_qn, 32)
    kb.on(G, lambda E: E.dma_start(
        wprTs[:], st.d["wprT"].ap().rearrange("(kc p) m -> p kc m", p=128)))
    kb.inc_last(G, s_wg, 16)
    kb.on(Sy, lambda E: E.dma_start(bprs[:], st.d["bprt"].ap()))
    kb.inc_last(Sy, s_w, 16)
    nw5 = s_w.n
    nw5g = s_wg.n
    s_sc = kb.sem("p5_sc")
    s_exp = kb.sem("p5_exp")
    s_avm = kb.sem("p5_avm")
    s_vh = kb.sem("p5_vh")
    s_nbr = kb.sem("p5_nbr")
    s_ye = kb.sem("p5_ye")
    kb.on(V, lambda E: E.memset(vh[:, :, 64:65], 1.0))
    kb.inc_last(V, s_vh, 1)
    nsc = 0
    nexp = 0
    nav = 0
    kb.wait(V, s_wg, nw5g)
    for h in range(16):
        kb.wait(V, s_avm, nav)
        dv(lambda E, h=h: E.tensor_copy(wva[0:64, :],
                                        wvuTs[0:64, 64 * h:64 * (h + 1)]))
        dv(lambda E, h=h: E.tensor_copy(wva[64:65, :],
                                        bvub[0:1, 64 * h:64 * (h + 1)]))
        twv = kb.inc_last(V, s_vh, 1)
        kb.wait(T, s_vh, twv)
        for g4 in range(16):
            nvh = 16 * h + g4
            if nvh >= 1:
                kb.wait(T, s_vh, s_vh.n)
            for k4 in range(4):
                kt = 4 * g4 + k4
                pe(lambda E, kt=kt, k4=k4: E.matmul(
                    bk[7].ap()[:, 64 * k4:64 * (k4 + 1)],
                    ckvA[:, 128 * kt:128 * (kt + 1)], wva[:],
                    start=True, stop=True))
            tvm = kb.inc_last(T, s_pe, 1)
            kb.wait(V, s_pe, tvm)
            dv(lambda E, g4=g4: E.tensor_copy(
                vh[:, 4 * g4:4 * (g4 + 1), 0:64],
                bk[7].ap()[:, 0:256].rearrange("p (a b) -> p a b", a=4)))
            kb.inc_last(V, s_vh, 1)
        tvh_done = s_vh.n
        for qt in range(4):
            if h + qt > 0:
                kb.wait(Sy, s_sc, nsc)
            kb.on(Sy, lambda E, h=h, qt=qt: E.dma_start(
                nbqr[:],
                nbq.ap()[h, 512 * qt:512 * (qt + 1)].rearrange("(o t) -> o t",
                                                               o=1)))
            kb.inc_last(Sy, s_nbr, 16)
            kb.wait(V, s_sc, nsc)
            dv(lambda E, h=h, qt=qt: E.tensor_copy(
                qstg[:], q5[64 * (h % 2):64 * (h % 2) + 64, h // 2,
                            512 * qt:512 * (qt + 1)]))
            tqs2 = kb.inc_last(V, s_ye, 1)
            kb.wait(T, s_ye, tqs2)
            kb.wait(T, s_nbr, 16 * (4 * h + qt + 1))
            kb.wait(T, s_vh, tvh_done)
            for kg in range(16):
                sl = kg % 2
                for k4 in range(4):
                    kt = 4 * kg + k4
                    if nexp >= 4:
                        kb.wait(T, s_exp, nexp - 3)
                    pe(lambda E, kt=kt, k4=k4: E.matmul(
                        bk[k4].ap()[:], ckvA[0:64, 128 * kt:128 * (kt + 1)],
                        qstg[:], start=True, stop=False))
                    pe(lambda E, k4=k4: E.matmul(
                        bk[k4].ap()[:], st.ones_rb[:], nbqr[:],
                        start=False, stop=True))
                    nsc += 1
                    kb.inc_last(T, s_sc, 1)
                    kb.wait(A, s_sc, nsc)
                    if nexp >= 8:
                        kb.wait(A, s_avm, nexp - 7)
                    kb.on(A, lambda E, sl=sl, k4=k4: E.activation(
                        probs[:, sl, 512 * k4:512 * (k4 + 1)],
                        bk[k4].ap()[:], AF.Exp))
                    nexp += 1
                    kb.inc_last(A, s_exp, 1)
                kb.wait(T, s_exp, nexp)
                for k4 in range(4):
                    kt = 4 * kg + k4
                    pe(lambda E, kt=kt, sl=sl, k4=k4: E.matmul(
                        bk[4].ap()[0:65, :], vh[:, kt, :],
                        probs[:, sl, 512 * k4:512 * (k4 + 1)],
                        start=(kt == 0), stop=(kt == 63)))
                    nav += 1
                    kb.inc_last(T, s_avm, 1)
            kb.wait(V, s_avm, nav)
            dv(lambda E: E.reciprocal(yr[:], bk[4].ap()[64:65, 0:512]))
            kb.pad(V, 1)
            dv(lambda E: E.tensor_copy(yrb[:], yr[:]))
            tyr = kb.inc_last(V, s_ye, 1)
            kb.wait(T, s_ye, tyr)
            pe(lambda E: E.matmul(bk[5].ap()[:], st.ones_rb[:], yrb[:],
                                  start=True, stop=True))
            tbcy = kb.inc_last(T, s_pe, 1)
            kb.wait(V, s_pe, tbcy)
            dv(lambda E: E.tensor_copy(rbs[:], bk[5].ap()[:]))
            dv(lambda E, h=h, qt=qt: E.tensor_tensor(
                yfull[64 * (h % 2):64 * (h % 2) + 64, h // 2,
                      512 * qt:512 * (qt + 1)],
                bk[4].ap()[0:64, :], rbs[0:64, :], ALU.mult))
            kb.inc_last(V, s_ye, 1)
            kb.wait(T, s_ye, s_ye.n)

    kb.wait(T, s_w, nw5)
    kb.wait(T, s_wg, nw5g)
    kb.wait(A, s_w, nw5)
    s_pp5 = kb.sem("p5_pp")
    s_pse = kb.sem("p5_pse")
    s_ptr = kb.sem("p5_ptr")
    s_ra5 = kb.sem("p5_ra")
    tv6 = bk[6].ap().bitcast(BF16)
    npj = 0
    for qt in range(4):
        for dm in range(8):
            npj += 1
            pb = bk[6 + (npj % 2)]
            if npj >= 3:
                kb.wait(T, s_pse, npj - 2)
            for kc in range(8):
                pe(lambda E, kc=kc, dm=dm, qt=qt, pb=pb: E.matmul(
                    pb.ap()[:], wprTs[:, kc, 128 * dm:128 * (dm + 1)],
                    yfull[:, kc, 512 * qt:512 * (qt + 1)],
                    start=(kc == 0), stop=(kc == 7)))
            kb.inc_last(T, s_pp5, 1)
            kb.wait(A, s_pp5, npj)
            if npj >= 2:
                kb.wait(A, s_ptr, npj - 1)
            kb.on(A, lambda E, dm=dm, pb=pb: E.activation(
                pstg[:], pb.ap()[:], AF.Identity, bias=bprs[:, dm:dm + 1]))
            kb.inc_last(A, s_pse, 1)
            kb.wait(T, s_pse, npj)
            if npj >= 2:
                kb.wait(T, s_ra5, npj - 1)
            for tc in range(4):
                pe(lambda E, tc=tc: E.transpose(
                    bk[5].ap().bitcast(BF16)[:, 128 * tc:128 * (tc + 1)],
                    pstg[:, 128 * tc:128 * (tc + 1)], st.idb[:]))
            kb.inc_last(T, s_ptr, 1)
            kb.wait(V, s_ptr, npj)
            for tc in range(4):
                dv(lambda E, qt=qt, tc=tc, dm=dm: E.tensor_tensor(
                    x5[:, 4 * qt + tc, 128 * dm:128 * (dm + 1)],
                    x5[:, 4 * qt + tc, 128 * dm:128 * (dm + 1)],
                    bk[5].ap().bitcast(BF16)[:, 128 * tc:128 * (tc + 1)],
                    ALU.add))
            kb.inc_last(V, s_ra5, 1)
    kb.wait(Sy, s_ra5, 32)
    kb.on(Sy, lambda E: E.dma_start(
        st.xres.ap().rearrange("c p d -> p c d"), x5[:]))
    kb.inc_last(Sy, s_x5, 16)
    if st.x3s is not None:
        kb.on(Sy, lambda E: E.dma_start(
            st.x3s.ap().rearrange("c p d -> p c d"), x5[:]))
        kb.inc_last(Sy, s_x5, 16)
        kb.wait(Sy, s_x5, 48)
    else:
        kb.wait(Sy, s_x5, 32)
    kb.end_scope()


# version 07 (final)
# speedup vs baseline: 1.0017x; 1.0017x over previous
import os
import numpy as np
import concourse.bass as bass
import concourse.mybir as mybir
from concourse.bass_utils import run_bass_kernel_spmd
from concourse.replica_groups import maybe_share_collective_output_space
from concourse.bass_types import DynSlice

F32 = mybir.dt.float32
BF16 = mybir.dt.bfloat16
I32 = mybir.dt.int32
AF = mybir.ActivationFunctionType
ALU = mybir.AluOpType
AX = mybir.AxisListType.X

NCORES = 8
B, L, D = 2, 8192, 1024
NH, DK, DV = 4, 256, 256
CH = 128
NC = L // CH
TPC = L // 4
QC = TPC // CH
FFN_H = 2048
ABH, ABHD, ABKV, ABQL = 16, 64, 64, 512

ENGS = ("tensor", "vector", "scalar", "gpsimd", "sync")

CONSTS = [1.0, 0.5, 1e-5, 1e-6, 1e-4, -0.5, 1.0 / 1024, 1.0 / 256,
          -1e30, 1.5, 1.1920928955078125e-07, 2.0, 0.125, 1.0 / FFN_H, 0.0,
          -1.0]
(C_ONE, C_HALF, C_EPS5, C_EPS6, C_EPS4, C_NHALF, C_INV1024, C_INV256,
 C_M1E30, C_3HALF, C_ULP, C_TWO, C_EIGHTH, C_INVH, C_ZERO,
 C_MONE) = range(len(CONSTS))


DYN_MAXS = [32, 6144, 124, 124, 124, 124, 1024, 256]


def dyn_dma(E, st, out_ap, in_ap_fn, ks_sizes, in_off=None, in_size=None):
    sls = []
    regs = []
    for i, (k, size) in enumerate(ks_sizes):
        v = E.value_load(st.dyn_sb[0:1, k:k + 1])
        if in_off is not None and in_off[i]:
            v = v + in_off[i]
        sz = in_size[i] if in_size is not None else size
        sls.append(DynSlice(v, sz))
        regs.append(v)
    inst = E.dma_start(out_ap, in_ap_fn(*sls))
    for r in regs:
        try:
            E.free_register(r.val)
        except Exception:
            pass
    return inst


class Sem:

    def __init__(self, handle, name):
        self.h = handle
        self.name = name
        self.n = 0
        self.base = 0

    def bump(self, k=1):
        self.n += k
        return self.n


class Kb:

    def __init__(self, taps=()):
        import os
        self.nc = bass.Bass(
            "TRN2", target_bir_lowering=False,
            detect_race_conditions=not os.environ.get("KSIMNORACE"))
        self.taps = set(taps)
        self.tap_tensors = {}
        self.q = {e: [] for e in ENGS}
        self._last_real = {e: None for e in ENGS}
        self._ctx = []
        self._sems = []
        self.junk = None
        self._names = set()

    def _uniq(self, name):
        assert name not in self._names, f"dup tensor name {name}"
        self._names.add(name)
        return name

    def sb(self, name, shape, dt):
        g = self.nc.sbuf_tensor(self._uniq(name), shape, dt)
        t = g.__enter__()
        self._ctx.append(g)
        return t

    def ps(self, name, shape, dt=F32):
        g = self.nc.psum_tensor(self._uniq(name), shape, dt)
        t = g.__enter__()
        self._ctx.append(g)
        return t

    def sem_dedicated(self, name):
        g = self.nc.semaphore(name)
        h = g.__enter__()
        self._ctx.append(g)
        s = Sem(h, name)
        self._sems.append(s)
        return s

    def new_phase(self):
        self._pool_cursor = 0

    def sem(self, name):
        if not hasattr(self, "_pool"):
            self._pool = []
            self._pool_cursor = 0
        i = self._pool_cursor
        self._pool_cursor += 1
        if i >= len(self._pool):
            self._pool.append(self.sem_dedicated(f"gp{i}"))
        s = self._pool[i]
        s.base += s.n
        s.n = 0
        return s

    def semg(self, name):
        return self.sem_dedicated(name)

    def din(self, name, shape, dt=F32):
        return self.nc.dram_tensor(self._uniq(name), shape, dt,
                                   kind="ExternalInput")

    def dout(self, name, shape, dt=F32):
        return self.nc.dram_tensor(self._uniq(name), shape, dt,
                                   kind="ExternalOutput")

    def dint(self, name, shape, dt=F32, addr_space="Local"):
        return self.nc.dram_tensor(self._uniq(name), shape, dt,
                                   kind="Internal", addr_space=addr_space)

    def on(self, eng, fn, real=True):
        import os
        tr = os.environ.get("KTRUNC")
        self._ecount = getattr(self, "_ecount", 0) + 1
        if tr is not None and self._ecount > int(tr):
            return None
        ent = [fn, [], real]
        self.q[eng].append(ent)
        if real:
            self._last_real[eng] = ent
        return ent

    def inc_last(self, eng, sem, k=1):
        import os
        if os.environ.get("KTRUNC") is not None and (
                self._ecount > int(os.environ["KTRUNC"])):
            return sem.bump(k)
        ent = self.q[eng][-1]
        assert ent[2], "then_inc target must be a real instruction"
        ent[1].append((sem, k))
        return sem.bump(k)

    def wait(self, eng, sem, tgt):
        self.on(eng, lambda E, s=sem.h, t=sem.base + tgt: E.wait_ge(s, t),
                real=False)

    def pad(self, eng, k=2):
        assert self.junk is not None
        for i in range(k):
            if eng == "scalar":
                self.on(eng, lambda E, i=i: E.copy(
                    self.junk[:, 4 + i % 2:5 + i % 2],
                    self.junk[:, i % 2:i % 2 + 1]))
            else:
                self.on(eng, lambda E, i=i: E.tensor_copy(
                    self.junk[:, 6 + i % 2:7 + i % 2],
                    self.junk[:, 2 + i % 2:3 + i % 2]))

    def barrier(self, bar, engs=ENGS):
        if not hasattr(self, "_bar_aux"):
            self._bar_aux = {e: self.sem_dedicated(f"aux_{e}") for e in ENGS}
        for e in engs:
            ent = self._last_real[e]
            if ent is not None:
                if not ent[1]:
                    assert e not in ("sync", "gpsimd"), \
                        f"DMA on {e} without a semaphore before barrier"
                    ent[1].append((self._bar_aux[e], 1))
                    self._bar_aux[e].bump(1)
                s0, _ = ent[1][-1]
                t0 = s0.base + s0.n
                self.on(e, lambda E, s=s0.h, t=t0: E.wait_ge(s, t), real=False)
            self.on(e, lambda E, s=bar.h: E.sem_inc(s, 1), real=False)
            bar.bump(1)
        tgt = bar.n
        for e in engs:
            self.wait(e, bar, tgt)
            self._last_real[e] = None

    def finalize(self):
        nc = self.nc
        with nc.Block() as blk:
            @blk.tensor
            def _(E):
                for fn, incs, _ in self.q["tensor"]:
                    inst = fn(E)
                    for s, k in incs:
                        inst.then_inc(s.h, k)

            @blk.vector
            def _(E):
                for fn, incs, _ in self.q["vector"]:
                    inst = fn(E)
                    for s, k in incs:
                        inst.then_inc(s.h, k)

            @blk.scalar
            def _(E):
                for fn, incs, _ in self.q["scalar"]:
                    inst = fn(E)
                    for s, k in incs:
                        inst.then_inc(s.h, k)

            @blk.gpsimd
            def _(E):
                for fn, incs, _ in self.q["gpsimd"]:
                    inst = fn(E)
                    for s, k in incs:
                        inst.then_inc(s.h, k)

            @blk.sync
            def _(E):
                for fn, incs, _ in self.q["sync"]:
                    inst = fn(E)
                    for s, k in incs:
                        inst.then_inc(s.h, k)
        for g in reversed(self._ctx):
            g.__exit__(None, None, None)
        return nc

    def begin_scope(self):
        if not hasattr(self, "_scopes"):
            self._scopes = []
        self._scopes.append([])

    def sbs(self, name, shape, dt):
        g = self.nc.sbuf_tensor(self._uniq(name), shape, dt)
        t = g.__enter__()
        self._scopes[-1].append(g)
        return t

    def end_scope(self):
        for g in reversed(self._scopes.pop()):
            g.__exit__(None, None, None)

    def rsqrt(self, eng, out, x, s1, s2, s3, eps_col=None, nsteps=2):
        def e(fn):
            self.on(eng, fn)
            self.pad(eng, 1)
        if eps_col is not None:
            e(lambda E: E.tensor_scalar(s1, x, eps_col, None, ALU.add))
            xs = s1
        else:
            xs = x
        e(lambda E: E.tensor_scalar(s2.bitcast(I32), xs.bitcast(I32),
                                    1, None, ALU.arith_shift_right))
        e(lambda E: E.tensor_scalar(s2.bitcast(I32), s2.bitcast(I32),
                                    -1, 0x5f3759df, ALU.mult, ALU.add))
        for it in range(nsteps):
            e(lambda E: E.tensor_tensor(s3, xs, s2, ALU.mult))
            e(lambda E: E.tensor_tensor(s3, s3, s2, ALU.mult))
            e(lambda E: E.tensor_scalar(s3, s3, -0.5, 1.5, ALU.mult, ALU.add))
            dst = out if it == nsteps - 1 else s2
            e(lambda E, d=dst: E.tensor_tensor(d, s3, s2, ALU.mult))
        return out



def _A(x):
    return np.ascontiguousarray(np.asarray(x, dtype=np.float32))


def core_inputs(p, c):
    b, h = divmod(c, 4)
    hs = slice(256 * h, 256 * h + 256)
    d = {}
    d["xb"] = _A(p["L0_attn_a_hidden_states"][b])
    d["wq"] = _A(p["L0_attn_a_Wq"][:, hs])
    d["wk"] = _A(p["L0_attn_a_Wk"][:, hs])
    d["wv"] = _A(p["L0_attn_a_Wv"][:, hs])
    d["wb"] = _A(np.asarray(p["L0_attn_a_Wb"])[:, h].reshape(8, 128).T)
    d["cw_q"] = _A(p["L0_attn_a_conv_q_w"][hs])
    d["cw_k"] = _A(p["L0_attn_a_conv_k_w"][hs])
    d["cw_v"] = _A(p["L0_attn_a_conv_v_w"][hs])
    d["firs"] = _A(p["L0_attn_a_fir_short_w"][h])
    d["firl"] = _A(p["L0_attn_a_fir_long_w"][h])
    d["w1a"] = _A(p["L0_attn_a_mlp_w1"][:1024])
    perm = [8 * (j // 2) + 4 * (j % 2) + hh for hh in range(4) for j in range(8)]
    d["w1b"] = _A(p["L0_attn_a_mlp_w1"][1024:][perm])
    d["w2"] = _A(p["L0_attn_a_mlp_w2"])
    d["b1t"] = _A(np.asarray(p["L0_attn_a_mlp_b1"]).reshape(16, 128).T)
    d["b2c"] = _A(np.asarray(p["L0_attn_a_mlp_b2"]).reshape(16, 1))
    d["glt"] = _A(np.asarray(p["L0_attn_a_gate_log_temp"])[h].reshape(1, 1))
    d["onw"] = _A(np.tile(p["L0_attn_a_o_norm_w"], (128, 1)))
    d["wo"] = _A(p["L0_attn_a_Wo"])
    d["wqdT"] = _A(np.asarray(p["L0_attn_b_wq_d"]).T)
    d["bqdt"] = _A(np.asarray(p["L0_attn_b_bq_d"]).reshape(4, 128).T)
    d["wqkT"] = _A(np.asarray(p["L0_attn_b_w_qk"]).T)
    d["bqkt"] = _A(np.asarray(p["L0_attn_b_b_qk"]).reshape(8, 128).T)
    d["wkvdT"] = _A(np.asarray(p["L0_attn_b_wkv_d"]).T)
    d["bkvc"] = _A(np.asarray(p["L0_attn_b_bkv_d"]).reshape(64, 1))
    d["wvuT"] = _A(np.asarray(p["L0_attn_b_wv_u"]).T)
    d["bvur"] = _A(np.asarray(p["L0_attn_b_bv_u"]).reshape(1, 1024))
    d["wprT"] = _A(np.asarray(p["L0_attn_b_w_proj"]).T)
    d["bprt"] = _A(np.asarray(p["L0_attn_b_b_proj"]).reshape(8, 128).T)
    d["gw"] = _A(p["L0_ffn_gate_w"])
    d["gb128"] = _A(np.tile(p["L0_ffn_gate_b"], (128, 1)))
    d["rw1"] = _A(p["L0_ffn_rw1"])
    d["rw2"] = _A(p["L0_ffn_rw2"])
    d["rw3"] = _A(p["L0_ffn_rw3"])
    d["sw1"] = _A(p["L0_ffn_sw1"])
    d["sw2"] = _A(p["L0_ffn_sw2"])
    d["sw3"] = _A(p["L0_ffn_sw3"])
    d["rb1t"] = _A(np.asarray(p["L0_ffn_rb1"]).reshape(8, 16, 128).transpose(0, 2, 1))
    d["rb2t"] = _A(np.asarray(p["L0_ffn_rb2"]).reshape(8, 16, 128).transpose(0, 2, 1))
    d["sb1t"] = _A(np.asarray(p["L0_ffn_sb1"]).reshape(2, 16, 128).transpose(0, 2, 1))
    d["sb2t"] = _A(np.asarray(p["L0_ffn_sb2"]).reshape(2, 16, 128).transpose(0, 2, 1))
    d["b3"] = _A(np.concatenate([np.asarray(p["L0_ffn_rb3"]),
                                 np.asarray(p["L0_ffn_sb3"])], 0))
    d["xq"] = _A(p["L0_attn_a_hidden_states"][b][2048 * h:2048 * (h + 1)])
    d["dyno"] = np.array([[32 * b, 2048 * h,
                           64 * b + 0 + 4 * h, 64 * b + 16 + 4 * h,
                           64 * b + 32 + 4 * h, 64 * b + 48 + 4 * h,
                           1024 * b, 256 * b]], dtype=np.int32)
    d["consts"] = np.tile(np.array(CONSTS, np.float32), (128, 1))
    d["ident"] = np.eye(128, dtype=np.float32)
    d["mtril"] = np.tril(np.ones((128, 128), np.float32), -1)
    d["mtriu"] = np.triu(np.ones((128, 128), np.float32), 0)
    return d


_NC_CACHE = {}


def kernel(**inputs):
    taps = inputs.pop("_taps", ())
    key = tuple(sorted(taps))
    if key not in _NC_CACHE:
        _NC_CACHE[key] = build_program(taps)
    nc, out_names = _NC_CACHE[key]
    in_maps = [core_inputs(inputs, c) for c in range(NCORES)]
    res = run_bass_kernel_spmd(nc, in_maps, core_ids=list(range(NCORES)))
    out = np.zeros((B, L, D), np.float32)
    for c in range(NCORES):
        b, h = divmod(c, 4)
        out[b, 2048 * h:2048 * (h + 1)] = res.results[c]["out"]
    if taps:
        kernel.tap_results = [{n: res.results[c][n] for n in out_names if n != "out"}
                              for c in range(NCORES)]
    return out



class St:
    pass


def build_program(taps=(), phases=(1, 2, 3, 4, 5, 6)):
    kb = Kb(taps)
    st = St()
    st.phases = phases
    T, V, A, G, Sy = ENGS

    st.bk = [kb.ps(f"bk{i}", [128, 512], F32) for i in range(8)]

    st.d = {}
    for n, sh in [("xb", [L, D]), ("wq", [D, DK]), ("wk", [D, DK]),
                  ("wv", [D, DV]), ("wb", [128, 8]), ("cw_q", [DK, 4]),
                  ("cw_k", [DK, 4]), ("cw_v", [DV, 4]), ("firs", [DV, 3]),
                  ("firl", [DV, 63]), ("w1a", [D, 2048]), ("w1b", [32, 2048]),
                  ("w2", [2048, 16]), ("b1t", [128, 16]), ("b2c", [16, 1]),
                  ("glt", [1, 1]), ("onw", [128, DV]), ("wo", [D, D]),
                  ("wqdT", [D, ABQL]), ("bqdt", [128, 4]),
                  ("wqkT", [ABQL, D]), ("bqkt", [128, 8]),
                  ("wkvdT", [D, ABKV]), ("bkvc", [ABKV, 1]),
                  ("wvuT", [ABKV, D]), ("bvur", [1, D]),
                  ("wprT", [D, D]), ("bprt", [128, 8]),
                  ("gw", [D, 8]), ("gb128", [128, 8]),
                  ("rw1", [8, D, 2048]), ("rw2", [8, D, 2048]),
                  ("rw3", [8, 2048, D]), ("sw1", [2, D, 2048]),
                  ("sw2", [2, D, 2048]), ("sw3", [2, 2048, D]),
                  ("rb1t", [8, 128, 16]), ("rb2t", [8, 128, 16]),
                  ("sb1t", [2, 128, 16]), ("sb2t", [2, 128, 16]),
                  ("b3", [10, D]), ("consts", [128, len(CONSTS)]),
                  ("ident", [128, 128]), ("mtril", [128, 128]),
                  ("mtriu", [128, 128]), ("xq", [TPC, D])]:
        st.d[n] = kb.din(n, sh, F32)
    st.d["dyno"] = kb.din("dyno", [1, 8], I32)
    st.out = kb.dout("out", [TPC, D], F32)

    st.qbn = (kb.dout if "qbn" in taps else kb.dint)("qbn", [NC, 2, 128, 128], BF16)
    st.kbn = (kb.dout if "kbn" in taps else kb.dint)("kbn", [NC, 2, 128, 128], BF16)
    st.fsd = (kb.dout if "fsd" in taps else kb.dint)("fsd", [NC, 128, DV], BF16)
    st.fld = (kb.dout if "fld" in taps else kb.dint)("fld", [NC, 128, DV], BF16)
    st.xres = (kb.dout if "xres" in taps else kb.dint)("xres", [QC, 128, D], F32)
    st.x2s = kb.dout("x2s", [QC, 128, D], F32) if "x2s" in taps else None
    st.x3s = kb.dout("x3s", [QC, 128, D], F32) if "x3s" in taps else None
    st.cq5s = kb.dout("cq5s", [128, 4, TPC], BF16) if "cq5s" in taps else None
    st.lgs = kb.dout("lgs", [16, TPC], F32) if "lgs" in taps else None
    st.st32s = kb.dout("st32s", [32, TPC], BF16) if "st32s" in taps else None
    st.g1s = kb.dout("g1s", [128, 16, 512], BF16) if "g1s" in taps else None
    st.rg3s = kb.dout("rg3s", [128, 4, 8, 128], BF16) if "rg3s" in taps else None
    st.q5s = kb.dout("q5s", [128, 8, TPC], BF16) if "q5s" in taps else None
    st.cq5l = kb.dout("cq5l", [128, 4, TPC], BF16) if "cq5l" in taps else None
    st.xh5s2 = kb.dout("xh5s2", [128, 8, TPC], BF16) if "xh5s2" in taps else None
    sh_ag = maybe_share_collective_output_space("AllGather",
                                                [list(range(NCORES))])
    st.stats_in = kb.dint("stats_in", [8, L], F32)
    st.stats_out = kb.dint("stats_out", [64, L], F32, addr_space=sh_ag)
    st.logit_in = kb.dint("logit_in", [16, TPC], F32)
    st.logit_out = kb.dint("logit_out", [128, TPC], F32, addr_space=sh_ag)
    st.o_in = kb.dint("o_in", [DV, L], BF16)
    st.o_out = kb.dint("o_out", [8 * DV, L], BF16, addr_space=sh_ag)
    st.ckv_in = kb.dint("ckv_in", [ABKV, TPC], BF16)
    st.ckv_out = kb.dint("ckv_out", [8 * ABKV, TPC], BF16, addr_space=sh_ag)

    st.consts = kb.sb("c_consts", [128, len(CONSTS)], F32)
    st.ones_bf = kb.sb("c_onesbf", [128, 1], BF16)
    st.idf = kb.sb("c_idf", [128, 128], F32)
    st.idb = kb.sb("c_idb", [128, 128], BF16)
    st.mtrilS = kb.sb("c_mtrilS", [128, 128], BF16)
    st.mtriuI = kb.sb("c_mtriuI", [128, 128], BF16)
    st.x_tm = None

    st.bar = kb.sem_dedicated("bar")

    def cst(col, p=128):
        return st.consts[0:p, col:col + 1]

    kb.new_phase()
    s0 = kb.sem_dedicated("s0")
    s0g = kb.sem_dedicated("s0g")
    kb.on(Sy, lambda E: E.dma_start(st.consts[:], st.d["consts"].ap()))
    kb.inc_last(Sy, s0, 16)
    kb.on(Sy, lambda E: E.dma_start(st.idf[:], st.d["ident"].ap()))
    kb.inc_last(Sy, s0, 16)
    kb.on(G, lambda E: E.dma_start(st.idb[:], st.d["ident"].ap()))
    kb.inc_last(G, s0g, 16)
    kb.on(G, lambda E: E.dma_start(st.mtrilS[:], st.d["mtril"].ap()))
    kb.inc_last(G, s0g, 16)
    kb.on(G, lambda E: E.dma_start(st.mtriuI[:], st.d["mtriu"].ap()))
    kb.inc_last(G, s0g, 16)

    st.wqkv = None
    for i, n in (enumerate(("wq", "wk", "wv")) if 1 in phases else ()):
        kb.on(G, lambda E, i=i, n=n: E.dma_start(
            st.wqkv[:, i, :, :],
            st.d[n].ap().rearrange("(kc p) m -> p kc m", p=128)))
        kb.inc_last(G, s0g, 16)
    if 1 in phases:
        kb.on(G, lambda E: E.dma_start(st.wbt[:], st.d["wb"].ap()))
        kb.inc_last(G, s0g, 16)
    for i, n in (enumerate(("cw_q", "cw_k", "cw_v")) if 1 in phases else ()):
        kb.on(G, lambda E, i=i, n=n: E.dma_start(
            st.cw[:, i, :, :],
            st.d[n].ap().rearrange("(g p) t -> p g t", p=128)))
        kb.inc_last(G, s0g, 16)
    if 1 in phases:
        kb.on(G, lambda E: E.dma_start(
            st.fw[:, :, 0:3],
            st.d["firs"].ap().rearrange("(g p) t -> p g t", p=128)))
        kb.inc_last(G, s0g, 16)
        kb.on(G, lambda E: E.dma_start(
            st.fw[:, :, 3:66],
            st.d["firl"].ap().rearrange("(g p) t -> p g t", p=128)))
        kb.inc_last(G, s0g, 16)

    st.dyn_sb = kb.sb("dyn_sb", [1, 8], I32)
    st.ones_row = kb.sb("ones_row", [1, 128], F32)
    st.ones_rb = kb.sb("ones_rb", [1, 128], BF16)
    kb.on(Sy, lambda E: E.dma_start(st.dyn_sb[:], st.d["dyno"].ap()))
    kb.inc_last(Sy, s0, 16)
    tdyn = s0.n
    kb.wait(Sy, s0, tdyn)
    kb.wait(G, s0, tdyn)
    kb.on(V, lambda E: E.memset(st.ones_row[:], 1.0))
    kb.on(V, lambda E: E.memset(st.ones_rb[:], 1.0))
    kb.junk = kb.sb("c_junk", [128, 8], F32)
    sjk = kb.sem_dedicated("sjk")
    kb.on(V, lambda E: E.memset(kb.junk[:], 0.0))
    kb.inc_last(V, sjk, 1)
    kb.wait(A, sjk, 1)

    kb.wait(V, s0, s0.n)
    kb.wait(V, s0g, s0g.n)
    kb.on(V, lambda E: E.tensor_copy(st.ones_bf[:], cst(C_ONE)))
    for m in range(3 if 1 in phases else 0):
        for g in range(2):
            for t in range(4):
                kb.on(V, lambda E, m=m, g=g, t=t: E.tensor_scalar(
                    st.cdiag[:, m, g, t, :], st.idb[:],
                    st.cw[:, m, g, t:t + 1], None, ALU.mult))
    for g in range(2 if 1 in phases else 0):
        for t in range(66):
            kb.on(V, lambda E, g=g, t=t: E.tensor_scalar(
                st.fdiag[:, g, t, :], st.idb[:],
                st.fw[:, g, t:t + 1], None, ALU.mult))
    for nm in (("preq", "prek", "prev", "vbuf") if 1 in phases else ()):
        kb.on(V, lambda E, nm=nm: E.memset(getattr(st, nm)[:, :, 0:64], 0.0))
    kb.barrier(st.bar)

    kb.begin_scope()
    st.v_tm = kb.sbs("v_tm", [128, NC, DV], BF16)
    st.dl_tm = kb.sbs("dl_tm", [128, NC, DV], BF16)
    st.braw = kb.sbs("braw", [128, NC], F32)
    st.beta = kb.sbs("beta", [128, NC], F32)
    st.ssq = kb.sbs("ssq", [128, 2, NC], F32)
    st.mr = kb.sbs("mr", [128, NC, 2], F32)
    st.stats_tm = kb.sbs("stats_tm", [128, NC, 8], F32)
    kb.new_phase()
    if 1 in phases:
        p1_trunk(kb, st, cst)
    kb.barrier(st.bar)
    kb.new_phase()
    if 2 in phases:
        p2_scan(kb, st, cst)
    kb.barrier(st.bar)
    kb.new_phase()
    if 3 in phases:
        p3_gate(kb, st, cst)
    kb.barrier(st.bar)
    kb.end_scope()
    kb.new_phase()
    if 4 in phases:
        p4_ffn(kb, st, cst, 0)
    kb.barrier(st.bar)
    kb.new_phase()
    if 5 in phases:
        p5_attnb(kb, st, cst)
    kb.barrier(st.bar)
    kb.new_phase()
    if 6 in phases:
        p4_ffn(kb, st, cst, 1)
    kb.barrier(st.bar)

    emit_taps(kb, st)

    out_names = ["out"]
    for t in sorted(kb.taps):
        if t in ("qbn", "kbn", "fsd", "fld", "xres", "x2s", "x3s",
                 "cq5s", "xh5s2", "q5s", "cq5l", "lgs", "st32s", "g1s",
                 "rg3s"):
            out_names.append(t)
        elif t == "nbq":
            out_names.append("p5_nbqd")
        elif getattr(st, t, None) is not None:
            out_names.append("tap_" + t)
    return kb.finalize(), out_names


def emit_taps(kb, st):
    T, V, A, G, Sy = ENGS
    stap = kb.sem_dedicated("stap")
    for name in sorted(kb.taps):
        if name in ("qbn", "kbn", "fsd", "fld", "xres", "x2s", "x3s"):
            continue
        t = getattr(st, name, None)
        if t is None:
            continue
        shp = list(t.ap().shape)
        o = kb.dout("tap_" + name, shp, t.ap().dtype)
        kb.on(Sy, lambda E, o=o, t=t: E.dma_start(o.ap(), t[:]))
        kb.inc_last(Sy, stap, 16)
    if stap.n:
        kb.wait(Sy, stap, stap.n)


def p1_trunk(kb, st, cst):
    import os
    NSG = int(os.environ.get("P1N", "16"))
    T, V, A, G, Sy = ENGS
    kb.begin_scope()
    st.wqkv = kb.sbs("wqkv", [128, 3, 8, DK], BF16)
    st.wbt = kb.sbs("wbt", [128, 8], BF16)
    st.cw = kb.sbs("cw", [128, 3, 2, 4], F32)
    st.fw = kb.sbs("fw", [128, 2, 66], F32)
    st.cdiag = kb.sbs("cdiag", [128, 3, 2, 4, 128], BF16)
    st.fdiag = kb.sbs("fdiag", [128, 2, 66, 128], BF16)
    st.preq = kb.sbs("preq", [128, 2, 576], BF16)
    st.prek = kb.sbs("prek", [128, 2, 576], BF16)
    st.prev = kb.sbs("prev", [128, 2, 576], BF16)
    st.vbuf = kb.sbs("vbuf", [128, 2, 576], BF16)
    xt = kb.sbs("p1_xt", [128, 2, D], F32)
    xh = kb.sbs("p1_xh", [128, 2, D], BF16)
    st.p1xt = xt
    st.p1xh = xh
    fstg = kb.sbs("p1_fstg", [128, 2, 2, DV], BF16)
    ring = kb.sbs("p1_ring", [128, 4, 8, 128], BF16)
    st.p1ring = ring
    qkb = kb.sbs("p1_qkb", [128, 2, 2, 512], BF16)
    sqs = kb.sbs("p1_sqs", [128, 2, 512], BF16)
    ffm = kb.sbs("p1_ffm", [128, 2, 2, 512], BF16)
    scr = kb.sbs("p1_scr", [128, D], F32)
    col = kb.sbs("p1_col", [128, 8], F32)
    stc = kb.sbs("p1_stc", [128, 4], F32)

    s_x = [kb.sem("p1_x0"), kb.sem("p1_x1")]
    s_ln = kb.sem("p1_ln")
    s_tr = kb.sem("p1_tr")
    s_xf = kb.sem("p1_xf")
    s_bb = kb.sem("p1_bb")
    s_bev = kb.sem("p1_bev")
    s_pp = kb.sem("p1_pp")
    s_pre = kb.sem("p1_pre")
    s_cv = kb.sem("p1_cv")
    s_si = kb.sem("p1_si")
    s_fir = kb.sem("p1_fir")
    s_fe = kb.sem("p1_fe")
    s_tt = kb.sem("p1_tt")
    s_te = kb.sem("p1_te")
    s_sq = kb.sem("p1_sq")
    s_ss = kb.sem("p1_ss")
    s_sse = kb.sem("p1_sse")
    s_qko = kb.sem("p1_qko")
    s_fso = [kb.sem("p1_fso0"), kb.sem("p1_fso1")]
    s_tl = kb.sem("p1_tl")

    bk = st.bk
    trv = bk[0].ap().bitcast(BF16)
    t0v = bk[6].ap().bitcast(BF16)
    t1v = bk[7].ap().bitcast(BF16)

    if NSG < 16:
        for t in (st.braw[:], st.beta[:], st.ssq[:],
                  st.stats_tm[:].rearrange("p a b -> p (a b)"),
                  st.v_tm[:].rearrange("p a b -> p (a b)")):
            kb.on(V, lambda E, t=t: E.memset(t, 0.0))

    for g in range(NSG):
        c0 = 4 * g
        for cc in range(4):
            c = c0 + cc
            if c >= 2:
                kb.wait(Sy, s_ln, c - 1)
            kb.on(Sy, lambda E, c=c, cc=cc: E.dma_start(
                xt[:, cc % 2, :], st.d["xb"].ap()[128 * c:128 * (c + 1), :]))
            kb.inc_last(Sy, s_x[c % 2], 16)
            kb.wait(V, s_x[c % 2], 16 * (c // 2 + 1))
            kb.on(V, lambda E, cc=cc: E.reduce_sum(col[:, 0:1], xt[:, cc % 2, :], axis=AX))
            kb.pad(V, 1)
            kb.on(V, lambda E: E.tensor_scalar(col[:, 0:1], col[:, 0:1],
                                               cst(C_INV1024), None, ALU.mult))
            kb.on(V, lambda E, cc=cc: E.tensor_tensor(scr[:], xt[:, cc % 2, :],
                                                      xt[:, cc % 2, :], ALU.mult))
            kb.on(V, lambda E: E.reduce_sum(col[:, 1:2], scr[:], axis=AX))
            kb.pad(V, 1)
            kb.on(V, lambda E: E.tensor_scalar(col[:, 1:2], col[:, 1:2],
                                               cst(C_INV1024), None, ALU.mult))
            kb.on(V, lambda E: E.tensor_tensor(col[:, 2:3], col[:, 0:1],
                                               col[:, 0:1], ALU.mult))
            kb.pad(V, 1)
            kb.on(V, lambda E: E.tensor_tensor(col[:, 1:2], col[:, 1:2],
                                               col[:, 2:3], ALU.subtract))
            kb.pad(V, 1)
            kb.rsqrt(V, col[:, 3:4], col[:, 1:2], col[:, 4:5], col[:, 5:6],
                     col[:, 6:7], eps_col=cst(C_EPS5))
            if c >= 2:
                kb.wait(V, s_tr, c - 1)
            kb.on(V, lambda E, cc=cc: E.tensor_scalar(
                xh[:, cc % 2, :], xt[:, cc % 2, :], col[:, 0:1], col[:, 3:4],
                ALU.subtract, ALU.mult))
            kb.inc_last(V, s_ln, 1)
            kb.wait(T, s_ln, c + 1)
            if c >= 1:
                kb.wait(T, s_xf, c)
            for j in range(8):
                kb.on(T, lambda E, j=j, cc=cc: E.transpose(
                    trv[:, 128 * j:128 * (j + 1)],
                    xh[:, cc % 2, 128 * j:128 * (j + 1)], st.idb[:]))
            kb.inc_last(T, s_tr, 1)
            kb.wait(A, s_tr, c + 1)
            if g >= 1:
                kb.wait(A, s_pp, 6 * g)
            kb.on(A, lambda E, cc=cc: E.copy(ring[:, cc, :, :].rearrange(
                "p a b -> p (a b)"), trv[:]))
            kb.inc_last(A, s_xf, 1)

        kb.wait(T, s_xf, c0 + 4)
        if g >= 1:
            kb.wait(T, s_sse, g)
        for cc in range(4):
            for kc in range(8):
                kb.on(T, lambda E, cc=cc, kc=kc: E.matmul(
                    bk[5].ap()[:, cc:cc + 1], ring[:, cc, kc, :],
                    st.wbt[:, kc:kc + 1], start=(kc == 0), stop=(kc == 7)))
        kb.inc_last(T, s_bb, 1)
        kb.wait(V, s_bb, g + 1)
        kb.on(V, lambda E, c0=c0: E.tensor_copy(st.braw[:, c0:c0 + 4],
                                                bk[5].ap()[:, 0:4]))
        kb.inc_last(V, s_bev, 1)

        if g >= 1:
            kb.wait(T, s_fe, 4 * g)
        nsil = 0
        for m in range(3):
            for mt in range(2):
                u = 2 * m + mt
                pb = bk[1 + (u % 2)]
                if g * 6 + u >= 2:
                    kb.wait(T, s_pre, g * 6 + u - 1)
                for kc in range(8):
                    kb.on(T, lambda E, m=m, mt=mt, kc=kc, pb=pb: E.matmul(
                        pb.ap()[:], st.wqkv[:, m, kc, 128 * mt:128 * (mt + 1)],
                        ring[:, :, kc, :],
                        start=(kc == 0), stop=(kc == 7)))
                kb.inc_last(T, s_pp, 1)
                kb.wait(A, s_pp, g * 6 + u + 1)
                pre = (st.preq, st.prek, st.prev)[m]
                if g >= 1:
                    kb.wait(A, s_tl, g)
                kb.on(A, lambda E, pre=pre, mt=mt, pb=pb: E.copy(
                    pre[:, mt, 64:576], pb.ap()[:]))
                kb.inc_last(A, s_pre, 1)
                cb = bk[3 + (u % 2)]
                kb.wait(T, s_pre, g * 6 + u + 1)
                if g * 6 + u >= 2:
                    kb.wait(T, s_si, g * 6 + u - 1)
                for t in range(4):
                    kb.on(T, lambda E, m=m, mt=mt, t=t, cb=cb, pre=pre: E.matmul(
                        cb.ap()[:], st.cdiag[:, m, mt, t, :],
                        pre[:, mt, 61 + t:61 + t + 512],
                        start=(t == 0), stop=(t == 3)))
                kb.inc_last(T, s_cv, 1)
                kb.wait(A, s_cv, g * 6 + u + 1)
                if m < 2:
                    if g >= 1:
                        kb.wait(A, s_qko, 64 * g)
                    kb.on(A, lambda E, m=m, mt=mt, cb=cb: E.activation(
                        qkb[:, m, mt, :], cb.ap()[:], AF.Silu))
                else:
                    if g >= 1:
                        kb.wait(A, s_tl, g)
                    kb.on(A, lambda E, mt=mt, cb=cb: E.activation(
                        st.vbuf[:, mt, 64:576], cb.ap()[:], AF.Silu))
                kb.inc_last(A, s_si, 1)
                nsil += 1

        kb.wait(T, s_si, 6 * g + 6)
        for br in range(2):
            taps = 3 if br == 0 else 63
            base = 0 if br == 0 else 3
            for pt in range(2):
                fu = 4 * g + 2 * br + pt
                fb = bk[3 + ((2 * br + pt) % 2)]
                if fu >= 2:
                    kb.wait(T, s_fe, fu - 1)
                for t in range(taps):
                    off = 64 - (taps - 1) + t
                    kb.on(T, lambda E, pt=pt, t=t, base=base, off=off,
                          fb=fb, stp=(t == taps - 1): E.matmul(
                              fb.ap()[:], st.fdiag[:, pt, base + t, :],
                              st.vbuf[:, pt, off:off + 512],
                              start=(t == 0), stop=stp))
                kb.inc_last(T, s_fir, 1)
                kb.wait(A, s_fir, fu + 1)
                if g >= 1:
                    kb.wait(A, s_tt, 12 * g)
                kb.on(A, lambda E, br=br, pt=pt, fb=fb: E.copy(
                    ffm[:, br, pt, :], fb.ap()[:]))
                kb.inc_last(A, s_fe, 1)

        kb.wait(V, s_fir, 4 * g + 4)
        kb.wait(V, s_cv, 6 * g + 6)
        for t in (st.preq, st.prek, st.prev, st.vbuf):
            kb.on(V, lambda E, t=t: E.tensor_copy(t[:, :, 0:64], t[:, :, 512:576]))
        kb.inc_last(V, s_tl, 1)

        for ti in range(3):
            for cc in range(4):
                c = c0 + cc
                tu = g * 12 + ti * 4 + cc
                tb = bk[6 + (tu % 2)]
                tv = (t0v, t1v)[tu % 2]
                if ti == 0:
                    kb.wait(T, s_si, 6 * g + 6)
                else:
                    kb.wait(T, s_fe, 4 * g + 4)
                if tu >= 2:
                    kb.wait(T, s_te, tu - 1)
                for pt in range(2):
                    if ti == 0:
                        sap = st.vbuf[:, pt, 64 + 128 * cc:64 + 128 * (cc + 1)]
                    else:
                        sap = ffm[:, ti - 1, pt, 128 * cc:128 * (cc + 1)]
                    kb.on(T, lambda E, sap=sap, pt=pt, tv=tv: E.transpose(
                        tv[:, 128 * pt:128 * (pt + 1)], sap, st.idb[:]))
                kb.inc_last(T, s_tt, 1)
                kb.wait(V, s_tt, tu + 1)
                if ti == 0:
                    dst = st.v_tm[:, c, :]
                else:
                    nfir = g * 8 + (ti - 1) * 4 + cc
                    if nfir >= 2:
                        kb.wait(V, s_fso[cc % 2], 16 * (nfir // 2))
                    dst = fstg[:, ti - 1, cc % 2, :]
                kb.on(V, lambda E, dst=dst, tv=tv: E.tensor_copy(
                    dst, tv[:, 0:256]))
                kb.inc_last(V, s_te, 1)
                if ti > 0:
                    kb.wait(Sy, s_te, s_te.n)
                    dn = st.fsd if ti == 1 else st.fld
                    kb.on(Sy, lambda E, dn=dn, c=c, ti=ti, cc=cc: E.dma_start(
                        dn.ap()[c], fstg[:, ti - 1, cc % 2, :]))
                    kb.inc_last(Sy, s_fso[cc % 2], 16)
                jm = (6, 0, 2)[ti]
                kb.on(V, lambda E, dst=dst: E.reduce_sum(
                    stc[:, 0:1], dst, axis=AX))
                kb.pad(V, 1)
                kb.on(V, lambda E: E.tensor_scalar(
                    stc[:, 0:1], stc[:, 0:1], cst(C_INV256), None, ALU.mult))
                kb.on(V, lambda E, dst=dst: E.tensor_tensor(
                    scr[:, 0:256], dst, dst, ALU.mult))
                kb.on(V, lambda E: E.reduce_sum(stc[:, 1:2], scr[:, 0:256], axis=AX))
                kb.pad(V, 1)
                kb.on(V, lambda E: E.tensor_scalar(
                    stc[:, 1:2], stc[:, 1:2], cst(C_INV256), None, ALU.mult))
                kb.on(V, lambda E: E.tensor_tensor(
                    stc[:, 2:3], stc[:, 0:1], stc[:, 0:1], ALU.mult))
                kb.pad(V, 1)
                kb.on(V, lambda E: E.tensor_tensor(
                    stc[:, 1:2], stc[:, 1:2], stc[:, 2:3], ALU.subtract))
                kb.pad(V, 1)
                kb.on(V, lambda E: E.tensor_scalar(
                    stc[:, 1:2], stc[:, 1:2], cst(C_EPS6), None, ALU.max))
                kb.pad(V, 1)
                kb.rsqrt(V, stc[:, 2:3], stc[:, 1:2], col[:, 4:5], col[:, 5:6],
                         col[:, 6:7])
                kb.on(V, lambda E, c=c, jm=jm: E.tensor_tensor(
                    st.stats_tm[:, c, jm + 1:jm + 2], stc[:, 1:2], stc[:, 2:3],
                    ALU.mult))
                kb.on(V, lambda E, c=c, jm=jm: E.tensor_copy(
                    st.stats_tm[:, c, jm:jm + 1], stc[:, 0:1]))

        kb.wait(T, s_bev, g + 1)
        for m in range(2):
            if m == 1:
                kb.wait(A, s_ss, 2 * g + 1)
            kb.on(A, lambda E, m=m: E.activation(
                sqs[:].rearrange("p a b -> p (a b)"),
                qkb[:, m, :, :].rearrange("p a b -> p (a b)"), AF.Square))
            kb.inc_last(A, s_sq, 1)
            kb.wait(T, s_sq, 2 * g + m + 1)
            for cc in range(4):
                for pt in range(2):
                    kb.on(T, lambda E, m=m, cc=cc, pt=pt: E.matmul(
                        bk[5].ap()[:, 4 + 4 * m + cc:5 + 4 * m + cc],
                        sqs[:, pt, 128 * cc:128 * (cc + 1)], st.ones_bf[:],
                        start=(pt == 0), stop=(pt == 1)))
            kb.inc_last(T, s_ss, 1)
        kb.wait(V, s_ss, 2 * g + 2)
        kb.on(V, lambda E, c0=c0: E.tensor_copy(
            st.ssq[:, 0, c0:c0 + 4], bk[5].ap()[:, 4:8]))
        kb.on(V, lambda E, c0=c0: E.tensor_copy(
            st.ssq[:, 1, c0:c0 + 4], bk[5].ap()[:, 8:12]))
        kb.inc_last(V, s_sse, 1)

        kb.wait(Sy, s_si, 6 * g + 4)
        for m, dn in ((0, st.qbn), (1, st.kbn)):
            for x in range(2):
                kb.on(Sy, lambda E, m=m, dn=dn, c0=c0, x=x: E.dma_start(
                    dn.ap()[c0:c0 + 4, x].rearrange("c p t -> p c t"),
                    qkb[:, m, x, :].rearrange("p (c t) -> p c t", c=4)))
                kb.inc_last(Sy, s_qko, 16)

    nbc = 4 * NSG
    kb.wait(A, s_bev, NSG)
    kb.on(A, lambda E: E.activation(st.beta[:, 0:nbc], st.braw[:, 0:nbc],
                                    AF.Tanh, scale=cst(C_HALF)))
    sfin = kb.sem("p1_fin")
    kb.inc_last(A, sfin, 1)
    kb.wait(V, sfin, 1)
    kb.on(V, lambda E: E.tensor_scalar(st.beta[:, 0:nbc], st.beta[:, 0:nbc],
                                       cst(C_HALF), cst(C_HALF),
                                       ALU.mult, ALU.add))
    kb.end_scope()


class Banks:

    def __init__(self, kb, banks, tag):
        self.kb = kb
        self.banks = banks
        self.freed = [None] * len(banks)
        self.i = 0

    def acquire(self):
        i = self.i
        self.i = (self.i + 1) % len(self.banks)
        if self.freed[i] is not None:
            s, t = self.freed[i]
            self.kb.wait("tensor", s, t)
        return i

    def release(self, i, sem, tgt):
        self.freed[i] = (sem, tgt)


def p2_scan(kb, st, cst):
    T, V, A, G, Sy = ENGS
    kb.begin_scope()
    qf = kb.sbs("p2_qf", [128, 2, 2, 128], BF16)
    kf = kb.sbs("p2_kf", [128, 2, 2, 128], BF16)
    ktm = kb.sbs("p2_ktm", [128, 256], BF16)
    kbt = kb.sbs("p2_kbt", [128, 256], BF16)
    vb = kb.sbs("p2_vb", [128, 256], BF16)
    Pa = kb.sbs("p2_Pa", [128, 7, 128], BF16)
    Pt = kb.sbs("p2_Pt", [128, 6, 128], BF16)
    Xb = kb.sbs("p2_X", [128, 2, 128], BF16)
    atT = kb.sbs("p2_atT", [128, 128], BF16)
    wT = kb.sbs("p2_wT", [128, 256], BF16)
    u_sb = kb.sbs("p2_u", [128, 256], BF16)
    u2r = kb.sbs("p2_u2r", [128, 256], BF16)
    S32 = kb.sbs("p2_S32", [128, 2, 256], F32)
    S16 = kb.sbs("p2_S16", [128, 2, 256], BF16)
    rn = kb.sbs("p2_rn", [128, 2], F32)
    colp = kb.sbs("p2_col", [128, 8], F32)
    scr2 = kb.sbs("p2_scr", [128, 256], F32)
    stc2 = kb.sbs("p2_stc", [128, 4], F32)
    stg = kb.sbs("p2_stg", [8, 128], F32)

    s_ld = [kb.sem("p2_ld0"), kb.sem("p2_ld1")]
    s_pe = kb.sem("p2_pe")
    po_marks = []
    s_dv = kb.sem("p2_dv")
    s_stg = kb.sem("p2_stg")
    s_std = kb.sem("p2_std")
    bks = Banks(kb, st.bk, "p2")

    kb.on(V, lambda E: E.memset(S32[:], 0.0))
    kb.on(V, lambda E: E.memset(S16[:], 0.0))
    pe_end_prev = None
    NCH = int(os.environ.get("P2N", str(NC)))
    if NCH < NC:
        kb.on(V, lambda E: E.memset(
            st.dl_tm[:].rearrange("p a b -> p (a b)"), 0.0))

    def pe(fn, mark=False):
        kb.on(T, fn)
        if mark:
            return kb.inc_last(T, s_pe, 1)

    def dv(fn, mark=False):
        kb.on(V, fn)
        if mark:
            return kb.inc_last(V, s_dv, 1)

    def prod(nmm, extra, width=128):
        bi = bks.acquire()
        bap = st.bk[bi].ap()[:, 0:width]
        for i, (lh, rh) in enumerate(nmm):
            pe(lambda E, b=bap, l=lh, r=rh, s1=(i == 0), s2=(i == len(nmm) - 1):
               E.matmul(b, l, r, start=s1, stop=s2))
        tpe = kb.inc_last(T, s_pe, 1)
        kb.wait(V, s_pe, tpe)
        extra(bap)
        tdv = kb.inc_last(V, s_dv, 1)
        bks.release(bi, s_dv, tdv)
        return tdv, tpe

    for c in range(NCH):
        sl = c % 2
        if c >= 2:
            kb.wait(Sy, s_pe, po_marks[c - 2])
        kb.on(Sy, lambda E, c=c, sl=sl: E.dma_start(
            qf[:, sl, :, :], st.qbn.ap()[c].rearrange("x p t -> p x t")))
        kb.inc_last(Sy, s_ld[sl], 16)
        kb.on(Sy, lambda E, c=c, sl=sl: E.dma_start(
            kf[:, sl, :, :], st.kbn.ap()[c].rearrange("x p t -> p x t")))
        kb.inc_last(Sy, s_ld[sl], 16)

        kb.rsqrt(V, rn[:], st.ssq[:, :, c], colp[:, 0:2], colp[:, 2:4],
                 colp[:, 4:6], eps_col=cst(C_EPS6))
        dv(lambda E, c=c: E.tensor_tensor(colp[:, 6:7], st.beta[:, c:c + 1],
                                          rn[:, 1:2], ALU.mult))
        dv(lambda E, c=c: E.tensor_scalar(vb[:], st.v_tm[:, c, :],
                                          st.beta[:, c:c + 1], None, ALU.mult))
        t_vb = kb.inc_last(V, s_dv, 1)

        kb.wait(T, s_ld[sl], 32 * (c // 2 + 1))
        bi = bks.acquire()
        bv = st.bk[bi].ap().bitcast(BF16)
        for pt in range(2):
            pe(lambda E, pt=pt, sl=sl, bv=bv: E.transpose(
                bv[:, 128 * pt:128 * (pt + 1)], kf[:, sl, pt, :], st.idb[:]))
        tp = kb.inc_last(T, s_pe, 1)
        kb.wait(V, s_pe, tp)
        dv(lambda E, bv=bv: E.tensor_copy(ktm[:], bv[:, 0:256]))
        t_ktm = kb.inc_last(V, s_dv, 1)
        bks.release(bi, s_dv, t_ktm)
        dv(lambda E: E.tensor_scalar(kbt[:], ktm[:], colp[:, 6:7], None,
                                     ALU.mult))
        t_kbt = kb.inc_last(V, s_dv, 1)

        if pe_end_prev is not None:
            kb.wait(V, s_pe, pe_end_prev)
        t_aap, _ = prod(
            nmm=[(kf[:, sl, 0, :], kf[:, sl, 0, :]),
                 (kf[:, sl, 1, :], kf[:, sl, 1, :])],
            extra=lambda bap: dv(lambda E, b=bap: E.scalar_tensor_tensor(
                Pa[:, 0, :], b, colp[:, 6:7], st.mtrilS[:],
                ALU.mult, ALU.mult)))
        bi = bks.acquire()
        bv2 = st.bk[bi].ap().bitcast(BF16)
        kb.wait(T, s_dv, t_aap)
        pe(lambda E, bv2=bv2: E.transpose(bv2[:, 0:128], Pa[:, 0, :], st.idb[:]))
        tp = kb.inc_last(T, s_pe, 1)
        kb.wait(V, s_pe, tp)
        dv(lambda E, bv2=bv2: E.tensor_scalar(Pt[:, 0, :], bv2[:, 0:128],
                                              rn[:, 1:2], None, ALU.mult))
        t_atf = kb.inc_last(V, s_dv, 1)
        bks.release(bi, s_dv, t_atf)
        bi = bks.acquire()
        bv3 = st.bk[bi].ap().bitcast(BF16)
        kb.wait(T, s_dv, t_atf)
        pe(lambda E, bv3=bv3: E.transpose(bv3[:, 0:128], Pt[:, 0, :], st.idb[:]))
        tp = kb.inc_last(T, s_pe, 1)
        kb.wait(V, s_pe, tp)
        dv(lambda E, bv3=bv3: E.tensor_copy(Pa[:, 0, :], bv3[:, 0:128]))
        t_aaf = kb.inc_last(V, s_dv, 1)
        bks.release(bi, s_dv, t_aaf)

        kb.wait(T, s_dv, t_aaf)
        for lev in range(1, 7):
            bi = bks.acquire()
            bap = st.bk[bi].ap()
            pe(lambda E, lev=lev, b=bap: E.matmul(
                b[:, 0:128], Pt[:, lev - 1, :], Pa[:, lev - 1, :],
                start=True, stop=True))
            if lev < 6:
                pe(lambda E, lev=lev, b=bap: E.matmul(
                    b[:, 128:256], Pa[:, lev - 1, :], Pt[:, lev - 1, :],
                    start=True, stop=True))
            tp = kb.inc_last(T, s_pe, 1)
            kb.wait(V, s_pe, tp)
            dv(lambda E, lev=lev, b=bap: E.tensor_copy(Pa[:, lev, :],
                                                       b[:, 0:128]))
            if lev < 6:
                dv(lambda E, lev=lev, b=bap: E.tensor_copy(Pt[:, lev, :],
                                                           b[:, 128:256]))
            t_lev = kb.inc_last(V, s_dv, 1)
            kb.wait(T, s_dv, t_lev)
            bks.release(bi, s_dv, t_lev)

        xcur = 0
        t_x = None
        first = True
        for e_idx, lev in ((6, "init"), (5, "add"), (4, "add"), (3, "add"),
                           (2, "add"), (1, "add"), (0, "sub")):
            bi = bks.acquire()
            bap = st.bk[bi].ap()[:, 0:128]
            rhs = st.idb[:] if first else Xb[:, xcur, :]
            if t_x is not None:
                kb.wait(T, s_dv, t_x)
            pe(lambda E, b=bap, e=e_idx, r=rhs: E.matmul(
                b, Pa[:, e, :], r, start=True, stop=True))
            tp = kb.inc_last(T, s_pe, 1)
            kb.wait(V, s_pe, tp)
            xn = 1 - xcur
            base = st.idb[:] if first else Xb[:, xcur, :]
            op = ALU.subtract if lev == "sub" else ALU.add
            dv(lambda E, b=bap, ba=base, xn=xn, op=op: E.tensor_tensor(
                Xb[:, xn, :], ba, b, op))
            t_x = kb.inc_last(V, s_dv, 1)
            bks.release(bi, s_dv, t_x)
            xcur = xn
            first = False
        Tt = Xb[:, xcur, :]
        kb.wait(T, s_dv, t_x)

        kb.wait(T, s_dv, t_vb)
        t_u, _ = prod(nmm=[(Tt, vb[:])], width=256,
                      extra=lambda bap: dv(lambda E, b=bap: E.tensor_copy(
                          u_sb[:], b)))
        kb.wait(T, s_dv, t_kbt)
        bi = bks.acquire()
        bap = st.bk[bi].ap()
        for hf in range(2):
            pe(lambda E, hf=hf, b=bap: E.matmul(
                b[:, 128 * hf:128 * (hf + 1)], kbt[:, 128 * hf:128 * (hf + 1)],
                Tt, start=True, stop=True))
        tp = kb.inc_last(T, s_pe, 1)
        kb.wait(V, s_pe, tp)
        dv(lambda E, b=bap: E.tensor_copy(wT[:], b[:, 0:256]))
        t_wt = kb.inc_last(V, s_dv, 1)
        bks.release(bi, s_dv, t_wt)

        t_at, _ = prod(
            nmm=[(kf[:, sl, 0, :], qf[:, sl, 0, :]),
                 (kf[:, sl, 1, :], qf[:, sl, 1, :])],
            extra=lambda bap: dv(lambda E, b=bap: E.tensor_tensor(
                atT[:], b, st.mtriuI[:], ALU.mult)))

        kb.wait(T, s_dv, t_wt)
        kb.wait(T, s_dv, t_at)
        bi = bks.acquire()
        bap = st.bk[bi].ap()[:, 0:256]
        for hf in range(2):
            pe(lambda E, hf=hf, b=bap: E.matmul(
                b, wT[:, 128 * hf:128 * (hf + 1)], S16[:, hf, :],
                start=(hf == 0), stop=(hf == 1)))
        tp = kb.inc_last(T, s_pe, 1)
        kb.wait(V, s_pe, tp)
        dv(lambda E, b=bap: E.tensor_tensor(scr2[:], u_sb[:], b, ALU.subtract))
        dv(lambda E: E.tensor_scalar(u2r[:], scr2[:], rn[:, 1:2], None,
                                     ALU.mult))
        t_u2 = kb.inc_last(V, s_dv, 1)
        bks.release(bi, s_dv, t_u2)

        kb.wait(T, s_dv, t_u2)
        bi = bks.acquire()
        bap = st.bk[bi].ap()[:, 0:256]
        pe(lambda E, b=bap, sl=sl: E.matmul(b, qf[:, sl, 0, :], S16[:, 0, :],
                                            start=True, stop=False))
        pe(lambda E, b=bap, sl=sl: E.matmul(b, qf[:, sl, 1, :], S16[:, 1, :],
                                            start=False, stop=False))
        pe(lambda E, b=bap: E.matmul(b, atT[:], u2r[:], start=False, stop=True))
        tp = kb.inc_last(T, s_pe, 1)
        po_marks.append(tp)
        kb.wait(V, s_pe, tp)
        dv(lambda E, b=bap, c=c: E.tensor_scalar(
            st.dl_tm[:, c, :], b, rn[:, 0:1], None, ALU.mult))
        t_dl = kb.inc_last(V, s_dv, 1)
        bks.release(bi, s_dv, t_dl)

        bi = bks.acquire()
        bap = st.bk[bi].ap()
        for hf in range(2):
            pe(lambda E, hf=hf, b=bap: E.matmul(
                b[:, 256 * hf:256 * (hf + 1)], ktm[:, 128 * hf:128 * (hf + 1)],
                u2r[:], start=True, stop=True))
        tp = kb.inc_last(T, s_pe, 1)
        kb.wait(V, s_pe, tp)
        for hf in range(2):
            dv(lambda E, hf=hf, b=bap: E.tensor_tensor(
                S32[:, hf, :], S32[:, hf, :], b[:, 256 * hf:256 * (hf + 1)],
                ALU.add))
        dv(lambda E: E.tensor_copy(S16[:].rearrange("p a b -> p (a b)"),
                                   S32[:].rearrange("p a b -> p (a b)")))
        t_s16 = kb.inc_last(V, s_dv, 1)
        bks.release(bi, s_dv, t_s16)
        kb.wait(T, s_dv, t_s16)

        dv(lambda E, c=c: E.reduce_sum(stc2[:, 0:1], st.dl_tm[:, c, :], axis=AX))
        kb.pad(V, 1)
        dv(lambda E: E.tensor_scalar(stc2[:, 0:1], stc2[:, 0:1],
                                     cst(C_INV256), None, ALU.mult))
        dv(lambda E, c=c: E.tensor_tensor(scr2[:], st.dl_tm[:, c, :],
                                          st.dl_tm[:, c, :], ALU.mult))
        dv(lambda E: E.reduce_sum(stc2[:, 1:2], scr2[:], axis=AX))
        kb.pad(V, 1)
        dv(lambda E: E.tensor_scalar(stc2[:, 1:2], stc2[:, 1:2],
                                     cst(C_INV256), None, ALU.mult))
        dv(lambda E: E.tensor_tensor(stc2[:, 2:3], stc2[:, 0:1], stc2[:, 0:1],
                                     ALU.mult))
        kb.pad(V, 1)
        dv(lambda E: E.tensor_tensor(stc2[:, 1:2], stc2[:, 1:2], stc2[:, 2:3],
                                     ALU.subtract))
        kb.pad(V, 1)
        dv(lambda E: E.tensor_scalar(stc2[:, 1:2], stc2[:, 1:2],
                                     cst(C_EPS6), None, ALU.max))
        kb.pad(V, 1)
        kb.rsqrt(V, stc2[:, 2:3], stc2[:, 1:2], colp[:, 0:1], colp[:, 1:2],
                 colp[:, 2:3])
        dv(lambda E, c=c: E.tensor_copy(st.stats_tm[:, c, 4:5], stc2[:, 0:1]))
        dv(lambda E, c=c: E.tensor_tensor(st.stats_tm[:, c, 5:6], stc2[:, 1:2],
                                          stc2[:, 2:3], ALU.mult))
        t_st = kb.inc_last(V, s_dv, 1)
        kb.wait(T, s_dv, t_st)
        bi = bks.acquire()
        bap = st.bk[bi].ap()[:, 0:128]
        pe(lambda E, b=bap, c=c: E.transpose(b[0:8, :], st.stats_tm[:, c, :],
                                             st.idf[:]))
        tp = kb.inc_last(T, s_pe, 1)
        kb.wait(V, s_pe, tp)
        if c >= 1:
            kb.wait(V, s_std, 16 * c)
        dv(lambda E, b=bap: E.tensor_copy(stg[:], b[0:8, :]))
        tstg = kb.inc_last(V, s_stg, 1)
        bks.release(bi, s_stg, tstg)
        kb.wait(Sy, s_stg, tstg)
        kb.on(Sy, lambda E, c=c: E.dma_start(
            st.stats_in.ap()[:, 128 * c:128 * (c + 1)], stg[:]))
        kb.inc_last(Sy, s_std, 16)
        pe_end_prev = s_pe.n

    kb.end_scope()


def p3_gate(kb, st, cst):
    T, V, A, G, Sy = ENGS
    kb.begin_scope()
    fsr = kb.sbs("p3_fsr", [128, 2, 2, DV], BF16)
    lg4 = kb.sbs("p3_lg4", [4, L], BF16)
    wsm = kb.sbs("p3_wsm", [128, 8], F32)
    acc = kb.sbs("p3_acc", [128, 256], F32)
    onwb = kb.sbs("p3_onw", [128, 256], BF16)
    ostg = kb.sbs("p3_ostg", [128, 2, 128], BF16)
    invt = kb.sbs("p3_invt", [128, 1], F32)
    col3 = kb.sbs("p3_col", [128, 8], F32)
    scr3 = kb.sbs("p3_scr", [128, 256], F32)
    glt = kb.sbs("p3_glt", [1, 4], F32)
    kb.begin_scope()
    st32 = kb.sbs("p3_st32", [32, TPC], BF16)
    st32f = kb.sbs("p3_st32f", [32, TPC], F32)
    xqt = kb.sbs("p3_xqt", [128, 2, D], F32)
    xh3 = kb.sbs("p3_xh3", [128, 2, D], BF16)
    ring3 = kb.sbs("p3_ring", [128, 4, 8, 128], BF16)
    w1a = kb.sbs("p3_w1a", [128, 8, 2048], BF16)
    w1b = kb.sbs("p3_w1b", [32, 2048], BF16)
    w2s = kb.sbs("p3_w2", [128, 16, 16], BF16)
    b1s = kb.sbs("p3_b1", [128, 16], F32)
    b2s = kb.sbs("p3_b2", [16, 1], F32)
    g1 = kb.sbs("p3_g1", [128, 16, 512], BF16)
    lgf = kb.sbs("p3_lgf", [16, TPC], F32)

    s_w = kb.sem("p3_w")
    s_wg = kb.semg("p3_wg")
    s_cc = kb.semg("p3_cc")
    s_bk = kb.sem("p3_bk")
    s_x = [kb.sem("p3_x0"), kb.sem("p3_x1")]
    s_ln = kb.sem("p3_ln")
    s_tr = kb.sem("p3_tr")
    s_xf = kb.sem("p3_xf")
    s_h1 = kb.sem("p3_h1")
    s_ge = kb.sem("p3_ge")
    s_l2 = kb.sem("p3_l2")
    s_le = kb.sem("p3_le")
    s_li = kb.sem("p3_li")
    s_tp = kb.sem("p3_tp")
    s_sm = kb.sem("p3_sm")
    s_ex = kb.sem("p3_ex")
    s_od = kb.sem("p3_od")
    s_oe = kb.sem("p3_oe")
    s_oi = kb.sem("p3_oi")
    s_xq = kb.sem("p3_xq")
    s_wo = kb.sem("p3_wo")
    s_we = kb.sem("p3_we")
    s_wt = kb.sem("p3_wt")
    s_wa = kb.sem("p3_wa")
    s_fl = [kb.sem("p3_fl0"), kb.sem("p3_fl1")]
    s_e1 = kb.sem("p3_e1")
    s_e1c = kb.sem("p3_e1c")
    s_fc = kb.sem("p3_fc")
    bk = st.bk

    kb.on(G, lambda E: E.collective_compute(
        "AllGather", ALU.bypass, replica_groups=[list(range(8))],
        ins=[st.stats_in.ap().opt()], outs=[st.stats_out.ap().opt()]))
    kb.inc_last(G, s_cc, 1)
    for n, dst, rr in [("w1a", w1a, "(kc p) m -> p kc m"),
                       ("w1b", w1b, None), ("w2", w2s, "(kc p) m -> p kc m"),
                       ("onw", onwb, None)]:
        kb.on(G, lambda E, n=n, dst=dst, rr=rr: E.dma_start(
            dst[:], st.d[n].ap() if rr is None
            else st.d[n].ap().rearrange(rr, p=128)))
        kb.inc_last(G, s_wg, 16)
    kb.wait(Sy, s_cc, 1)
    kb.on(Sy, lambda E: dyn_dma(
        E, st, st32f[:], lambda a, b: st.stats_out.ap()[a, b],
        [(0, 32), (1, TPC)]))
    kb.inc_last(Sy, s_e1, 16)
    kb.wait(V, s_e1, 16)
    kb.on(V, lambda E: E.tensor_copy(st32[:], st32f[:]))
    kb.inc_last(V, s_e1c, 1)
    if st.st32s is not None:
        s32g = kb.sem("p3_32s")
        kb.wait(Sy, s_e1c, 1)
        kb.on(Sy, lambda E: E.dma_start(st.st32s.ap(), st32[:]))
        kb.inc_last(Sy, s32g, 16)
        kb.wait(Sy, s32g, 16)
    for n, dst in [("b1t", b1s), ("b2c", b2s)]:
        kb.on(Sy, lambda E, n=n, dst=dst: E.dma_start(dst[:], st.d[n].ap()))
        kb.inc_last(Sy, s_w, 16)
    kb.on(Sy, lambda E: E.dma_start(glt[:, 0:1], st.d["glt"].ap()))
    kb.inc_last(Sy, s_w, 16)

    trv = bk[0].ap().bitcast(BF16)
    nweights = s_w.n
    nweightsg = s_wg.n
    for sg in range(4):
        for cc in range(4):
            qc_ = 4 * sg + cc
            if qc_ >= 2:
                kb.wait(Sy, s_ln, qc_ - 1)
            kb.on(Sy, lambda E, qc_=qc_, cc=cc: E.dma_start(
                xqt[:, cc % 2, :],
                st.d["xq"].ap()[128 * qc_:128 * (qc_ + 1), :]))
            kb.inc_last(Sy, s_x[qc_ % 2], 16)
            kb.wait(V, s_x[qc_ % 2], 16 * (qc_ // 2 + 1))
            kb.on(V, lambda E, cc=cc: E.reduce_sum(
                col3[:, 0:1], xqt[:, cc % 2, :], axis=AX))
            kb.on(V, lambda E: E.tensor_scalar(col3[:, 0:1], col3[:, 0:1],
                                               cst(C_INV1024), None, ALU.mult))
            kb.on(V, lambda E, cc=cc: E.tensor_tensor(
                scr3[:], xqt[:, cc % 2, 0:256], xqt[:, cc % 2, 0:256], ALU.mult))
            kb.on(V, lambda E: E.reduce_sum(col3[:, 1:2], scr3[:], axis=AX))
            kb.pad(V, 1)
            for pp in range(1, 4):
                kb.on(V, lambda E, cc=cc, pp=pp: E.tensor_tensor(
                    scr3[:], xqt[:, cc % 2, 256 * pp:256 * (pp + 1)],
                    xqt[:, cc % 2, 256 * pp:256 * (pp + 1)], ALU.mult))
                kb.on(V, lambda E: E.reduce_sum(col3[:, 2:3], scr3[:], axis=AX))
                kb.pad(V, 1)
                kb.on(V, lambda E: E.tensor_tensor(col3[:, 1:2], col3[:, 1:2],
                                                   col3[:, 2:3], ALU.add))
                kb.pad(V, 1)
            kb.pad(V, 1)
            kb.on(V, lambda E: E.tensor_scalar(col3[:, 1:2], col3[:, 1:2],
                                               cst(C_INV1024), None, ALU.mult))
            kb.on(V, lambda E: E.tensor_tensor(col3[:, 2:3], col3[:, 0:1],
                                               col3[:, 0:1], ALU.mult))
            kb.pad(V, 1)
            kb.on(V, lambda E: E.tensor_tensor(col3[:, 1:2], col3[:, 1:2],
                                               col3[:, 2:3], ALU.subtract))
            kb.pad(V, 1)
            kb.rsqrt(V, col3[:, 3:4], col3[:, 1:2], col3[:, 4:5], col3[:, 5:6],
                     col3[:, 6:7], eps_col=cst(C_EPS5))
            if qc_ >= 2:
                kb.wait(V, s_tr, qc_ - 1)
            kb.on(V, lambda E, cc=cc: E.tensor_scalar(
                xh3[:, cc % 2, :], xqt[:, cc % 2, :], col3[:, 0:1], col3[:, 3:4],
                ALU.subtract, ALU.mult))
            kb.inc_last(V, s_ln, 1)
            kb.wait(T, s_ln, qc_ + 1)
            if qc_ >= 1:
                kb.wait(T, s_xf, qc_)
            for j in range(8):
                kb.on(T, lambda E, j=j, cc=cc: E.transpose(
                    trv[:, 128 * j:128 * (j + 1)],
                    xh3[:, cc % 2, 128 * j:128 * (j + 1)], st.idb[:]))
            kb.inc_last(T, s_tr, 1)
            kb.wait(A, s_tr, qc_ + 1)
            if sg >= 1:
                kb.wait(A, s_h1, 16 * sg)
            kb.on(A, lambda E, cc=cc: E.copy(
                ring3[:, cc, :, :].rearrange("p a b -> p (a b)"), trv[:]))
            kb.inc_last(A, s_xf, 1)

        kb.wait(T, s_xf, 4 * sg + 4)
        if sg == 0:
            kb.wait(T, s_w, nweights)
            kb.wait(T, s_wg, nweightsg)
            kb.wait(T, s_e1c, 1)
            kb.wait(A, s_w, nweights)
        for mtl in range(16):
            un = 16 * sg + mtl
            pb = bk[1 + (un % 2)]
            if un >= 2:
                kb.wait(T, s_ge, un - 1)
            for kc in range(8):
                kb.on(T, lambda E, kc=kc, mtl=mtl, pb=pb: E.matmul(
                    pb.ap()[:], w1a[:, kc, 128 * mtl:128 * (mtl + 1)],
                    ring3[:, :, kc, :], start=(kc == 0), stop=False))
            kb.on(T, lambda E, mtl=mtl, pb=pb, sg=sg: E.matmul(
                pb.ap()[:], w1b[:, 128 * mtl:128 * (mtl + 1)],
                st32[:, 512 * sg:512 * (sg + 1)], start=False, stop=True))
            kb.inc_last(T, s_h1, 1)
            kb.wait(A, s_h1, un + 1)
            if mtl == 0 and sg >= 1:
                kb.wait(A, s_l2, sg)
            kb.on(A, lambda E, mtl=mtl, pb=pb: E.activation(
                g1[:, mtl, :], pb.ap()[:], AF.Gelu, bias=b1s[:, mtl:mtl + 1]))
            kb.inc_last(A, s_ge, 1)
        if sg == 0 and st.rg3s is not None:
            srg = kb.sem("p3_rg3s")
            kb.wait(Sy, s_xf, 4)
            kb.on(Sy, lambda E: E.dma_start(
                st.rg3s.ap().rearrange("p a b c -> p (a b c)"),
                ring3[:].rearrange("p a b c -> p (a b c)")))
            kb.inc_last(Sy, srg, 16)
            kb.wait(Sy, srg, 16)
        if sg == 0 and st.g1s is not None:
            sg1 = kb.sem("p3_g1s")
            kb.wait(Sy, s_ge, 16)
            kb.on(Sy, lambda E: E.dma_start(st.g1s.ap(), g1[:]))
            kb.inc_last(Sy, sg1, 16)
            kb.wait(Sy, sg1, 16)
        kb.wait(T, s_ge, 16 * sg + 16)
        if sg >= 1:
            kb.wait(T, s_le, sg)
        for kc in range(16):
            kb.on(T, lambda E, kc=kc: E.matmul(
                bk[3].ap()[0:16, :], w2s[:, kc, :], g1[:, kc, :],
                start=(kc == 0), stop=(kc == 15)))
        kb.inc_last(T, s_l2, 1)
        kb.wait(V, s_l2, sg + 1)
        kb.wait(V, s_w, nweights)
        kb.on(V, lambda E, sg=sg: E.tensor_scalar(
            lgf[:, 512 * sg:512 * (sg + 1)], bk[3].ap()[0:16, :],
            b2s[:], None, ALU.add))
        kb.inc_last(V, s_le, 1)
        kb.wait(Sy, s_le, sg + 1)
        kb.on(Sy, lambda E, sg=sg: E.dma_start(
            st.logit_in.ap()[:, 512 * sg:512 * (sg + 1)],
            lgf[:, 512 * sg:512 * (sg + 1)]))
        kb.inc_last(Sy, s_li, 16)

    if st.lgs is not None:
        slg = kb.sem("p3_lgs")
        kb.wait(Sy, s_le, 4)
        kb.on(Sy, lambda E: E.dma_start(st.lgs.ap(), lgf[:]))
        kb.inc_last(Sy, slg, 16)
        kb.wait(Sy, slg, 16)
    kb.wait(G, s_li, 64)
    kb.on(G, lambda E: E.collective_compute(
        "AllGather", ALU.bypass, replica_groups=[list(range(8))],
        ins=[st.logit_in.ap().opt()], outs=[st.logit_out.ap().opt()]))
    kb.inc_last(G, s_cc, 1)
    kb.wait(Sy, s_cc, 2)
    kb.begin_scope()
    lgstg = kb.sbs("p3_lgstg", [4, TPC], F32)
    for j in range(4):
        if j >= 1:
            kb.wait(Sy, s_e1c, 1 + j)
        kb.on(Sy, lambda E, j=j: dyn_dma(
            E, st, lgstg[:], lambda a: st.logit_out.ap()[a, :], [(2 + j, 4)]))
        kb.inc_last(Sy, s_bk, 16)
        kb.wait(V, s_bk, 16 * (j + 1))
        kb.on(V, lambda E, j=j: E.tensor_copy(
            lg4[:, 2048 * j:2048 * (j + 1)], lgstg[:]))
        kb.inc_last(V, s_e1c, 1)
    kb.end_scope()

    kb.end_scope()
    kb.barrier(st.bar)

    kb.wait(A, s_w, nweights)
    stmp = kb.sem("p3_tmp")
    kb.on(A, lambda E: E.activation(glt[:, 1:2], glt[:, 0:1], AF.Abs))
    kb.pad(A, 2)
    kb.on(A, lambda E: E.activation(glt[:, 2:3], glt[:, 1:2], AF.Exp,
                                    scale=cst(C_MONE, 1)))
    kb.pad(A, 2)
    kb.on(A, lambda E: E.activation(glt[:, 2:3], glt[:, 2:3], AF.Ln,
                                    bias=cst(C_ONE, 1)))
    kb.inc_last(A, stmp, 1)
    kb.wait(V, stmp, 1)
    kb.on(V, lambda E: E.tensor_scalar(glt[:, 3:4], glt[:, 0:1], 0.0, None,
                                       ALU.max))
    kb.pad(V, 1)
    kb.on(V, lambda E: E.tensor_tensor(glt[:, 3:4], glt[:, 3:4], glt[:, 2:3],
                                       ALU.add))
    kb.pad(V, 1)
    kb.on(V, lambda E: E.tensor_scalar(glt[:, 3:4], glt[:, 3:4],
                                       cst(C_EPS4, 1), None, ALU.add))
    kb.pad(V, 1)
    kb.on(V, lambda E: E.reciprocal(glt[:, 2:3], glt[:, 3:4]))
    kb.inc_last(V, stmp, 1)
    kb.wait(T, stmp, 2)
    kb.on(T, lambda E: E.matmul(bk[4].ap()[:, 0:1], st.ones_row[:],
                                glt[:, 2:3], start=True, stop=True))
    kb.inc_last(T, stmp, 1)
    kb.wait(V, stmp, 3)
    kb.on(V, lambda E: E.tensor_copy(invt[:], bk[4].ap()[:, 0:1]))
    kb.inc_last(V, stmp, 1)
    kb.pad(V, 1)

    kb.wait(T, s_e1c, 5)
    kb.wait(V, s_wg, nweightsg)
    kb.wait(T, stmp, 4)
    for c in range(NC):
        if c == 0:
            for cpre in (0, 1):
                for br, dn in ((0, st.fsd), (1, st.fld)):
                    kb.on(Sy, lambda E, br=br, dn=dn, cpre=cpre: E.dma_start(
                        fsr[:, br, cpre % 2, :], dn.ap()[cpre]))
                    kb.inc_last(Sy, s_fl[cpre % 2], 16)
        elif c + 1 < NC:
            kb.wait(Sy, s_fc, c - 1)
            for br, dn in ((0, st.fsd), (1, st.fld)):
                kb.on(Sy, lambda E, br=br, dn=dn, c=c: E.dma_start(
                    fsr[:, br, (c + 1) % 2, :], dn.ap()[c + 1]))
                kb.inc_last(Sy, s_fl[(c + 1) % 2], 16)
        tb = bk[5]
        tv5 = bk[5].ap().bitcast(BF16)
        if c >= 1:
            kb.wait(T, s_sm, 3 * c)
        kb.on(T, lambda E, c=c: E.transpose(
            tv5[:, 0:4], lg4[:, 128 * c:128 * (c + 1)], st.idb[0:4, 0:4]))
        kb.inc_last(T, s_tp, 1)
        kb.wait(V, s_tp, c + 1)
        kb.on(V, lambda E: E.tensor_scalar(wsm[:, 0:4], tv5[:, 0:4],
                                           invt[:, 0:1], None, ALU.mult))
        kb.inc_last(V, s_sm, 1)
        kb.pad(V, 1)
        kb.on(V, lambda E: E.reduce_max(wsm[:, 4:5], wsm[:, 0:4], axis=AX))
        kb.pad(V, 1)
        kb.on(V, lambda E: E.tensor_scalar(wsm[:, 0:4], wsm[:, 0:4],
                                           wsm[:, 4:5], None, ALU.subtract))
        sxc = kb.inc_last(V, s_sm, 1)
        kb.wait(A, s_sm, sxc)
        kb.on(A, lambda E: E.activation(wsm[:, 0:4], wsm[:, 0:4], AF.Exp))
        kb.inc_last(A, s_ex, 1)
        kb.wait(V, s_ex, c + 1)
        kb.on(V, lambda E: E.reduce_sum(wsm[:, 4:5], wsm[:, 0:4], axis=AX))
        kb.pad(V, 1)
        kb.on(V, lambda E: E.reciprocal(wsm[:, 5:6], wsm[:, 4:5]))
        kb.pad(V, 1)
        kb.on(V, lambda E: E.tensor_scalar(wsm[:, 0:4], wsm[:, 0:4],
                                           wsm[:, 5:6], None, ALU.mult))
        kb.pad(V, 1)
        kb.wait(V, s_fl[c % 2], 32 * (c // 2 + 1))
        kb.on(V, lambda E, c=c: E.tensor_scalar(
            acc[:], fsr[:, 0, c % 2, :], wsm[:, 0:1], None, ALU.mult))
        kb.on(V, lambda E, c=c: E.scalar_tensor_tensor(
            acc[:], fsr[:, 1, c % 2, :], wsm[:, 1:2], acc[:], ALU.mult, ALU.add))
        kb.on(V, lambda E, c=c: E.scalar_tensor_tensor(
            acc[:], st.dl_tm[:, c, :], wsm[:, 2:3], acc[:], ALU.mult, ALU.add))
        kb.on(V, lambda E, c=c: E.scalar_tensor_tensor(
            acc[:], st.v_tm[:, c, :], wsm[:, 3:4], acc[:], ALU.mult, ALU.add))
        kb.inc_last(V, s_fc, 1)
        kb.on(V, lambda E: E.tensor_tensor(scr3[:], acc[:], acc[:], ALU.mult))
        kb.on(V, lambda E: E.reduce_sum(col3[:, 0:1], scr3[:], axis=AX))
        kb.pad(V, 1)
        kb.on(V, lambda E: E.tensor_scalar(col3[:, 0:1], col3[:, 0:1],
                                           cst(C_INV256), None, ALU.mult))
        kb.pad(V, 1)
        kb.rsqrt(V, col3[:, 1:2], col3[:, 0:1], col3[:, 4:5], col3[:, 5:6],
                 col3[:, 6:7], eps_col=cst(C_EPS5))
        kb.on(V, lambda E: E.tensor_scalar(acc[:], acc[:], col3[:, 1:2],
                                           None, ALU.mult))
        kb.on(V, lambda E, c=c: E.tensor_tensor(st.dl_tm[:, c, :], acc[:],
                                                onwb[:], ALU.mult))
        kb.inc_last(V, s_sm, 1)
        kb.wait(T, s_sm, s_sm.n)
        if c >= 1:
            kb.wait(T, s_oe, c)
        for pt in range(2):
            kb.on(T, lambda E, c=c, pt=pt: E.transpose(
                tv5[:, 512 + 128 * pt:512 + 128 * (pt + 1)],
                st.dl_tm[:, c, 128 * pt:128 * (pt + 1)], st.idb[:]))
        kb.inc_last(T, s_od, 1)
        kb.wait(V, s_od, c + 1)
        if c >= 1:
            kb.wait(V, s_oi, 16 * c)
        kb.on(V, lambda E: E.tensor_copy(
            ostg[:].rearrange("p a b -> p (a b)"), tv5[:, 512:768]))
        kb.inc_last(V, s_oe, 1)
        kb.wait(Sy, s_oe, c + 1)
        kb.on(Sy, lambda E, c=c: E.dma_start(
            st.o_in.ap()[:, 128 * c:128 * (c + 1)].rearrange(
                "(x p) t -> p x t", p=128), ostg[:]))
        kb.inc_last(Sy, s_oi, 16)

    kb.barrier(st.bar)
    kb.begin_scope()
    oall = kb.sbs("p3_oall", [128, 8, 2048], BF16)
    wos = kb.sbs("p3_wos", [128, 8, 1024], BF16)
    wstg = kb.sbs("p3_wstg", [128, 512], BF16)
    st.x_tm = kb.sbs("p3_x", [128, QC, D], F32)
    kb.on(G, lambda E: E.dma_start(
        wos[:], st.d["wo"].ap().rearrange("(kc p) m -> p kc m", p=128)))
    kb.inc_last(G, s_wg, 16)
    kb.on(Sy, lambda E: E.dma_start(
        st.x_tm[:], st.d["xq"].ap().rearrange("(c p) d -> p c d", p=128)))
    kb.inc_last(Sy, s_xq, 16)

    kb.wait(G, s_oi, 16 * NC)
    kb.on(G, lambda E: E.collective_compute(
        "AllGather", ALU.bypass, replica_groups=[list(range(8))],
        ins=[st.o_in.ap().opt()], outs=[st.o_out.ap().opt()]))
    kb.inc_last(G, s_cc, 1)
    kb.wait(Sy, s_cc, 3)
    kb.on(Sy, lambda E: dyn_dma(
        E, st, oall[:],
        lambda a, b: st.o_out.ap()[a, b].rearrange("(kc p) t -> p kc t",
                                                   p=128),
        [(6, 1024), (1, TPC)]))
    kb.inc_last(Sy, s_bk, 16)

    kb.wait(T, s_bk, s_bk.n)
    kb.wait(T, s_wg, s_wg.n)
    kb.wait(V, s_xq, 16)
    for tt in range(4):
        for dm in range(8):
            un = 8 * tt + dm
            pb = bk[1 + (un % 2)]
            if un >= 2:
                kb.wait(T, s_we, un - 1)
            for kc in range(8):
                kb.on(T, lambda E, kc=kc, dm=dm, tt=tt, pb=pb: E.matmul(
                    pb.ap()[:], wos[:, kc, 128 * dm:128 * (dm + 1)],
                    oall[:, kc, 512 * tt:512 * (tt + 1)],
                    start=(kc == 0), stop=(kc == 7)))
            kb.inc_last(T, s_wo, 1)
            kb.wait(A, s_wo, un + 1)
            if un >= 1:
                kb.wait(A, s_wt, un)
            kb.on(A, lambda E, pb=pb: E.copy(wstg[:], pb.ap()[:]))
            kb.inc_last(A, s_we, 1)
            kb.wait(T, s_we, un + 1)
            if un >= 1:
                kb.wait(T, s_wa, un)
            tv6 = bk[6].ap().bitcast(BF16)
            for tc in range(4):
                kb.on(T, lambda E, tc=tc, tv6=tv6: E.transpose(
                    tv6[:, 128 * tc:128 * (tc + 1)],
                    wstg[:, 128 * tc:128 * (tc + 1)], st.idb[:]))
            kb.inc_last(T, s_wt, 1)
            kb.wait(V, s_wt, un + 1)
            for tc in range(4):
                kb.on(V, lambda E, tt=tt, tc=tc, dm=dm, tv6=tv6: E.tensor_tensor(
                    st.x_tm[:, 4 * tt + tc, 128 * dm:128 * (dm + 1)],
                    st.x_tm[:, 4 * tt + tc, 128 * dm:128 * (dm + 1)],
                    tv6[:, 128 * tc:128 * (tc + 1)], ALU.add))
            kb.inc_last(V, s_wa, 1)
    s_xs = kb.sem("p3_xs")
    kb.wait(Sy, s_wa, 32)
    kb.on(Sy, lambda E: E.dma_start(
        st.xres.ap().rearrange("c p d -> p c d"), st.x_tm[:]))
    kb.inc_last(Sy, s_xs, 16)
    kb.wait(Sy, s_xs, 16)
    kb.end_scope()
    kb.end_scope()


def p4_ffn(kb, st, cst, pi):
    T, V, A, G, Sy = ENGS
    fx = f"p4{pi}"
    kb.begin_scope()
    xh4 = kb.sbs(fx + "_xh", [128, 8, 2048], BF16)
    facc = kb.sbs(fx + "_facc", [128, 8, 2048], BF16)
    wts = kb.sbs(fx + "_wts", [16, TPC], F32)
    st.p4xh = xh4
    st.p4facc = facc
    st.p4wts = wts
    wtsb = kb.sbs(fx + "_wtsb", [16, TPC], BF16)
    gwb = kb.sbs(fx + "_gw", [128, 8, 8], BF16)
    gbs = kb.sbs(fx + "_gb", [128, 8], F32)
    b3s = kb.sbs(fx + "_b3", [10, 1024], BF16)
    s1b = kb.sbs(fx + "_s1", [128, 512], BF16)
    wbc = kb.sbs(fx + "_wbc", [128, 512], BF16)
    wrow = kb.sbs(fx + "_wrow", [1, 512], F32)
    xhs = kb.sbs(fx + "_xhs", [128, D], BF16)
    m8 = kb.sbs(fx + "_m8", [128, 8], F32)
    e1 = kb.sbs(fx + "_e1", [128, 8], F32)
    m2 = kb.sbs(fx + "_m2", [128, 8], F32)
    col4 = kb.sbs(fx + "_col", [128, 8], F32)
    scr4 = kb.sbs(fx + "_scr", [128, D], F32)

    s_w = kb.sem(fx + "_w")
    s_wg = kb.semg(fx + "_wg")
    s_x = kb.sem(fx + "_x")
    s_ln = kb.sem(fx + "_ln")
    s_tr = kb.sem(fx + "_tr")
    s_xf = kb.sem(fx + "_xf")
    s_gt = kb.sem(fx + "_gt")
    s_gd = kb.sem(fx + "_gd")
    s_ex = kb.sem(fx + "_ex")
    s_wtp = kb.sem(fx + "_wtp")
    s_wte = kb.sem(fx + "_wte")
    s_wr = kb.semg(fx + "_wr")
    s_bc = kb.sem(fx + "_bc")
    s_h = kb.sem(fx + "_h")
    s_sil = kb.sem(fx + "_sil")
    s_gl = kb.sem(fx + "_gl")
    s_w3 = kb.sem(fx + "_w3m")
    s_fa = kb.sem(fx + "_fa")
    s_b1 = [kb.sem(fx + "_b1d0"), kb.sem(fx + "_b1d1")]
    s_w12 = kb.semg(fx + "_w12d")
    s_w3d = kb.semg(fx + "_w3d")
    s_tp2 = kb.sem(fx + "_tp2")
    s_ra = kb.sem(fx + "_ra")
    bk = st.bk
    trv = bk[0].ap().bitcast(BF16)

    kb.on(G, lambda E: E.dma_start(
        gwb[:], st.d["gw"].ap().rearrange("(kc p) m -> p kc m", p=128)))
    kb.inc_last(G, s_wg, 16)
    kb.on(G, lambda E: E.dma_start(b3s[:], st.d["b3"].ap()))
    kb.inc_last(G, s_wg, 16)
    kb.on(Sy, lambda E: E.dma_start(gbs[:], st.d["gb128"].ap()))
    kb.inc_last(Sy, s_w, 16)
    kb.on(V, lambda E: E.memset(facc[:].rearrange("p a b -> p (a b)"), 0.0))
    kb.on(V, lambda E: E.memset(wts[:], 0.5))
    svm = kb.sem(fx + "_vm")
    kb.inc_last(V, svm, 1)
    nw0 = s_w.n
    nw0g = s_wg.n

    kb.begin_scope()
    xa = kb.sbs(fx + "_xa", [128, QC, D], F32)
    kb.on(Sy, lambda E: E.dma_start(
        xa[:], st.xres.ap().rearrange("c p d -> p c d")))
    kb.inc_last(Sy, s_x, 16)
    kb.wait(V, s_x, 16)
    for qc_ in range(QC):
        for pp in range(2):
            kb.on(V, lambda E, qc_=qc_, pp=pp: E.tensor_tensor(
                scr4[:, 512 * pp:512 * (pp + 1)],
                xa[:, qc_, 512 * pp:512 * (pp + 1)],
                xa[:, qc_, 512 * pp:512 * (pp + 1)], ALU.mult))
        kb.on(V, lambda E: E.reduce_sum(col4[:, 1:2], scr4[:], axis=AX))
        kb.on(V, lambda E, qc_=qc_: E.reduce_sum(col4[:, 0:1],
                                                 xa[:, qc_, :], axis=AX))
        kb.on(V, lambda E: E.tensor_scalar(col4[:, 0:1], col4[:, 0:1],
                                           cst(C_INV1024), None, ALU.mult))
        kb.on(V, lambda E: E.tensor_scalar(col4[:, 1:2], col4[:, 1:2],
                                           cst(C_INV1024), None, ALU.mult))
        kb.on(V, lambda E: E.tensor_tensor(col4[:, 2:3], col4[:, 0:1],
                                           col4[:, 0:1], ALU.mult))
        kb.pad(V, 1)
        kb.on(V, lambda E: E.tensor_tensor(col4[:, 1:2], col4[:, 1:2],
                                           col4[:, 2:3], ALU.subtract))
        kb.pad(V, 1)
        kb.rsqrt(V, col4[:, 3:4], col4[:, 1:2], col4[:, 4:5], col4[:, 5:6],
                 col4[:, 6:7], eps_col=cst(C_EPS5))
        if qc_ >= 1:
            kb.wait(V, s_tr, qc_)
        kb.on(V, lambda E, qc_=qc_: E.tensor_scalar(
            xhs[:], xa[:, qc_, :], col4[:, 0:1], col4[:, 3:4],
            ALU.subtract, ALU.mult))
        kb.inc_last(V, s_ln, 1)
        kb.wait(T, s_ln, qc_ + 1)
        if qc_ >= 1:
            kb.wait(T, s_xf, qc_)
        for j in range(8):
            kb.on(T, lambda E, j=j: E.transpose(
                trv[:, 128 * j:128 * (j + 1)], xhs[:, 128 * j:128 * (j + 1)],
                st.idb[:]))
        kb.inc_last(T, s_tr, 1)
        kb.wait(A, s_tr, qc_ + 1)
        kb.on(A, lambda E, qc_=qc_: E.copy(xh4[:, :, 128 * qc_:128 * (qc_ + 1)],
                                           trv[:].rearrange("p (a b) -> p a b",
                                                            a=8)))
        kb.inc_last(A, s_xf, 1)
        if qc_ == 0:
            kb.wait(T, s_w, nw0)
            kb.wait(T, s_wg, nw0g)
        kb.wait(T, s_xf, qc_ + 1)
        if qc_ >= 1:
            kb.wait(T, s_gd, 3 * qc_)
        for kc in range(8):
            kb.on(T, lambda E, kc=kc, qc_=qc_: E.matmul(
                bk[5].ap()[:, 0:8], xh4[:, kc, 128 * qc_:128 * (qc_ + 1)],
                gwb[:, kc, :], start=(kc == 0), stop=(kc == 7)))
        kb.inc_last(T, s_gt, 1)
        kb.wait(V, s_gt, qc_ + 1)
        if qc_ == 0:
            kb.wait(V, s_w, nw0)
        kb.on(V, lambda E: E.tensor_tensor(m8[:], bk[5].ap()[:, 0:8], gbs[:],
                                           ALU.add))
        kb.inc_last(V, s_gd, 1)
        kb.pad(V, 1)
        kb.on(V, lambda E: E.reduce_max(col4[:, 0:1], m8[:], axis=AX))
        kb.pad(V, 1)
        kb.on(V, lambda E: E.tensor_scalar(m8[:], m8[:], col4[:, 0:1], None,
                                           ALU.subtract))
        tex = kb.inc_last(V, s_gd, 1)
        kb.wait(A, s_gd, tex)
        kb.on(A, lambda E: E.activation(m8[:], m8[:], AF.Exp))
        kb.inc_last(A, s_ex, 1)
        kb.wait(V, s_ex, qc_ + 1)
        kb.on(V, lambda E: E.reduce_sum(col4[:, 0:1], m8[:], axis=AX))
        kb.pad(V, 1)
        kb.on(V, lambda E: E.reciprocal(col4[:, 1:2], col4[:, 0:1]))
        kb.pad(V, 1)
        kb.on(V, lambda E: E.tensor_scalar(m8[:], m8[:], col4[:, 1:2], None,
                                           ALU.mult))
        kb.pad(V, 1)
        kb.on(V, lambda E: E.reduce_max(col4[:, 0:1], m8[:], axis=AX))
        kb.pad(V, 1)
        kb.on(V, lambda E: E.tensor_scalar(e1[:], m8[:], col4[:, 0:1], None,
                                           ALU.is_equal))
        kb.pad(V, 1)
        kb.on(V, lambda E: E.reduce_sum(col4[:, 1:2], e1[:], axis=AX))
        kb.pad(V, 1)
        kb.on(V, lambda E: E.scalar_tensor_tensor(
            m2[:], e1[:], cst(C_M1E30), m8[:], ALU.mult, ALU.add))
        kb.pad(V, 1)
        kb.on(V, lambda E: E.reduce_max(col4[:, 2:3], m2[:], axis=AX))
        kb.pad(V, 1)
        kb.on(V, lambda E: E.tensor_scalar(m2[:], m8[:], col4[:, 2:3], None,
                                           ALU.is_equal))
        kb.on(V, lambda E: E.tensor_scalar(col4[:, 3:4], col4[:, 1:2], 1.5,
                                           None, ALU.is_lt))
        kb.pad(V, 1)
        kb.on(V, lambda E: E.scalar_tensor_tensor(
            e1[:], m2[:], col4[:, 3:4], e1[:], ALU.mult, ALU.add))
        kb.pad(V, 1)
        kb.on(V, lambda E: E.tensor_tensor(m8[:], m8[:], e1[:], ALU.mult))
        tw = kb.inc_last(V, s_gd, 1)
        kb.wait(T, s_gd, tw)
        if qc_ >= 1:
            kb.wait(T, s_wte, qc_)
        kb.on(T, lambda E: E.transpose(bk[6].ap()[0:8, 0:128], m8[:],
                                       st.idf[:]))
        kb.inc_last(T, s_wtp, 1)
        kb.wait(V, s_wtp, qc_ + 1)
        if qc_ == 0:
            kb.wait(V, svm, 1)
        kb.on(V, lambda E, qc_=qc_: E.tensor_copy(
            wts[0:8, 128 * qc_:128 * (qc_ + 1)], bk[6].ap()[0:8, 0:128]))
        kb.inc_last(V, s_wte, 1)
    kb.wait(V, s_wte, QC)
    kb.on(V, lambda E: E.tensor_copy(wtsb[:], wts[:]))
    swb = kb.sem(fx + "_swb")
    kb.inc_last(V, swb, 1)
    kb.end_scope()
    kb.barrier(st.bar)

    kb.begin_scope()
    w12r = kb.sbs(fx + "_w12", [128, 2, 8, 2048], BF16)
    w3s = kb.sbs(fx + "_w3", [128, 16, 1024], BF16)
    glu = kb.sbs(fx + "_glu", [128, 16, 512], BF16)
    b1c = kb.sbs(fx + "_b1", [128, 2, 16], F32)
    b2c = kb.sbs(fx + "_b2", [128, 2, 16], F32)
    for e in range(10):
        rn1, rn2, rn3, ei = (("rw1", "rw2", "rw3", e) if e < 8 else
                             ("sw1", "sw2", "sw3", e - 8))
        bn1, bn2 = (("rb1t", "rb2t") if e < 8 else ("sb1t", "sb2t"))
        if e >= 2:
            kb.wait(Sy, s_gl, 64 * (e - 1))
        for bn, dst in ((bn1, b1c), (bn2, b2c)):
            kb.on(Sy, lambda E, bn=bn, dst=dst, ei=ei, e=e: E.dma_start(
                dst[:, e % 2, :], st.d[bn].ap()[ei]))
            kb.inc_last(Sy, s_b1[e % 2], 16)
        if e >= 1:
            kb.wait(G, s_h, 64 * e)
        for mi, rn in ((0, rn1), (1, rn2)):
            kb.on(G, lambda E, rn=rn, ei=ei, mi=mi: E.dma_start(
                w12r[:, mi, :, :],
                st.d[rn].ap()[ei].rearrange("(kc p) m -> p kc m", p=128)))
            kb.inc_last(G, s_w12, 16)
        if e >= 1:
            kb.wait(G, s_w3, 64 * e)
        kb.on(G, lambda E, rn3=rn3, ei=ei: E.dma_start(
            w3s[:], st.d[rn3].ap()[ei].rearrange("(kc p) m -> p kc m", p=128)))
        kb.inc_last(G, s_w3d, 16)
        for t4 in range(4):
            nbu = 4 * e + t4 + 1
            kb.wait(G, s_bc, 2 * max(0, nbu - 1))
            kb.on(G, lambda E, e=e, t4=t4: E.dma_start(
                wrow[:], wts[e:e + 1, 512 * t4:512 * (t4 + 1)]))
            kb.inc_last(G, s_wr, 16)
            kb.wait(T, s_wr, 16 * nbu)
            if e == 0 and t4 == 0:
                kb.wait(T, s_gd, 3 * QC)
            kb.wait(T, s_gl, 16 * max(0, nbu - 1))
            kb.on(T, lambda E: E.matmul(bk[5].ap()[:], st.ones_row[:],
                                        wrow[:], start=True, stop=True))
            tbc = kb.inc_last(T, s_bc, 1)
            kb.wait(A, s_bc, tbc)
            kb.wait(A, s_gl, 16 * max(0, nbu - 1))
            kb.on(A, lambda E: E.copy(wbc[:], bk[5].ap()[:]))
            kb.inc_last(A, s_bc, 1)
            kb.wait(T, s_w12, 32 * (e + 1))
            for ht in range(16):
                un = 16 * (4 * e + t4) + ht
                hb1 = bk[1]
                hb2 = bk[2]
                if un >= 1:
                    kb.wait(T, s_gl, un)
                for kc in range(8):
                    kb.on(T, lambda E, kc=kc, ht=ht, t4=t4: E.matmul(
                        hb1.ap()[:], w12r[:, 0, kc, 128 * ht:128 * (ht + 1)],
                        xh4[:, kc, 512 * t4:512 * (t4 + 1)],
                        start=(kc == 0), stop=(kc == 7)))
                for kc in range(8):
                    kb.on(T, lambda E, kc=kc, ht=ht, t4=t4: E.matmul(
                        hb2.ap()[:], w12r[:, 1, kc, 128 * ht:128 * (ht + 1)],
                        xh4[:, kc, 512 * t4:512 * (t4 + 1)],
                        start=(kc == 0), stop=(kc == 7)))
                tmm = kb.inc_last(T, s_h, 1)
                kb.wait(A, s_h, tmm)
                if t4 == 0 and ht == 0:
                    kb.wait(A, s_b1[e % 2], 32 * (e // 2 + 1))
                kb.on(A, lambda E, ht=ht, e=e: E.activation(
                    s1b[:], hb1.ap()[:], AF.Silu, bias=b1c[:, e % 2, ht:ht + 1]))
                tsl = kb.inc_last(A, s_sil, 1)
                kb.wait(V, s_sil, tsl)
                kb.wait(V, s_bc, 2 * nbu)
                kb.on(V, lambda E, ht=ht, e=e: E.scalar_tensor_tensor(
                    glu[:, ht, :], hb2.ap()[:], b2c[:, e % 2, ht:ht + 1],
                    s1b[:], ALU.add, ALU.mult))
                kb.on(V, lambda E, ht=ht: E.tensor_tensor(
                    glu[:, ht, :], glu[:, ht, :], wbc[:], ALU.mult))
                kb.inc_last(V, s_gl, 1)
            if e == 9 and t4 == 0:
                kb.wait(T, swb, 1)
            kb.wait(T, s_gl, 16 * (4 * e + t4 + 1))
            if t4 == 0:
                kb.wait(T, s_w3d, 16 * (e + 1))
            for half in range(2):
                for dm in range(8):
                    w3u = 16 * (4 * e + t4) + 8 * half + dm
                    ob = bk[3 + (w3u % 2)]
                    if w3u >= 2:
                        kb.wait(T, s_fa, w3u - 1)
                    last = (e == 9)
                    for kc in range(16):
                        kb.on(T, lambda E, kc=kc, dm=dm, half=half, t4=t4,
                              ob=ob, stp=(kc == 15 and not last): E.matmul(
                                  ob.ap()[:, 0:256],
                                  w3s[:, kc, 128 * dm:128 * (dm + 1)],
                                  glu[:, kc, 256 * half:256 * (half + 1)],
                                  start=(kc == 0), stop=stp))
                    if last:
                        kb.on(T, lambda E, dm=dm, half=half, t4=t4, ob=ob:
                              E.matmul(
                                  ob.ap()[:, 0:256],
                                  b3s[0:10, 128 * dm:128 * (dm + 1)],
                                  wtsb[0:10, 512 * t4 + 256 * half:
                                       512 * t4 + 256 * (half + 1)],
                                  start=False, stop=True))
                    tpm = kb.inc_last(T, s_w3, 1)
                    kb.wait(V, s_w3, tpm)
                    kb.on(V, lambda E, dm=dm, half=half, t4=t4, ob=ob:
                          E.tensor_tensor(
                              facc[:, dm, 512 * t4 + 256 * half:
                                   512 * t4 + 256 * (half + 1)],
                              facc[:, dm, 512 * t4 + 256 * half:
                                   512 * t4 + 256 * (half + 1)],
                              ob.ap()[:, 0:256], ALU.add))
                    kb.inc_last(V, s_fa, 1)
    kb.end_scope()
    kb.barrier(st.bar)

    kb.begin_scope()
    xc = kb.sbs(fx + "_xc", [128, QC, D], F32)
    kb.on(Sy, lambda E: E.dma_start(
        xc[:], st.xres.ap().rearrange("c p d -> p c d")))
    kb.inc_last(Sy, s_x, 32)
    kb.wait(V, s_x, 48)
    tv6 = bk[6].ap().bitcast(BF16)
    for qc_ in range(QC):
        if qc_ >= 1:
            kb.wait(T, s_ra, qc_)
        for dm in range(8):
            kb.on(T, lambda E, qc_=qc_, dm=dm: E.transpose(
                tv6[:, 128 * dm:128 * (dm + 1)],
                facc[:, dm, 128 * qc_:128 * (qc_ + 1)], st.idb[:]))
        kb.inc_last(T, s_tp2, 1)
        kb.wait(V, s_tp2, qc_ + 1)
        kb.on(V, lambda E, qc_=qc_: E.tensor_tensor(
            xc[:, qc_, :], xc[:, qc_, :], tv6[:], ALU.add))
        kb.inc_last(V, s_ra, 1)
    kb.wait(Sy, s_ra, QC)
    if pi == 0:
        kb.on(Sy, lambda E: E.dma_start(
            st.xres.ap().rearrange("c p d -> p c d"), xc[:]))
    else:
        kb.on(Sy, lambda E: E.dma_start(
            st.out.ap().rearrange("(c p) d -> p c d", p=128), xc[:]))
    kb.inc_last(Sy, s_x, 16)
    if pi == 0 and st.x2s is not None:
        kb.on(Sy, lambda E: E.dma_start(
            st.x2s.ap().rearrange("c p d -> p c d"), xc[:]))
        kb.inc_last(Sy, s_x, 16)
        kb.wait(Sy, s_x, 80)
    else:
        kb.wait(Sy, s_x, 64)
    kb.end_scope()
    kb.end_scope()


def p5_attnb(kb, st, cst):
    T, V, A, G, Sy = ENGS
    kb.begin_scope()
    x5 = kb.sbs("p5_x", [128, QC, D], F32)
    ckvA = kb.sbs("p5_ckv", [65, L], BF16)
    q5 = kb.sbs("p5_q", [128, 8, TPC], BF16)
    st.p5ckvA = ckvA
    st.p5q5 = q5
    col5 = kb.sbs("p5_col", [128, 16], F32)
    scr5 = kb.sbs("p5_scr", [128, D], F32)
    qn2 = kb.sbs("p5_qn2", [128, 16], F32)
    bqs = kb.sbs("p5_bqs", [16, 128], BF16)
    kn2 = kb.sbs("p5_kn2", [128, 64], F32)
    knm = kb.sbs("p5_knm", [1, 8], F32)
    st.p5knm = knm
    wvuTs = kb.sbs("p5_wvu", [64, D], BF16)
    bvub = kb.sbs("p5_bvu", [1, D], BF16)

    s_w = kb.sem("p5_w")
    s_wg = kb.semg("p5_wg")
    s_cc = kb.semg("p5_cc")
    s_bk = kb.sem("p5_bk")
    s_pe = kb.sem("p5_pe")
    s_dv = kb.sem("p5_dv")
    s_ac = kb.sem("p5_ac")
    s_ex = kb.sem("p5_ex")
    s_av = kb.sem("p5_av")
    s_nb = kb.sem("p5_nb")
    s_ci = kb.sem("p5_ci")
    s_ra = kb.sem("p5_ra")
    bk = st.bk
    bks = Banks(kb, st.bk, "p5")

    def pe(fn):
        kb.on(T, fn)

    def dv(fn):
        kb.on(V, fn)

    nbq = (kb.dout if "nbq" in kb.taps else kb.dint)(
        "p5_nbqd", [16, TPC], BF16)

    kb.on(Sy, lambda E: E.dma_start(
        x5[:], st.xres.ap().rearrange("c p d -> p c d")))
    s_x5 = kb.sem("p5_x5")
    kb.inc_last(Sy, s_x5, 16)
    kb.wait(V, s_x5, 16)
    kb.on(G, lambda E: E.dma_start(
        wvuTs[:], st.d["wvuT"].ap()))
    kb.inc_last(G, s_wg, 16)
    kb.on(G, lambda E: E.dma_start(bvub[:], st.d["bvur"].ap()))
    kb.inc_last(G, s_wg, 16)

    kb.begin_scope()
    xh5 = kb.sbs("p5_xh5", [128, 8, TPC], BF16)
    cq5 = kb.sbs("p5_cq", [128, 4, TPC], BF16)
    ckvo = kb.sbs("p5_ckvo", [64, TPC], BF16)
    wqdTs = kb.sbs("p5_wqd", [128, 8, ABQL], BF16)
    wqkTs = kb.sbs("p5_wqk", [128, 4, D], BF16)
    wkvdTs = kb.sbs("p5_wkv", [128, 8, ABKV], BF16)
    bqds = kb.sbs("p5_bqd", [128, 4], F32)
    bqks = kb.sbs("p5_bqk", [128, 8], F32)
    bkvs = kb.sbs("p5_bkv", [64, 1], F32)
    xh5s = kb.sbs("p5_xhs", [128, D], BF16)
    qsq = kb.sbs("p5_qsq", [64, TPC], BF16)
    ksq = kb.sbs("p5_ksq", [64, 2048], BF16)
    kb.on(G, lambda E: E.dma_start(
        wqdTs[:], st.d["wqdT"].ap().rearrange("(kc p) m -> p kc m", p=128)))
    kb.inc_last(G, s_wg, 16)
    kb.on(G, lambda E: E.dma_start(
        wqkTs[:], st.d["wqkT"].ap().rearrange("(kc p) m -> p kc m", p=128)))
    kb.inc_last(G, s_wg, 16)
    kb.on(G, lambda E: E.dma_start(
        wkvdTs[:], st.d["wkvdT"].ap().rearrange("(kc p) m -> p kc m", p=128)))
    kb.inc_last(G, s_wg, 16)
    kb.on(Sy, lambda E: E.dma_start(bqds[:], st.d["bqdt"].ap()))
    kb.inc_last(Sy, s_w, 16)
    kb.on(Sy, lambda E: E.dma_start(bqks[:], st.d["bqkt"].ap()))
    kb.inc_last(Sy, s_w, 16)
    kb.on(Sy, lambda E: E.dma_start(bkvs[:], st.d["bkvc"].ap()))
    kb.inc_last(Sy, s_w, 16)
    nw = s_w.n
    nwg = s_wg.n
    kb.wait(V, s_w, nw)
    kb.wait(V, s_wg, nwg)
    kb.on(V, lambda E: E.tensor_scalar(bqks[:], bqks[:], cst(C_EIGHTH), None,
                                       ALU.mult))
    t_bqscale = kb.inc_last(V, s_dv, 1)
    kb.wait(A, s_w, nw)
    kb.wait(A, s_dv, t_bqscale)
    s_ln5 = kb.sem("p5_ln")
    s_tr5 = kb.sem("p5_tr")
    s_xf5 = kb.sem("p5_xf")
    trv = bk[0].ap().bitcast(BF16)
    for qc_ in range(QC):
        kb.on(V, lambda E, qc_=qc_: E.tensor_tensor(
            scr5[:], x5[:, qc_, :], x5[:, qc_, :], ALU.mult))
        kb.on(V, lambda E: E.reduce_sum(col5[:, 1:2], scr5[:], axis=AX))
        kb.on(V, lambda E, qc_=qc_: E.reduce_sum(col5[:, 0:1],
                                                 x5[:, qc_, :], axis=AX))
        kb.on(V, lambda E: E.tensor_scalar(col5[:, 0:1], col5[:, 0:1],
                                           cst(C_INV1024), None, ALU.mult))
        kb.on(V, lambda E: E.tensor_scalar(col5[:, 1:2], col5[:, 1:2],
                                           cst(C_INV1024), None, ALU.mult))
        kb.on(V, lambda E: E.tensor_tensor(col5[:, 2:3], col5[:, 0:1],
                                           col5[:, 0:1], ALU.mult))
        kb.pad(V, 1)
        kb.on(V, lambda E: E.tensor_tensor(col5[:, 1:2], col5[:, 1:2],
                                           col5[:, 2:3], ALU.subtract))
        kb.pad(V, 1)
        kb.rsqrt(V, col5[:, 3:4], col5[:, 1:2], col5[:, 4:5], col5[:, 5:6],
                 col5[:, 6:7], eps_col=cst(C_EPS5))
        if qc_ >= 1:
            kb.wait(V, s_tr5, qc_)
        kb.on(V, lambda E, qc_=qc_: E.tensor_scalar(
            xh5s[:], x5[:, qc_, :], col5[:, 0:1], col5[:, 3:4],
            ALU.subtract, ALU.mult))
        kb.inc_last(V, s_ln5, 1)
        kb.wait(T, s_ln5, qc_ + 1)
        if qc_ >= 1:
            kb.wait(T, s_xf5, qc_)
        for j in range(8):
            kb.on(T, lambda E, j=j: E.transpose(
                trv[:, 128 * j:128 * (j + 1)], xh5s[:, 128 * j:128 * (j + 1)],
                st.idb[:]))
        kb.inc_last(T, s_tr5, 1)
        kb.wait(A, s_tr5, qc_ + 1)
        kb.on(A, lambda E, qc_=qc_: E.copy(
            xh5[:, :, 128 * qc_:128 * (qc_ + 1)],
            trv[:].rearrange("p (a b) -> p a b", a=8)))
        kb.inc_last(A, s_xf5, 1)
    kb.wait(T, s_xf5, QC)
    kb.wait(T, s_w, nw)
    kb.wait(T, s_wg, nwg)
    s_cp = kb.sem("p5_cp")
    s_ce = kb.sem("p5_ce")
    ncp = 0
    for t4 in range(4):
        for mt in range(5):
            ncp += 1
            pb = bk[1 + (ncp % 2)]
            if ncp >= 3:
                kb.wait(T, s_ce, ncp - 2)
            for kc in range(8):
                if mt < 4:
                    pe(lambda E, kc=kc, mt=mt, t4=t4, pb=pb: E.matmul(
                        pb.ap()[:], wqdTs[:, kc, 128 * mt:128 * (mt + 1)],
                        xh5[:, kc, 512 * t4:512 * (t4 + 1)],
                        start=(kc == 0), stop=(kc == 7)))
                else:
                    pe(lambda E, kc=kc, t4=t4, pb=pb: E.matmul(
                        pb.ap()[0:64, :], wkvdTs[:, kc, :],
                        xh5[:, kc, 512 * t4:512 * (t4 + 1)],
                        start=(kc == 0), stop=(kc == 7)))
            kb.inc_last(T, s_cp, 1)
            kb.wait(A, s_cp, ncp)
            if mt < 4:
                kb.on(A, lambda E, mt=mt, t4=t4, pb=pb: E.activation(
                    cq5[:, mt, 512 * t4:512 * (t4 + 1)], pb.ap()[:], AF.Identity,
                    bias=bqds[:, mt:mt + 1]))
            else:
                kb.on(A, lambda E, t4=t4, pb=pb: E.activation(
                    ckvo[:, 512 * t4:512 * (t4 + 1)], pb.ap()[0:64, :],
                    AF.Identity, bias=bkvs[:, 0:1]))
            kb.inc_last(A, s_ce, 1)
    kb.wait(Sy, s_ce, 20)
    kb.on(Sy, lambda E: E.dma_start(st.ckv_in.ap(), ckvo[:]))
    kb.inc_last(Sy, s_ci, 16)
    kb.wait(G, s_ci, 16)
    kb.on(G, lambda E: E.collective_compute(
        "AllGather", ALU.bypass, replica_groups=[list(range(8))],
        ins=[st.ckv_in.ap().opt()], outs=[st.ckv_out.ap().opt()]))
    kb.inc_last(G, s_cc, 1)
    kb.wait(Sy, s_cc, 1)
    kb.on(Sy, lambda E: dyn_dma(
        E, st, ckvA[0:64, :].rearrange("p (j t) -> p j t", j=4),
        lambda a: st.ckv_out.ap()[a, :].rearrange("(j r) t -> r j t", r=64),
        [(7, 256)]))
    kb.inc_last(Sy, s_bk, 16)
    kb.wait(V, s_bk, 16)
    kb.on(V, lambda E: E.memset(ckvA[64:65, :], 1.0))
    kb.inc_last(V, s_dv, 1)
    t_ones = s_dv.n

    nq = 0
    s_qp = kb.sem("p5_qp")
    s_qe = kb.sem("p5_qe")
    kb.wait(T, s_ce, 20)
    for t4 in range(4):
        for mt in range(8):
            nq += 1
            pb = bk[1 + (nq % 2)]
            if nq >= 3:
                kb.wait(T, s_qe, nq - 2)
            for kc in range(4):
                pe(lambda E, kc=kc, mt=mt, t4=t4, pb=pb: E.matmul(
                    pb.ap()[:], wqkTs[:, kc, 128 * mt:128 * (mt + 1)],
                    cq5[:, kc, 512 * t4:512 * (t4 + 1)],
                    start=(kc == 0), stop=(kc == 3)))
            kb.inc_last(T, s_qp, 1)
            kb.wait(A, s_qp, nq)
            kb.on(A, lambda E, mt=mt, t4=t4, pb=pb: E.activation(
                q5[:, mt, 512 * t4:512 * (t4 + 1)], pb.ap()[:], AF.Identity,
                bias=bqks[:, mt:mt + 1], scale=cst(C_EIGHTH)))
            kb.inc_last(A, s_qe, 1)

    if st.cq5s is not None:
        sdbg = kb.sem("p5_dbg")
        kb.wait(Sy, s_ce, 20)
        kb.on(Sy, lambda E: E.dma_start(st.cq5s.ap(), cq5[:]))
        kb.inc_last(Sy, sdbg, 16)
        kb.on(Sy, lambda E: E.dma_start(st.xh5s2.ap(), xh5[:]))
        kb.inc_last(Sy, sdbg, 16)
        kb.wait(Sy, sdbg, 32)
    if st.q5s is not None:
        sdbg2 = kb.sem("p5_dbg2")
        kb.wait(Sy, s_qe, 32)
        kb.on(Sy, lambda E: E.dma_start(st.q5s.ap(), q5[:]))
        kb.inc_last(Sy, sdbg2, 16)
        if st.cq5l is not None:
            kb.on(Sy, lambda E: E.dma_start(st.cq5l.ap(), cq5[:]))
            kb.inc_last(Sy, sdbg2, 16)
        kb.wait(Sy, sdbg2, sdbg2.n)

    kb.wait(A, s_bk, 16)
    for pc in range(4):
        if pc >= 1:
            kb.wait(A, s_pe, s_pe.n)
        kb.on(A, lambda E, pc=pc: E.activation(
            ksq[:], ckvA[0:64, 2048 * pc:2048 * (pc + 1)], AF.Square))
        tks = kb.inc_last(A, s_ac, 1)
        kb.wait(T, s_ac, tks)
        for c in range(16):
            pe(lambda E, c=c, pc=pc: E.matmul(
                bk[5].ap()[:, 16 * pc + c:16 * pc + c + 1],
                ksq[0:64, 128 * c:128 * (c + 1)],
                st.ones_bf[0:64, :], start=True, stop=True))
        kb.inc_last(T, s_pe, 1)
    tkn = s_pe.n
    kb.wait(V, s_pe, tkn)
    dv(lambda E: E.tensor_copy(kn2[:], bk[5].ap()[:, 0:64]))
    kb.pad(V, 1)
    dv(lambda E: E.reduce_max(col5[:, 0:1], kn2[:], axis=AX))
    tkm = kb.inc_last(V, s_dv, 1)
    kb.wait(T, s_dv, tkm)
    pe(lambda E: E.transpose(bk[5].ap()[0:1, 64:192], col5[:, 0:1],
                             st.idf[:]))
    tkt = kb.inc_last(T, s_pe, 1)
    kb.wait(V, s_pe, tkt)
    dv(lambda E: E.reduce_max(knm[:, 0:1], bk[5].ap()[0:1, 64:192], axis=AX))
    kb.pad(V, 1)
    kb.rsqrt(V, knm[:, 1:2], knm[:, 0:1], knm[:, 2:3], knm[:, 3:4],
             knm[:, 4:5], eps_col=cst(C_EPS6, 1))
    dv(lambda E: E.tensor_tensor(knm[:, 2:3], knm[:, 0:1], knm[:, 1:2],
                                 ALU.mult))
    kb.pad(V, 1)
    dv(lambda E: E.tensor_scalar(knm[:, 2:3], knm[:, 2:3], -1.0, None,
                                 ALU.mult))
    tneg = kb.inc_last(V, s_dv, 1)
    kb.wait(T, s_dv, tneg)
    pe(lambda E: E.matmul(bk[5].ap()[:, 192:193], st.ones_row[:],
                          knm[:, 2:3], start=True, stop=True))
    tbc5 = kb.inc_last(T, s_pe, 1)
    kb.wait(V, s_pe, tbc5)
    dv(lambda E: E.tensor_copy(col5[:, 8:9], bk[5].ap()[:, 192:193]))
    tnk = kb.inc_last(V, s_dv, 1)

    s_qn = kb.sem("p5_qn")
    for h in range(16):
        kb.on(A, lambda E, h=h: E.activation(
            qsq[:], q5[64 * (h % 2):64 * (h % 2) + 64, h // 2, :], AF.Square))
        tqs = kb.inc_last(A, s_ac, 1)
        kb.wait(T, s_ac, tqs)
        if h >= 1:
            kb.wait(T, s_qn, 2 * h)
        for tc in range(16):
            pe(lambda E, tc=tc: E.matmul(
                bk[6].ap()[:, tc:tc + 1], qsq[0:64, 128 * tc:128 * (tc + 1)],
                st.ones_bf[0:64, :], start=True, stop=True))
        tqp = kb.inc_last(T, s_pe, 1)
        kb.wait(V, s_pe, tqp)
        kb.wait(V, s_dv, tnk)
        dv(lambda E: E.tensor_copy(qn2[:], bk[6].ap()[:, 0:16]))
        kb.inc_last(V, s_qn, 1)
        kb.pad(V, 1)
        kb.rsqrt(V, scr5[:, 0:16], qn2[:], scr5[:, 16:32], scr5[:, 32:48],
                 scr5[:, 48:64], eps_col=cst(C_EPS6))
        dv(lambda E: E.tensor_tensor(scr5[:, 0:16], scr5[:, 0:16], qn2[:],
                                     ALU.mult))
        kb.pad(V, 1)
        dv(lambda E: E.tensor_scalar(scr5[:, 0:16], scr5[:, 0:16],
                                     col5[:, 8:9], None, ALU.mult))
        tbq = kb.inc_last(V, s_dv, 1)
        kb.wait(T, s_dv, tbq)
        pe(lambda E: E.transpose(bk[6].ap()[0:16, 16:144], scr5[:, 0:16],
                                 st.idf[:]))
        tqt = kb.inc_last(T, s_pe, 1)
        kb.wait(V, s_pe, tqt)
        if h >= 1:
            kb.wait(V, s_nb, 16 * h)
        dv(lambda E: E.tensor_copy(bqs[:], bk[6].ap()[0:16, 16:144]))
        tbe = kb.inc_last(V, s_qn, 1)
        kb.wait(Sy, s_qn, tbe)
        kb.on(Sy, lambda E, h=h: E.dma_start(
            nbq.ap()[h].rearrange("(c t) -> c t", c=16), bqs[:]))
        kb.inc_last(Sy, s_nb, 16)
    kb.end_scope()

    yfull = kb.sbs("p5_yfull", [128, 8, TPC], BF16)
    st.p5yfull = yfull
    probs = kb.sbs("p5_pr", [128, 2, 2048], BF16)
    vh = kb.sbs("p5_vh", [128, 64, 65], BF16)
    wva = kb.sbs("p5_wva", [65, 64], BF16)
    wprTs = kb.sbs("p5_wpr", [128, 8, D], BF16)
    bprs = kb.sbs("p5_bpr", [128, 8], F32)
    nbqr = kb.sbs("p5_nbqr", [1, 512], BF16)
    qstg = kb.sbs("p5_qstg", [64, 512], BF16)
    rbs = kb.sbs("p5_rbs", [128, 512], BF16)
    pstg = kb.sbs("p5_pstg", [128, 512], BF16)
    yr = kb.sbs("p5_yr", [1, 512], F32)
    yrb = kb.sbs("p5_yrb", [1, 512], BF16)
    kb.wait(G, s_qn, 32)
    kb.on(G, lambda E: E.dma_start(
        wprTs[:], st.d["wprT"].ap().rearrange("(kc p) m -> p kc m", p=128)))
    kb.inc_last(G, s_wg, 16)
    kb.on(Sy, lambda E: E.dma_start(bprs[:], st.d["bprt"].ap()))
    kb.inc_last(Sy, s_w, 16)
    nw5 = s_w.n
    nw5g = s_wg.n
    s_sc = kb.sem("p5_sc")
    s_exp = kb.sem("p5_exp")
    s_avm = kb.sem("p5_avm")
    s_vh = kb.sem("p5_vh")
    s_nbr = kb.sem("p5_nbr")
    s_ye = kb.sem("p5_ye")
    kb.on(V, lambda E: E.memset(vh[:, :, 64:65], 1.0))
    kb.inc_last(V, s_vh, 1)
    nsc = 0
    nexp = 0
    nav = 0
    kb.wait(V, s_wg, nw5g)
    for h in range(16):
        kb.wait(V, s_avm, nav)
        dv(lambda E, h=h: E.tensor_copy(wva[0:64, :],
                                        wvuTs[0:64, 64 * h:64 * (h + 1)]))
        dv(lambda E, h=h: E.tensor_copy(wva[64:65, :],
                                        bvub[0:1, 64 * h:64 * (h + 1)]))
        twv = kb.inc_last(V, s_vh, 1)
        kb.wait(T, s_vh, twv)
        for g4 in range(16):
            nvh = 16 * h + g4
            if nvh >= 1:
                kb.wait(T, s_vh, s_vh.n)
            for k4 in range(4):
                kt = 4 * g4 + k4
                pe(lambda E, kt=kt, k4=k4: E.matmul(
                    bk[7].ap()[:, 64 * k4:64 * (k4 + 1)],
                    ckvA[:, 128 * kt:128 * (kt + 1)], wva[:],
                    start=True, stop=True))
            tvm = kb.inc_last(T, s_pe, 1)
            kb.wait(V, s_pe, tvm)
            dv(lambda E, g4=g4: E.tensor_copy(
                vh[:, 4 * g4:4 * (g4 + 1), 0:64],
                bk[7].ap()[:, 0:256].rearrange("p (a b) -> p a b", a=4)))
            kb.inc_last(V, s_vh, 1)
        tvh_done = s_vh.n
        for qt in range(4):
            if h + qt > 0:
                kb.wait(Sy, s_sc, nsc)
            kb.on(Sy, lambda E, h=h, qt=qt: E.dma_start(
                nbqr[:],
                nbq.ap()[h, 512 * qt:512 * (qt + 1)].rearrange("(o t) -> o t",
                                                               o=1)))
            kb.inc_last(Sy, s_nbr, 16)
            kb.wait(V, s_sc, nsc)
            dv(lambda E, h=h, qt=qt: E.tensor_copy(
                qstg[:], q5[64 * (h % 2):64 * (h % 2) + 64, h // 2,
                            512 * qt:512 * (qt + 1)]))
            tqs2 = kb.inc_last(V, s_ye, 1)
            kb.wait(T, s_ye, tqs2)
            kb.wait(T, s_nbr, 16 * (4 * h + qt + 1))
            kb.wait(T, s_vh, tvh_done)
            for kg in range(16):
                sl = kg % 2
                for k4 in range(4):
                    kt = 4 * kg + k4
                    if nexp >= 4:
                        kb.wait(T, s_exp, nexp - 3)
                    pe(lambda E, kt=kt, k4=k4: E.matmul(
                        bk[k4].ap()[:], ckvA[0:64, 128 * kt:128 * (kt + 1)],
                        qstg[:], start=True, stop=False))
                    pe(lambda E, k4=k4: E.matmul(
                        bk[k4].ap()[:], st.ones_rb[:], nbqr[:],
                        start=False, stop=True))
                    nsc += 1
                    kb.inc_last(T, s_sc, 1)
                    kb.wait(A, s_sc, nsc)
                    if nexp >= 8:
                        kb.wait(A, s_avm, nexp - 7)
                    kb.on(A, lambda E, sl=sl, k4=k4: E.activation(
                        probs[:, sl, 512 * k4:512 * (k4 + 1)],
                        bk[k4].ap()[:], AF.Exp))
                    nexp += 1
                    kb.inc_last(A, s_exp, 1)
                kb.wait(T, s_exp, nexp)
                for k4 in range(4):
                    kt = 4 * kg + k4
                    pe(lambda E, kt=kt, sl=sl, k4=k4: E.matmul(
                        bk[4].ap()[0:65, :], vh[:, kt, :],
                        probs[:, sl, 512 * k4:512 * (k4 + 1)],
                        start=(kt == 0), stop=(kt == 63)))
                    nav += 1
                    kb.inc_last(T, s_avm, 1)
            kb.wait(V, s_avm, nav)
            dv(lambda E: E.reciprocal(yr[:], bk[4].ap()[64:65, 0:512]))
            kb.pad(V, 1)
            dv(lambda E: E.tensor_copy(yrb[:], yr[:]))
            tyr = kb.inc_last(V, s_ye, 1)
            kb.wait(T, s_ye, tyr)
            pe(lambda E: E.matmul(bk[5].ap()[:], st.ones_rb[:], yrb[:],
                                  start=True, stop=True))
            tbcy = kb.inc_last(T, s_pe, 1)
            kb.wait(V, s_pe, tbcy)
            dv(lambda E: E.tensor_copy(rbs[:], bk[5].ap()[:]))
            dv(lambda E, h=h, qt=qt: E.tensor_tensor(
                yfull[64 * (h % 2):64 * (h % 2) + 64, h // 2,
                      512 * qt:512 * (qt + 1)],
                bk[4].ap()[0:64, :], rbs[0:64, :], ALU.mult))
            kb.inc_last(V, s_ye, 1)
            kb.wait(T, s_ye, s_ye.n)

    kb.wait(T, s_w, nw5)
    kb.wait(T, s_wg, nw5g)
    kb.wait(A, s_w, nw5)
    s_pp5 = kb.sem("p5_pp")
    s_pse = kb.sem("p5_pse")
    s_ptr = kb.sem("p5_ptr")
    s_ra5 = kb.sem("p5_ra")
    tv6 = bk[6].ap().bitcast(BF16)
    npj = 0
    for qt in range(4):
        for dm in range(8):
            npj += 1
            pb = bk[6 + (npj % 2)]
            if npj >= 3:
                kb.wait(T, s_pse, npj - 2)
            for kc in range(8):
                pe(lambda E, kc=kc, dm=dm, qt=qt, pb=pb: E.matmul(
                    pb.ap()[:], wprTs[:, kc, 128 * dm:128 * (dm + 1)],
                    yfull[:, kc, 512 * qt:512 * (qt + 1)],
                    start=(kc == 0), stop=(kc == 7)))
            kb.inc_last(T, s_pp5, 1)
            kb.wait(A, s_pp5, npj)
            if npj >= 2:
                kb.wait(A, s_ptr, npj - 1)
            kb.on(A, lambda E, dm=dm, pb=pb: E.activation(
                pstg[:], pb.ap()[:], AF.Identity, bias=bprs[:, dm:dm + 1]))
            kb.inc_last(A, s_pse, 1)
            kb.wait(T, s_pse, npj)
            if npj >= 2:
                kb.wait(T, s_ra5, npj - 1)
            for tc in range(4):
                pe(lambda E, tc=tc: E.transpose(
                    bk[5].ap().bitcast(BF16)[:, 128 * tc:128 * (tc + 1)],
                    pstg[:, 128 * tc:128 * (tc + 1)], st.idb[:]))
            kb.inc_last(T, s_ptr, 1)
            kb.wait(V, s_ptr, npj)
            for tc in range(4):
                dv(lambda E, qt=qt, tc=tc, dm=dm: E.tensor_tensor(
                    x5[:, 4 * qt + tc, 128 * dm:128 * (dm + 1)],
                    x5[:, 4 * qt + tc, 128 * dm:128 * (dm + 1)],
                    bk[5].ap().bitcast(BF16)[:, 128 * tc:128 * (tc + 1)],
                    ALU.add))
            kb.inc_last(V, s_ra5, 1)
    kb.wait(Sy, s_ra5, 32)
    kb.on(Sy, lambda E: E.dma_start(
        st.xres.ap().rearrange("c p d -> p c d"), x5[:]))
    kb.inc_last(Sy, s_x5, 16)
    if st.x3s is not None:
        kb.on(Sy, lambda E: E.dma_start(
            st.x3s.ap().rearrange("c p d -> p c d"), x5[:]))
        kb.inc_last(Sy, s_x5, 16)
        kb.wait(Sy, s_x5, 48)
    else:
        kb.wait(Sy, s_x5, 32)
    kb.end_scope()
